# version 13
# baseline (speedup 1.0000x reference)
import numpy as np
import concourse.bass as bass
import concourse.mybir as mybir
from concourse.bass_utils import run_bass_kernel_spmd

F32 = mybir.dt.float32
BF16 = mybir.dt.bfloat16
U8 = mybir.dt.uint8
AF = mybir.ActivationFunctionType
ALU = mybir.AluOpType
AX = mybir.AxisListType

T = 4096
D = 1024
NT = 32
NG = 8

ALL_STAGES = ("fff1", "att0", "fff2", "moe1", "att1", "moe2")
STAGE_GROUPS = [ALL_STAGES]
DEBUG_TAPS = False


class _Op:
    __slots__ = ("stream", "emit", "kind", "pos", "sem", "val", "waits", "know")

    def __init__(self, stream, emit, kind, pos):
        self.stream = stream
        self.emit = emit
        self.kind = kind
        self.pos = pos
        self.sem = None
        self.val = None
        self.waits = []
        self.know = None


class Sched:

    CSTREAMS = ("pe", "act", "dve", "pool")
    DSTREAMS = ("sp", "pool")
    NDSEM = 12

    def __init__(self):
        self.ops = {s: [] for s in ("pe", "act", "dve", "sp", "pool")}
        self.ccnt = {s: 0 for s in self.CSTREAMS}
        self.last_assigned_pos = {s: -1 for s in self.CSTREAMS}
        self.didx = {s: 0 for s in self.DSTREAMS}
        self.dsem_last = {}
        self.know = {s: {} for s in ("pe", "act", "dve", "sp", "pool")}
        self.snap = {}
        self.reg = {}
        self.sem_totals = {}
        self._barrier_deps = []
        self._barrier_absorbed = set()

    def _completion(self, op):
        if op.sem is not None:
            if op.kind == "d":
                return op.sem, self.dsem_last[op.sem]
            return op.sem, op.val
        s = op.stream
        assert op.kind == "c"
        if op.pos > self.last_assigned_pos[s]:
            self.ccnt[s] += 1
            op.sem = "c_" + s
            op.val = self.ccnt[s]
            self.last_assigned_pos[s] = op.pos
            know = dict(op.know)
            know["c_" + s] = op.val
            self.snap[(op.sem, op.val)] = know
            self.sem_totals[op.sem] = op.val
            return op.sem, op.val
        for later in self.ops[s][op.pos + 1:]:
            if later.sem is not None and later.kind == "c":
                return later.sem, later.val
        raise AssertionError("inconsistent inc assignment")

    def _add_wait(self, op, dep_op):
        sem, val = self._completion(dep_op)
        know = self.know[op.stream]
        if know.get(sem, 0) >= val:
            return
        op.waits.append((sem, val))
        snap = self.snap.get((sem, val))
        if snap:
            for k, v in snap.items():
                if know.get(k, 0) < v:
                    know[k] = v
        if know.get(sem, 0) < val:
            know[sem] = val

    def op(self, stream, emit, reads=(), writes=(), kind="c", wmode="w"):
        lst = self.ops[stream]
        o = _Op(stream, emit, kind, len(lst))
        if self._barrier_deps and stream not in self._barrier_absorbed:
            self._barrier_absorbed.add(stream)
            for d in self._barrier_deps:
                if d is not None:
                    self._add_wait(o, d)
        deps = []
        for r in reads:
            e = self.reg.get(r)
            if e and e[0] is not None:
                deps.append(e[0])
        if wmode == "w":
            for w in writes:
                e = self.reg.get(w)
                if e:
                    if e[0] is not None:
                        deps.append(e[0])
                    deps.extend(e[1])
        seen = set()
        for d in deps:
            if d is o or id(d) in seen:
                continue
            seen.add(id(d))
            self._add_wait(o, d)
        if kind == "d":
            i = self.didx[stream]
            self.didx[stream] = i + 1
            o.sem = "d_%s_%d" % (stream, i % self.NDSEM)
            o.val = 16 * (i // self.NDSEM + 1)
            if i >= self.NDSEM:
                know = self.know[stream]
                if know.get(o.sem, 0) < o.val - 16:
                    o.waits.append((o.sem, o.val - 16))
                    snap = self.snap.get((o.sem, o.val - 16))
                    if snap:
                        for k, v in snap.items():
                            if know.get(k, 0) < v:
                                know[k] = v
                    know[o.sem] = max(know.get(o.sem, 0), o.val - 16)
            self.dsem_last[o.sem] = o.val
            self.sem_totals[o.sem] = max(self.sem_totals.get(o.sem, 0), o.val)
        o.know = dict(self.know[stream])
        if o.sem is not None:
            know = dict(o.know)
            know[o.sem] = o.val
            self.snap[(o.sem, o.val)] = know
        lst.append(o)
        for r in reads:
            e = self.reg.setdefault(r, [None, [], "w"])
            e[1].append(o)
        for w in writes:
            e = self.reg.setdefault(w, [None, [], "w"])
            e[0] = o
            if wmode == "w":
                e[1] = []
        return o

    def soft_barrier(self):
        deps = []
        for s in ("pe", "act", "dve", "sp", "pool"):
            if self.ops[s]:
                deps.append(self.ops[s][-1])
        self._barrier_deps = deps
        self._barrier_absorbed = set()

    def barrier_regions(self, keys, dep_keys):
        ops = []
        for k in dep_keys:
            e = self.reg.get(k)
            if e:
                if e[0] is not None:
                    ops.append(e[0])
                ops.extend(e[1])
                del self.reg[k]
        if not ops:
            return
        for k in keys:
            e = self.reg.setdefault(k, [None, [], "w"])
            e[1].extend(ops)

    def finalize(self, nc, stack):
        sems = {}
        for s in self.CSTREAMS:
            sems["c_" + s] = stack.enter_context(nc.semaphore("c_" + s))
        for s in self.DSTREAMS:
            for j in range(self.NDSEM):
                sems["d_%s_%d" % (s, j)] = stack.enter_context(
                    nc.semaphore("d_%s_%d" % (s, j)))
        block = stack.enter_context(nc.Block())
        sched = self

        def run_stream(eng, name):
            for o in sched.ops[name]:
                for sem, val in o.waits:
                    eng.wait_ge(sems[sem], val)
                instr = o.emit(eng)
                if o.sem is not None:
                    instr.then_inc(sems[o.sem], 1 if o.kind == "c" else 16)

        @block.tensor
        def _(t):
            run_stream(t, "pe")

        @block.scalar
        def _(s):
            run_stream(s, "act")

        @block.vector
        def _(v):
            run_stream(v, "dve")

        @block.gpsimd
        def _(g):
            run_stream(g, "pool")

        @block.sync
        def _(sy):
            run_stream(sy, "sp")
            for sem, val in sorted(sched.sem_totals.items()):
                sy.wait_ge(sems[sem], val)


_INPUT_NAMES = (
    "L0_attn_x", "L0_attn_wq", "L0_attn_wk", "L0_attn_wv", "L0_attn_wo_w",
    "L0_attn_wo_b", "L0_ffn_q_node_w", "L0_ffn_q_node_b", "L0_ffn_q_w1",
    "L0_ffn_q_b1", "L0_ffn_q_w2", "L0_ffn_q_b2", "L0_ffn_k_node_w",
    "L0_ffn_k_node_b", "L0_ffn_k_w1", "L0_ffn_k_b1", "L0_ffn_k_w2",
    "L0_ffn_k_b2", "L0_ffn_v_node_w", "L0_ffn_v_node_b", "L0_ffn_v_w1",
    "L0_ffn_v_b1", "L0_ffn_v_w2", "L0_ffn_v_b2", "L1_attn_Wqa", "L1_attn_Wqb",
    "L1_attn_Wkva", "L1_attn_Wkvb", "L1_attn_Wout", "L1_attn_bout",
    "L1_ffn_router_w", "L1_ffn_w1", "L1_ffn_b1", "L1_ffn_w2", "L1_ffn_b2",
)


def prep_weights(inp):
    for _n in _INPUT_NAMES:
        assert _n in inp, _n
    w = {}
    nw = np.concatenate([inp["L0_ffn_%s_node_w" % i].T for i in "qkv"], axis=1)
    w["f_nodeW"] = np.ascontiguousarray(nw)
    nb = np.concatenate([inp["L0_ffn_%s_node_b" % i] for i in "qkv"])
    w["f_nodeB"] = np.ascontiguousarray(nb.reshape(1, 45))
    for i in "qkv":
        w["f_w1_" + i] = np.ascontiguousarray(
            inp["L0_ffn_%s_w1" % i].transpose(1, 0, 2).reshape(D, D))
        w["f_b1_" + i] = np.ascontiguousarray(
            inp["L0_ffn_%s_b1" % i].reshape(1, D))
        w["f_w2_" + i] = np.ascontiguousarray(
            inp["L0_ffn_%s_w2" % i].reshape(D, D))
        w["f_b2_" + i] = np.ascontiguousarray(inp["L0_ffn_%s_b2" % i])
    perm = np.concatenate([np.arange(0, D, 2), np.arange(1, D, 2)])
    w["a0_wqT"] = np.ascontiguousarray(inp["L0_attn_wq"][perm, :].T)
    w["a0_wkT"] = np.ascontiguousarray(inp["L0_attn_wk"][perm, :].T)
    w["a0_wvT"] = np.ascontiguousarray(inp["L0_attn_wv"].T)
    w["a0_woT"] = np.ascontiguousarray(inp["L0_attn_wo_w"].T)
    w["a0_wob"] = np.ascontiguousarray(inp["L0_attn_wo_b"].reshape(1, D))
    r = np.arange(512, dtype=np.float64)
    fr = np.exp(r * 2 * (-np.log(10000.0) / D))
    ang = np.arange(T, dtype=np.float64)[None, :] * fr[:, None]
    w["a0_cos"] = np.cos(ang).astype(np.float32)
    w["a0_sin"] = np.sin(ang).astype(np.float32)
    w["a1_Wqa"] = np.ascontiguousarray(inp["L1_attn_Wqa"])
    w["a1_Wqb"] = np.ascontiguousarray(inp["L1_attn_Wqb"])
    w["a1_Wkva"] = np.ascontiguousarray(inp["L1_attn_Wkva"])
    kvb = inp["L1_attn_Wkvb"].reshape(32, 4, 64, 2)
    w["a1_Wkvb"] = np.ascontiguousarray(
        np.concatenate([kvb[..., 0].reshape(32, 256),
                        kvb[..., 1].reshape(32, 256)], axis=1))
    w["a1_Wout"] = np.ascontiguousarray(inp["L1_attn_Wout"])
    w["a1_bout"] = np.ascontiguousarray(inp["L1_attn_bout"].reshape(1, D))
    j = np.arange(32, dtype=np.float64)
    invf = 1.0 / 10000.0 ** (2 * j / 64)
    ang1 = np.arange(T, dtype=np.float64)[:, None] * invf[None, :]
    w["a1_cos"] = np.cos(ang1).astype(np.float32)
    w["a1_sin"] = np.sin(ang1).astype(np.float32)
    w["m_routerT"] = np.ascontiguousarray(inp["L1_ffn_router_w"].T)
    w["m_w1"] = np.ascontiguousarray(inp["L1_ffn_w1"].transpose(0, 2, 1))
    w["m_b1"] = np.ascontiguousarray(
        inp["L1_ffn_b1"].reshape(8, 32, 128).transpose(0, 2, 1))
    w["m_w2"] = np.ascontiguousarray(inp["L1_ffn_w2"].transpose(0, 2, 1))
    w["m_b2"] = np.ascontiguousarray(inp["L1_ffn_b2"].reshape(1, 8 * D))
    w["ident"] = np.eye(128, dtype=np.float32)
    w["ones_col"] = np.ones((1, 128), np.float32)
    w["ones_k"] = np.ones((128, 1), np.float32)
    con = np.zeros((128, 16), np.float32)
    con[:, 0] = 0.125
    con[:, 1] = 0.03125
    con[:, 2] = 1e-5
    w["con"] = con
    tri = (np.arange(128)[:, None] <= np.arange(128)[None, :]).astype(np.float32)
    w["tri"] = tri
    return w


CON_SCALE_A = 0.125
CON_SCALE_F = 0.03125
CON_EPS = 1e-5


class Ctx:
    pass


def build_nc(stages):
    import contextlib
    stack = contextlib.ExitStack()
    nc = bass.Bass("TRN2", target_bir_lowering=False)
    S = Sched()
    C = Ctx()
    C.nc = nc
    C.S = S
    C.stages = stages
    need = set(stages)

    dr = {}
    dr["xin"] = nc.dram_tensor("xin", [T, D], F32, kind="ExternalInput").ap()
    dr["xout"] = nc.dram_tensor("xout", [T, D], F32, kind="ExternalOutput").ap()
    innames = ["xin"]

    def din(name, shape, dt=F32):
        dr[name] = nc.dram_tensor(name, list(shape), dt, kind="ExternalInput").ap()
        innames.append(name)

    din("con", (128, 16))
    din("ident", (128, 128))
    din("ones_col", (1, 128))
    din("ones_k", (128, 1))
    if {"fff1", "fff2"} & need:
        din("f_nodeW", (D, 45)); din("f_nodeB", (1, 45))
        for i in "qkv":
            din("f_w1_" + i, (D, D)); din("f_b1_" + i, (1, D))
            din("f_w2_" + i, (D, D)); din("f_b2_" + i, (16, D))
    if "att0" in need:
        for n in ("a0_wqT", "a0_wkT", "a0_wvT", "a0_woT"):
            din(n, (D, D))
        din("a0_wob", (1, D))
        din("a0_cos", (512, T)); din("a0_sin", (512, T))
    if "att1" in need:
        din("a1_Wqa", (D, 64)); din("a1_Wqb", (64, D))
        din("a1_Wkva", (D, 32)); din("a1_Wkvb", (32, 512))
        din("a1_Wout", (D, D)); din("a1_bout", (1, D))
        din("a1_cos", (T, 32)); din("a1_sin", (T, 32))
        din("tri", (128, 128))
    if {"moe1", "moe2"} & need:
        din("m_routerT", (D, 8))
        din("m_w1", (8, D, 4 * D)); din("m_b1", (8, 128, 32))
        din("m_w2", (8, 4 * D, D)); din("m_b2", (1, 8 * D))
    nscratch = len(stages) - 1
    xs = [dr["xin"]]
    for k in range(nscratch):
        dr["xs%d" % k] = nc.dram_tensor("xs%d" % k, [T, D], F32, kind="Internal").ap()
        xs.append(dr["xs%d" % k])
    xs.append(dr["xout"])
    C.xs = xs
    if {"fff1", "fff2", "att0", "moe1", "moe2", "att1"} & need:
        dr["ysc"] = nc.dram_tensor("ysc", [128, 8, T], BF16, kind="Internal").ap()
    if {"fff1", "fff2", "att0"} & need:
        dr["qsc"] = nc.dram_tensor("qsc", [128, 8, T], BF16, kind="Internal").ap()
        dr["ksc"] = nc.dram_tensor("ksc", [128, 8, T], BF16, kind="Internal").ap()
    if DEBUG_TAPS:
        dr["dbg_y"] = nc.dram_tensor("dbg_y", [128, 8, T], BF16, kind="ExternalOutput").ap()
        dr["dbg_q"] = nc.dram_tensor("dbg_q", [128, 8, T], BF16, kind="ExternalOutput").ap()
        dr["dbg_k"] = nc.dram_tensor("dbg_k", [128, 8, T], BF16, kind="ExternalOutput").ap()
        dr["dbg_v"] = nc.dram_tensor("dbg_v", [128, 32, 1040], BF16, kind="ExternalOutput").ap()
        dr["dbg_s"] = nc.dram_tensor("dbg_s", [128, NT * 45], F32, kind="ExternalOutput").ap()
        dr["dbg_e"] = nc.dram_tensor("dbg_e", [128, 3 * NT * 16], BF16, kind="ExternalOutput").ap()
    C.dr = dr

    sb = {}

    def sbuf(name, shape, dt):
        sb[name] = stack.enter_context(nc.sbuf_tensor("s_" + name, list(shape), dt))

    sbuf("BIG1", (128, 32768), BF16)
    sbuf("BIG2", (128, 33280), BF16)
    sbuf("MEDA", (128, 16384), BF16)
    sbuf("MEDB", (128, 8192), BF16)
    sbuf("FA", (128, 2144), F32)
    sbuf("BA", (128, 4226), BF16)
    sbuf("POOLB", (128, 4208), BF16)
    sbuf("POOLF", (128, 640), F32)
    sbuf("ROW", (1, 1232), BF16)
    sbuf("con", (128, 16), F32)

    fa, ba = sb["FA"], sb["BA"]
    sb["xa0"] = fa[:, 0:1024]
    sb["xmix0"] = fa[:, 1024:2048]
    sb["stats"] = fa[:, 2048:2080]
    sb["small"] = fa[:, 2080:2144]
    sb["ident"] = ba[:, 0:128]
    sb["y0"] = ba[:, 128:1152]
    sb["ytt0"] = ba[:, 1152:2176].rearrange("p (c q) -> p c q", c=8)
    sb["eT0"] = ba[:, 2176:3200]
    sb["eT1"] = ba[:, 3200:4224]
    sb["junk"] = sb["eT0"]
    sb["ones_k"] = ba[:, 4224:4225]
    sb["ones_col"] = sb["ROW"][:, 0:128]
    C.sb = sb

    ps = {}

    def psum(name, shape, dt):
        ps[name] = stack.enter_context(nc.psum_tensor(name, list(shape), dt))

    psum("big0", (128, 1024), F32)
    psum("big1", (128, 1024), F32)
    psum("pd", (128, 1536), F32)
    psum("ptr", (128, 1024), BF16)
    ps["med0"] = ps["big1"][:, 0:512]
    ps["med1"] = ps["big1"][:, 512:1024]
    C.ps = ps

    nc.const_aps.register(F32, CON_SCALE_A, sb["con"][:, 0:1])
    nc.const_aps.register(F32, CON_SCALE_F, sb["con"][:, 1:2])
    nc.const_aps.register(F32, CON_EPS, sb["con"][:, 2:3])

    S.op("pool", lambda g: g.dma_start(sb["con"][:], dr["con"]),
         writes=[("con",)], kind="d")
    S.op("pool", lambda g: g.dma_start(sb["ident"], dr["ident"]),
         writes=[("ident",)], kind="d")
    S.op("pool", lambda g: g.dma_start(sb["ones_col"][:], dr["ones_col"]),
         writes=[("ones_col",)], kind="d")
    S.op("pool", lambda g: g.dma_start(sb["ones_k"], dr["ones_k"]),
         writes=[("ones_k",)], kind="d")

    for si, st in enumerate(stages):
        x_src, x_dst = C.xs[si], C.xs[si + 1]
        S.soft_barrier()
        if st in ("fff1", "fff2"):
            emit_fff(C, x_src, x_dst, si)
        elif st == "att0":
            emit_att0(C, x_src, x_dst, si)
        elif st == "att1":
            emit_att1(C, x_src, x_dst, si)
        else:
            emit_moe(C, x_src, x_dst, si)

    S.finalize(nc, stack)
    stack.close()
    return nc, innames


def ln_stageA(C, x_src, t, si, extra_reads=()):
    S, sb = C.S, C.sb
    p = t % 2
    xa = sb["xa0"] if p == 0 else sb["xmix0"]
    st = sb["stats"]
    base = p * 8
    s1 = st[:, base + 0:base + 1]
    s2 = st[:, base + 1:base + 2]
    rxa = ("xa", si, p) if p == 0 else ("xmix",)
    rst = ("st", si, p)
    rs2 = ("st2", si, p)
    S.op("sp", lambda e, xa=xa, t=t: e.dma_start(xa[:], x_src[128 * t:128 * (t + 1), :]),
         reads=[("x", si)] + list(extra_reads), writes=[rxa], kind="d")
    S.op("act", lambda e, xa=xa, s2=s2: e.activation(sb["junk"][:], xa[:], AF.Square, accum_out=s2),
         reads=[rxa], writes=[("eT", 0), rs2])
    S.op("dve", lambda e, xa=xa, s1=s1: e.reduce_sum(s1, xa[:], axis=AX.X),
         reads=[rxa], writes=[rst])


def ln_stageB(C, t, si):
    S, sb = C.S, C.sb
    p = t % 2
    xa = sb["xa0"] if p == 0 else sb["xmix0"]
    y = C.ybuf[p]
    st = sb["stats"]
    base = p * 8
    s1 = st[:, base + 0:base + 1]
    s2 = st[:, base + 1:base + 2]
    mean = st[:, base + 2:base + 3]
    msq = st[:, base + 3:base + 4]
    var = st[:, base + 4:base + 5]
    sd = st[:, base + 5:base + 6]
    rstd = st[:, base + 6:base + 7]
    aa = st[:, base + 7:base + 8]
    rxa = ("xa", si, p) if p == 0 else ("xmix",)
    ry = ("yb", si, p)
    rst = ("st", si, p)
    rs2 = ("st2", si, p)
    S.op("dve", lambda e, s1=s1, mean=mean: e.tensor_scalar_mul(mean, s1, 1.0 / D),
         reads=[rst], writes=[rst], wmode="acc")
    S.op("dve", lambda e, mean=mean, msq=msq: e.tensor_mul(msq, mean, mean),
         reads=[rst], writes=[rst], wmode="acc")
    S.op("dve", lambda e, s2=s2, msq=msq, var=var: e.scalar_tensor_tensor(
        var, s2, 1.0 / D, msq, ALU.mult, ALU.subtract),
        reads=[rst, rs2], writes=[rst], wmode="acc")
    S.op("act", lambda e, sd=sd, var=var: e.activation(sd, var, AF.Sqrt, bias=CON_EPS),
         reads=[rst, ("con",)], writes=[rst], wmode="acc")
    S.op("dve", lambda e, rstd=rstd, sd=sd: e.reciprocal(rstd, sd),
         reads=[rst], writes=[rst], wmode="acc")
    S.op("dve", lambda e, aa=aa, mean=mean, rstd=rstd: e.scalar_tensor_tensor(
        aa, mean, -1.0, rstd, ALU.mult, ALU.mult),
        reads=[rst], writes=[rst], wmode="acc")
    S.op("pool", lambda e, xa=xa, y=y, rstd=rstd, aa=aa: e.tensor_scalar(
        y[:], xa[:], rstd, aa, ALU.mult, ALU.add),
        reads=[rxa, rst], writes=[ry])
    return p, rxa, ry


def ln_loop(C, x_src, si, per_tile_fn):
    ln_stageA(C, x_src, 0, si)
    for t in range(NT):
        if t + 1 < NT:
            ln_stageA(C, x_src, t + 1, si)
        p, rxa, ry = ln_stageB(C, t, si)
        per_tile_fn(t, p, ry)


def transpose_y(C, ry, p, t, si, dest_fn, dest_regs):
    S, sb, ps = C.S, C.sb, C.ps
    y = C.ybuf[p]
    for c in range(8):
        S.op("pe",
             lambda e, c=c, y=y: e.transpose(ps["ptr"][:, 128 * c:128 * (c + 1)],
                                             y[:, 128 * c:128 * (c + 1)], sb["ident"]),
             reads=[ry, ("ident",)], writes=[("ptr",)],
             wmode="w" if c == 0 else "acc")
    S.op("act", dest_fn, reads=[("ptr",)], writes=list(dest_regs))


def emit_fff(C, x_src, x_dst, si):
    S, sb, ps, dr, nc = C.S, C.sb, C.ps, C.dr, C.nc
    ysc, qsc, ksc = dr["ysc"], dr["qsc"], dr["ksc"]
    BIG1, BIG2, MEDA, MEDB = sb["BIG1"], sb["BIG2"], sb["MEDA"], sb["MEDB"]
    PB, PF, ROW = sb["POOLB"], sb["POOLF"], sb["ROW"]
    nodeW = PB[:, 0:360].rearrange("p (c j) -> p c j", c=8)
    hms = [PB[:, 368:1392], PB[:, 1392:2416]]
    eqb = PB[:, 2416:3952].rearrange("p (i t l) -> p i t l", i=3, t=NT)
    bits = PB[:, 3952:4144].bitcast(U8).rearrange("p (i b t) -> p i b t", i=3, b=4)
    nodeB = ROW[:, 128:173]
    b1row = ROW[:, 176:1200]
    treetmp = PF[:, 0:256].rearrange("p (k t) -> p k t", k=8)
    leafI = PF[:, 256:352].rearrange("p (i t) -> p i t", i=3)
    C.ybuf = [sb["y0"], sb["eT1"]]
    scoresAll = MEDA.bitcast(F32)[:, 6656:6656 + NT * 45]
    w1v = BIG1[:, 0:16384].rearrange("p (b c F) -> p b c F", b=2, c=8)
    w2v = BIG1[:, 16384:].rearrange("p (b f E) -> p b f E", b=2, f=8)
    Vv = BIG2[:, :32768].rearrange("p (t e) -> p t e", t=32)
    eqT = MEDA[0:16, 0:4096]
    b2s = MEDA[0:16, 4096:5120]
    hmT = MEDA[:, 5120:9216].rearrange("p (c q) -> p c q", c=8)
    qTg = MEDA[:, 9216:13312].rearrange("p (c q) -> p c q", c=8)
    ytg = MEDB.rearrange("p (b c q) -> p b c q", b=2, c=8)

    S.op("pool", lambda g: g.dma_start(
        nodeW, dr["f_nodeW"].rearrange("(c p) j -> p c j", p=128)),
        writes=[("f_nodeW",)], kind="d")
    S.op("pool", lambda g: g.dma_start(nodeB, dr["f_nodeB"]),
        writes=[("rowbuf2",)], kind="d")

    def fff_passA(t, p, ry):
        ytt = sb["ytt0"]
        rytt = ("ytt", si)
        transpose_y(C, ry, p, t, si,
                    lambda e, ytt=ytt: e.copy(ytt[:, :, :],
                                              ps["ptr"].rearrange("p (c q) -> p c q", c=8)),
                    [rytt])
        S.op("sp", lambda e, ytt=ytt, t=t: e.dma_start(ysc[:, :, 128 * t:128 * (t + 1)], ytt[:, :, :]),
             reads=[rytt], writes=[("ysc", si, t // 4)], kind="d",
             wmode="w" if t % 4 == 0 else "acc")
        mp = ps["med%d" % p]
        rmp = ("med%d" % p,)
        for c in range(8):
            S.op("pe", lambda e, c=c, ytt=ytt, mp=mp: e.matmul(
                mp[:, 0:45], ytt[:, c, :], nodeW[:, c, :],
                start=(c == 0), stop=False),
                reads=[rytt, ("f_nodeW",)], writes=[rmp],
                wmode="w" if c == 0 else "acc")
        S.op("pe", lambda e, mp=mp: e.matmul(
            mp[:, 0:45], sb["ones_col"], nodeB,
            start=False, stop=True),
            reads=[("ones_col",), ("rowbuf2",)], writes=[rmp], wmode="acc")
        S.op("dve", lambda e, mp=mp, t=t, scoresAll=scoresAll: e.tensor_copy(
            scoresAll[:, 45 * t:45 * (t + 1)], mp[:, 0:45]),
            reads=[rmp], writes=[("scoresAll",)],
            wmode="w" if t < 2 else "acc")

    ln_loop(C, x_src, si, fff_passA)

    sA = scoresAll
    tt = treetmp

    def sj(i, j):
        return sA[:, i * 15 + j: NT * 45: 45]

    for i in range(3):
        rb = ("bits",)
        rt = ("treetmp",)
        rl = ("leafI",)
        rs = ("scoresAll",)

        def V(fn, reads, writes, wmode="w"):
            S.op("dve", fn, reads=reads, writes=writes, wmode=wmode)
        b0, b1, b2, b3 = (bits[:, i, k, :] for k in range(4))
        V(lambda e, i=i, b0=b0: e.tensor_single_scalar(b0, sj(i, 0), 0.0, ALU.is_ge), [rs], [rb], "acc")
        V(lambda e, i=i, b0=b0: e.select(tt[:, 0, :], b0, sj(i, 2), sj(i, 1), add_drain=True), [rs, rb], [rt], "acc")
        V(lambda e, b1=b1: e.tensor_single_scalar(b1, tt[:, 0, :], 0.0, ALU.is_ge), [rt, rb], [rb], "acc")
        V(lambda e, i=i, b1=b1: e.select(tt[:, 1, :], b1, sj(i, 4), sj(i, 3), add_drain=True), [rs, rb], [rt], "acc")
        V(lambda e, i=i, b1=b1: e.select(tt[:, 2, :], b1, sj(i, 6), sj(i, 5), add_drain=True), [rs, rb], [rt], "acc")
        V(lambda e, b0=b0: e.select(tt[:, 3, :], b0, tt[:, 2, :], tt[:, 1, :], add_drain=True), [rt, rb], [rt], "acc")
        V(lambda e, b2=b2: e.tensor_single_scalar(b2, tt[:, 3, :], 0.0, ALU.is_ge), [rt], [rb], "acc")
        V(lambda e, i=i, b2=b2: e.select(tt[:, 0, :], b2, sj(i, 8), sj(i, 7), add_drain=True), [rs, rb], [rt], "acc")
        V(lambda e, i=i, b2=b2: e.select(tt[:, 1, :], b2, sj(i, 10), sj(i, 9), add_drain=True), [rs, rb], [rt], "acc")
        V(lambda e, i=i, b2=b2: e.select(tt[:, 2, :], b2, sj(i, 12), sj(i, 11), add_drain=True), [rs, rb], [rt], "acc")
        V(lambda e, i=i, b2=b2: e.select(tt[:, 3, :], b2, sj(i, 14), sj(i, 13), add_drain=True), [rs, rb], [rt], "acc")
        V(lambda e, b1=b1: e.select(tt[:, 4, :], b1, tt[:, 1, :], tt[:, 0, :], add_drain=True), [rt, rb], [rt], "acc")
        V(lambda e, b1=b1: e.select(tt[:, 5, :], b1, tt[:, 3, :], tt[:, 2, :], add_drain=True), [rt, rb], [rt], "acc")
        V(lambda e, b0=b0: e.select(tt[:, 6, :], b0, tt[:, 5, :], tt[:, 4, :], add_drain=True), [rt, rb], [rt], "acc")
        V(lambda e, b3=b3: e.tensor_single_scalar(b3, tt[:, 6, :], 0.0, ALU.is_ge), [rt], [rb], "acc")
        li = leafI[:, i, :]
        V(lambda e, li=li, b0=b0, b1=b1: e.scalar_tensor_tensor(li, b0, 2.0, b1, ALU.mult, ALU.add), [rb], [rl], "acc")
        V(lambda e, li=li, b2=b2: e.scalar_tensor_tensor(li, li, 2.0, b2, ALU.mult, ALU.add), [rb, rl], [rl], "acc")
        V(lambda e, li=li, b3=b3: e.scalar_tensor_tensor(li, li, 2.0, b3, ALU.mult, ALU.add), [rb, rl], [rl], "acc")
        for l in range(16):
            V(lambda e, i=i, l=l, li=li: e.tensor_single_scalar(
                eqb[:, i, :, l], li, float(l), ALU.is_equal), [rl], [("eqb",)], "acc")

    for ii, inst in enumerate("qkv"):
        bi = ii % 2
        rw1 = ("fw1", si, ii)
        rw2 = ("fw2", si, ii)
        S.op("pool", lambda g, inst=inst, bi=bi: g.dma_start(
            w1v[:, bi], dr["f_w1_" + inst].rearrange("(c p) F -> p c F", p=128)),
            writes=[rw1, ("w1v", bi)], kind="d")
        S.op("pool", lambda g, inst=inst, bi=bi: g.dma_start(
            w2v[:, bi], dr["f_w2_" + inst].rearrange("(f p) E -> p f E", p=128)),
            writes=[rw2, ("w2v", bi)], kind="d")
        S.op("pool", lambda g, inst=inst: g.dma_start(b1row, dr["f_b1_" + inst]),
             writes=[("rowbuf",)], kind="d")
        S.op("pool", lambda g, inst=inst: g.dma_start(b2s[:, :], dr["f_b2_" + inst]),
             writes=[("b2s",)], kind="d")
        for t in range(NT):
            S.op("pe", lambda e, t=t, ii=ii: e.transpose(
                ps["ptr"][0:16, 0:128], eqb[:, ii, t, :], sb["ident"]),
                reads=[("eqb",), ("ident",)], writes=[("ptr",)])
            S.op("act", lambda e, t=t: e.copy(eqT[:, 128 * t:128 * (t + 1)],
                                              ps["ptr"][0:16, 0:128]),
                 reads=[("ptr",)], writes=[("eqT",)],
                 wmode="w" if t == 0 else "acc")
        for g in range(NG):
            gp = g % 2
            rytg = ("ytg", gp)
            S.op("sp", lambda e, g=g, gp=gp: e.dma_start(
                ytg[:, gp], ysc[:, :, 512 * g:512 * (g + 1)]),
                reads=[("ysc", si, g)], writes=[rytg], kind="d")
            def c_mm1(tloc):
                t = 4 * g + tloc
                hp = tloc % 2
                hm = hms[hp]
                rhm = ("hm", hp)
                bp = ps["big0"] if hp == 0 else ps["pd"][:, 0:1024]
                rbig = ("big0",) if hp == 0 else ("pd",)
                for h2 in range(2):
                    cl = slice(512 * h2, 512 * (h2 + 1))
                    for c in range(8):
                        S.op("pe", lambda e, c=c, gp=gp, tloc=tloc, bi=bi, cl=cl, bp=bp: e.matmul(
                            bp[:, cl], ytg[:, gp, c, 128 * tloc:128 * (tloc + 1)],
                            w1v[:, bi, c, cl], start=(c == 0), stop=False),
                            reads=[rytg, rw1], writes=[rbig],
                            wmode="w" if (c == 0 and h2 == 0) else "acc")
                    S.op("pe", lambda e, cl=cl, bp=bp: e.matmul(
                        bp[:, cl], sb["ones_col"], b1row[0:1, cl],
                        start=False, stop=True),
                        reads=[("ones_col",), ("rowbuf",)], writes=[rbig], wmode="acc")
                S.op("dve", lambda e, hm=hm, ii=ii, t=t, bp=bp: e.scalar_tensor_tensor(
                    hm.rearrange("p (l h) -> p l h", l=16), bp.rearrange("p (l h) -> p l h", l=16),
                    0.0, eqb[:, ii, t, :, None].broadcast_to([128, 16, 64]),
                    ALU.max, ALU.mult),
                    reads=[rbig, ("eqb",)], writes=[rhm])

            def c_tr(tloc):
                hp = tloc % 2
                hm = hms[hp]
                rhm = ("hm", hp)
                for c in range(8):
                    S.op("pe", lambda e, c=c, hm=hm: e.transpose(
                        ps["ptr"][:, 128 * c:128 * (c + 1)],
                        hm[:, 128 * c:128 * (c + 1)], sb["ident"]),
                        reads=[rhm, ("ident",)], writes=[("ptr",)],
                        wmode="w" if c == 0 else "acc")
                S.op("act", lambda e, tloc=tloc: e.copy(
                    hmT[:, :, 128 * tloc:128 * (tloc + 1)],
                    ps["ptr"].rearrange("p (c q) -> p c q", c=8)),
                    reads=[("ptr",)], writes=[("hmT", tloc)])

            for tloc in range(4):
                c_mm1(tloc)
                if tloc > 0:
                    c_tr(tloc - 1)
            c_tr(3)
            if inst in ("q", "k"):
                for E in range(8):
                    mp = ps["med%d" % (E % 2)]
                    rmp = ("med%d" % (E % 2),)
                    for f in range(8):
                        S.op("pe", lambda e, E=E, f=f, bi=bi, mp=mp: e.matmul(
                            mp[:], w2v[:, bi, f, 128 * E:128 * (E + 1)], hmT[:, f, :],
                            start=(f == 0), stop=False),
                            reads=[rw2, ("hmT", 0), ("hmT", 1), ("hmT", 2), ("hmT", 3)],
                            writes=[rmp], wmode="w" if f == 0 else "acc")
                    S.op("pe", lambda e, E=E, g=g, mp=mp: e.matmul(
                        mp[:], b2s[:, 128 * E:128 * (E + 1)], eqT[:, 512 * g:512 * (g + 1)],
                        start=False, stop=True),
                        reads=[("b2s",), ("eqT",)], writes=[rmp], wmode="acc")
                    S.op("dve", lambda e, E=E, mp=mp: e.tensor_copy(qTg[:, E, :], mp[:]),
                         reads=[rmp], writes=[("qTg", E)])
                dst = qsc if inst == "q" else ksc
                S.op("sp", lambda e, g=g, dst=dst: e.dma_start(
                    dst[:, :, 512 * g:512 * (g + 1)], qTg[:, :, :]),
                    reads=[("qTg", E) for E in range(8)],
                    writes=[("qsc" if inst == "q" else "ksc", si)], kind="d",
                    wmode="w" if g == 0 else "acc")
            else:
                for tloc in range(4):
                    t = 4 * g + tloc
                    hp = tloc % 2
                    bp = ps["big0"] if hp == 0 else ps["pd"][:, 0:1024]
                    rbig = ("big0",) if hp == 0 else ("pd",)
                    for h2 in range(2):
                        cl = slice(512 * h2, 512 * (h2 + 1))
                        for f in range(8):
                            S.op("pe", lambda e, f=f, tloc=tloc, bi=bi, cl=cl, bp=bp: e.matmul(
                                bp[:, cl], hmT[:, f, 128 * tloc:128 * (tloc + 1)],
                                w2v[:, bi, f, cl], start=(f == 0), stop=False),
                                reads=[("hmT", tloc), rw2], writes=[rbig],
                                wmode="w" if (f == 0 and h2 == 0) else "acc")
                        S.op("pe", lambda e, t=t, cl=cl, bp=bp: e.matmul(
                            bp[:, cl], eqT[:, 128 * t:128 * (t + 1)], b2s[:, cl],
                            start=False, stop=True),
                            reads=[("eqT",), ("b2s",)], writes=[rbig], wmode="acc")
                    S.op("act", lambda e, t=t, bp=bp: e.copy(Vv[:, t, :], bp[:]),
                         reads=[rbig], writes=[("Vv", si)],
                         wmode="w" if t == 0 else "acc")

    if DEBUG_TAPS:
        S.op("sp", lambda e: e.dma_start(dr["dbg_y"], ysc[:, :, :]),
             reads=[("ysc", si)], kind="d")
        S.op("sp", lambda e: e.dma_start(dr["dbg_q"], qsc[:, :, :]),
             reads=[("qsc", si)], kind="d")
        S.op("sp", lambda e: e.dma_start(dr["dbg_k"], ksc[:, :, :]),
             reads=[("ksc", si)], kind="d")
        S.op("sp", lambda e: e.dma_start(dr["dbg_v"][:, :, 0:1024], Vv[:, :, :]),
             reads=[("Vv", si)], kind="d")
        S.op("sp", lambda e: e.dma_start(dr["dbg_s"], scoresAll[:, :]),
             reads=[("scoresAll",)], kind="d")
        S.op("sp", lambda e: e.dma_start(
            dr["dbg_e"], eqb.rearrange("p i t l -> p (i t l)")),
            reads=[("eqb",)], kind="d")

    kres = BIG1.rearrange("p (c t) -> p c t", c=8)
    S.barrier_regions([("kres",)], [("fw1", si, 0), ("fw1", si, 1), ("fw1", si, 2),
                                    ("fw2", si, 0), ("fw2", si, 1), ("fw2", si, 2),
                                    ("w1v", 0), ("w1v", 1), ("w2v", 0), ("w2v", 1)])
    S.op("sp", lambda e: e.dma_start(kres[:, :, :], ksc[:, :, :]),
         reads=[("ksc", si)], writes=[("kres",)], kind="d")
    qttv = [sb["ytt0"], hms[0].rearrange("p (c q) -> p c q", c=8)]
    for qt in range(NT):
        qp = qt % 2
        qtt = qttv[qp]
        rqtt = ("qtt", si, qp)
        S.op("sp", lambda e, qt=qt, qtt=qtt: e.dma_start(
            qtt[:, :, :], qsc[:, :, 128 * qt:128 * (qt + 1)]),
            reads=[("qsc", si)], writes=[rqtt], kind="d")
        xa = sb["xa0"]
        rxa = ("xa", si, 0)
        S.op("sp", lambda e, qt=qt, xa=xa: e.dma_start(
            xa[:], x_src[128 * qt:128 * (qt + 1), :]),
            reads=[("x", si)], writes=[rxa], kind="d")
        rpd = ("pd",)

        def fff_av(kt):
            kp = kt % 2
            eT = sb["eT%d" % kp]
            reT = ("eT", kp)
            for h2 in range(2):
                cl = slice(512 * h2, 512 * (h2 + 1))
                S.op("pe", lambda e, kt=kt, eT=eT, cl=cl: e.matmul(
                    ps["pd"][:, cl], eT[:, 0:128], Vv[:, kt, cl],
                    start=(kt == 0), stop=(kt == NT - 1)),
                    reads=[reT, ("Vv", si)], writes=[rpd],
                    wmode="w" if (kt == 0 and h2 == 0) else "acc")
            S.op("pe", lambda e, kt=kt, eT=eT: e.matmul(
                ps["pd"][:, 1024:1025], eT[:, 0:128], sb["ones_k"],
                start=(kt == 0), stop=(kt == NT - 1)),
                reads=[reT, ("ones_k",)], writes=[rpd], wmode="acc")

        for kt in range(NT):
            kp = kt % 2
            mp = ps["med%d" % kp]
            rmp = ("med%d" % kp,)
            for c in range(8):
                S.op("pe", lambda e, c=c, kt=kt, qtt=qtt, mp=mp: e.matmul(
                    mp[:, 0:128], kres[:, c, 128 * kt:128 * (kt + 1)], qtt[:, c, :],
                    start=(c == 0), stop=(c == 7)),
                    reads=[("kres",), rqtt], writes=[rmp],
                    wmode="w" if c == 0 else "acc")
            eT = sb["eT%d" % kp]
            reT = ("eT", kp)
            S.op("act", lambda e, eT=eT, mp=mp: e.activation(
                eT[:, 0:128], mp[:, 0:128], AF.Exp, scale=CON_SCALE_F),
                reads=[rmp, ("con",)], writes=[reT])
            if kt > 0:
                fff_av(kt - 1)
        fff_av(NT - 1)
        sm = sb["small"]
        S.op("dve", lambda e, qp=qp: e.tensor_scalar_mul(
            sm[:, qp:qp + 1], ps["pd"][:, 1024:1025], 2.0),
            reads=[rpd], writes=[("small", qp)])
        S.op("dve", lambda e, qp=qp: e.reciprocal(sm[:, 2 + qp:3 + qp], sm[:, qp:qp + 1]),
             reads=[("small", qp)], writes=[("small", qp)], wmode="acc")
        xm = sb["xmix0"]
        rxm = ("xmix",)
        S.op("dve", lambda e, qp=qp, xa=xa, xm=xm: e.scalar_tensor_tensor(
            xm[:], ps["pd"][:, 0:1024], sm[:, 2 + qp:3 + qp], xa[:], ALU.mult, ALU.add),
            reads=[rpd, ("small", qp), rxa], writes=[rxm])
        S.op("sp", lambda e, qt=qt, xm=xm: e.dma_start(
            x_dst[128 * qt:128 * (qt + 1), :], xm[:]),
            reads=[rxm], writes=[("x", si + 1)],
            wmode="w" if qt == 0 else "acc", kind="d")


def emit_att0(C, x_src, x_dst, si):
    S, sb, ps, dr, nc = C.S, C.sb, C.ps, C.dr, C.nc
    ysc, qsc, ksc = dr["ysc"], dr["qsc"], dr["ksc"]
    BIG1, BIG2, MEDA, MEDB = sb["BIG1"], sb["BIG2"], sb["MEDA"], sb["MEDB"]
    PB, ROW = sb["POOLB"], sb["ROW"]
    wqv = BIG1[:, 0:8192].rearrange("p (c E) -> p c E", c=8)
    wkv = BIG1[:, 8192:16384].rearrange("p (c E) -> p c E", c=8)
    wvv = BIG1[:, 16384:24576].rearrange("p (c E) -> p c E", c=8)
    wov = MEDA[:, 0:8192].rearrange("p (c E) -> p c E", c=8)
    wob = ROW[:, 176:1200]
    Vp = BIG2[:, 0:33280].rearrange("p (t h j) -> p t h j", t=32, h=16)
    cosb = MEDB[:, 0:2048].rearrange("p (c q) -> p c q", c=4)
    sinb = MEDB[:, 2048:4096].rearrange("p (c q) -> p c q", c=4)
    ytg = PB[:, 0:4096].rearrange("p (c q) -> p c q", c=8)
    eT0, eT1 = sb["eT0"], sb["eT1"]
    eT0f = eT0.bitcast(F32)
    C.ybuf = [sb["y0"], MEDB[:, 4096:5120]]

    for nm, dst_ in (("a0_wqT", wqv), ("a0_wkT", wkv), ("a0_wvT", wvv)):
        S.op("pool", lambda g, nm=nm, dst_=dst_: g.dma_start(
            dst_, dr[nm].rearrange("(c p) E -> p c E", p=128)),
            writes=[("a0w", nm)], kind="d")
    S.op("pool", lambda g: g.dma_start(
        wov, dr["a0_woT"].rearrange("(c p) E -> p c E", p=128)),
        writes=[("a0wo",)], kind="d")
    S.op("pool", lambda g: g.dma_start(wob, dr["a0_wob"]),
        writes=[("rowbuf",)], kind="d")
    S.op("dve", lambda e: e.memset(Vp[:, :, :, 64:65], 1.0), writes=[("Vv", si)])

    def a0_ln(t, p, ry):
        ytt = sb["ytt0"]
        rytt = ("ytt", si)
        transpose_y(C, ry, p, t, si,
                    lambda e, ytt=ytt: e.copy(ytt[:, :, :],
                                              ps["ptr"].rearrange("p (c q) -> p c q", c=8)),
                    [rytt])
        S.op("sp", lambda e, ytt=ytt, t=t: e.dma_start(ysc[:, :, 128 * t:128 * (t + 1)], ytt[:, :, :]),
             reads=[rytt], writes=[("ysc", si, t // 4)], kind="d",
             wmode="w" if t % 4 == 0 else "acc")

    ln_loop(C, x_src, si, a0_ln)

    for g in range(NG):
        S.op("pool", lambda e, g=g: e.dma_start(
            cosb, dr["a0_cos"].rearrange("(c p) t -> p c t", p=128)[:, :, 512 * g:512 * (g + 1)]),
            writes=[("cosb",)], kind="d")
        S.op("pool", lambda e, g=g: e.dma_start(
            sinb, dr["a0_sin"].rearrange("(c p) t -> p c t", p=128)[:, :, 512 * g:512 * (g + 1)]),
            writes=[("sinb",)], kind="d")
        S.op("sp", lambda e, g=g: e.dma_start(ytg, ysc[:, :, 512 * g:512 * (g + 1)]),
             reads=[("ysc", si, g)], writes=[("ytg", 0)], kind="d")
        for nm, wv_, dst in (("q", wqv, qsc), ("k", wkv, ksc)):
            for c in range(4):
                cp = c % 2
                if cp == 0:
                    mA, mB = ps["med0"], ps["med1"]
                    rA, rB = ("big1", 0), ("big1", 1)
                else:
                    mA, mB = ps["pd"][:, 512:1024], ps["pd"][:, 1024:1536]
                    rA, rB = ("pd", 1), ("pd", 2)
                for mi, rc, mp, rm in ((0, c, mA, rA), (1, c + 4, mB, rB)):
                    for cc in range(8):
                        S.op("pe", lambda e, cc=cc, rc=rc, wv_=wv_, mp=mp: e.matmul(
                            mp[:], wv_[:, cc, 128 * rc:128 * (rc + 1)], ytg[:, cc, :],
                            start=(cc == 0), stop=(cc == 7)),
                            reads=[("ytg", 0), ("a0w", "a0_" + ("wqT" if nm == "q" else "wkT"))],
                            writes=[rm],
                            wmode="w" if cc == 0 else "acc")
                t1 = sb["ytt0"].rearrange("p c q -> p (c q)").bitcast(F32)[:, 0:512]
                t2 = eT0f
                oE = eT1[:, 0:512]; oO = eT1[:, 512:1024]
                S.op("dve", lambda e, c=c, t1=t1, mA=mA: e.tensor_mul(t1, mA, cosb[:, c, :]),
                     reads=[rA, ("cosb",)], writes=[("ytt", si)])
                S.op("dve", lambda e, c=c, t2=t2, mB=mB: e.tensor_mul(t2, mB, sinb[:, c, :]),
                     reads=[rB, ("sinb",)], writes=[("eT", 0)])
                S.op("pool", lambda e, t1=t1, t2=t2, oE=oE: e.tensor_sub(oE, t1, t2),
                     reads=[("ytt", si), ("eT", 0)], writes=[("eT", 1)])
                S.op("dve", lambda e, c=c, t1=t1, mA=mA: e.tensor_mul(t1, mA, sinb[:, c, :]),
                     reads=[rA, ("sinb",)], writes=[("ytt", si)])
                S.op("dve", lambda e, c=c, t2=t2, mB=mB: e.tensor_mul(t2, mB, cosb[:, c, :]),
                     reads=[rB, ("cosb",)], writes=[("eT", 0)])
                S.op("pool", lambda e, t1=t1, t2=t2, oO=oO: e.tensor_add(oO, t1, t2),
                     reads=[("ytt", si), ("eT", 0)], writes=[("eT", 1)], wmode="acc")
                S.op("sp", lambda e, c=c, g=g, dst=dst: e.dma_start(
                    dst[:, c:c + 5:4, 512 * g:512 * (g + 1)],
                    eT1.rearrange("p (a q) -> p a q", a=2)),
                    reads=[("eT", 1)], writes=[("qsc" if dst is qsc else "ksc", si)],
                    kind="d", wmode="w" if (g == 0 and c == 0) else "acc")
        for tloc in range(4):
            t = 4 * g + tloc
            for h2 in range(2):
                cl = slice(512 * h2, 512 * (h2 + 1))
                for cc in range(8):
                    S.op("pe", lambda e, cc=cc, tloc=tloc, cl=cl: e.matmul(
                        ps["big0"][:, cl], ytg[:, cc, 128 * tloc:128 * (tloc + 1)],
                        wvv[:, cc, cl], start=(cc == 0), stop=(cc == 7)),
                        reads=[("ytg", 0), ("a0w", "a0_wvT")], writes=[("big0", 0), ("big0", 1)],
                        wmode="w" if (cc == 0 and h2 == 0) else "acc")
            S.op("act", lambda e, t=t: e.copy(
                Vp[:, t, :, 0:64], ps["big0"].rearrange("p (h j) -> p h j", h=16)),
                reads=[("big0", 0), ("big0", 1)], writes=[("Vv", si)], wmode="acc")

    if DEBUG_TAPS:
        S.op("sp", lambda e: e.dma_start(dr["dbg_y"], ysc[:, :, :]),
             reads=[("ysc", si)], kind="d")
        S.op("sp", lambda e: e.dma_start(dr["dbg_q"], qsc[:, :, :]),
             reads=[("qsc", si)], kind="d")
        S.op("sp", lambda e: e.dma_start(dr["dbg_k"], ksc[:, :, :]),
             reads=[("ksc", si)], kind="d")
        S.op("sp", lambda e: e.dma_start(dr["dbg_v"], BIG2[:, 0:33280].rearrange("p (t j) -> p t j", t=32)),
             reads=[("Vv", si)], kind="d")

    kres = BIG1[:, 0:32768].rearrange("p (c t) -> p c t", c=8)
    S.barrier_regions([("kres",)], [("a0w", "a0_wqT"), ("a0w", "a0_wkT"), ("a0w", "a0_wvT")])
    S.op("sp", lambda e: e.dma_start(kres[:, :, :], ksc[:, :, :]),
         reads=[("ksc", si)], writes=[("kres",)], kind="d")
    qttv = [PB[:, 0:2048].rearrange("p (c q) -> p c q", c=8),
            PB[:, 2048:4096].rearrange("p (c q) -> p c q", c=8)]
    aoT = sb["ytt0"]
    ptrf = ps["ptr"].bitcast(F32)
    eAslot = [eT0[:, 0:512], eT0[:, 512:1024], MEDB[:, 5120:5632]]
    eBslot = [eT1[:, 0:512], eT1[:, 512:1024], MEDB[:, 5632:6144]]
    avac = [[ps["pd"][:, 0:65], ps["pd"][:, 1024:1089]],
            [ps["pd"][:, 512:577], ptrf[:, 0:65]]]
    avkey = [[("pd",), ("pd", 2)], [("pd", 1), ("ptr",)]]
    sm = sb["small"]
    y0 = sb["y0"]
    xa = sb["xa0"]
    rxa = ("xa", si, 0)
    KB2 = 16
    for QP in range(16):
        qp = QP % 2
        qtt = qttv[qp]
        rqtt = ("qtt", si, qp)
        S.op("sp", lambda e, QP=QP, qtt=qtt: e.dma_start(
            qtt[:, :, :], qsc[:, :, 256 * QP:256 * (QP + 1)]),
            reads=[("qsc", si)],
            writes=[rqtt] + ([("ytg", 0)] if QP < 2 else []), kind="d")

        def a0_scores(snum, j, ktB2):
            par = snum % 2
            sl = snum % 3
            sA = ps["big0"][:, 512 * par:512 * (par + 1)]
            sB = ps["big1"][:, 512 * par:512 * (par + 1)]
            rsA = ("big0", par)
            rsB = ("big1", par)
            for ktl in range(2):
                kt = 2 * ktB2 + ktl
                S.op("pe", lambda e, j=j, kt=kt, ktl=ktl, sA=sA, qtt=qtt: e.matmul(
                    sA[:, 256 * ktl:256 * (ktl + 1)], kres[0:64, j, 128 * kt:128 * (kt + 1)],
                    qtt[0:64, j, :], start=True, stop=True, tile_position=(0, 0)),
                    reads=[("kres",), rqtt], writes=[rsA],
                    wmode="w" if ktl == 0 else "acc")
                S.op("pe", lambda e, j=j, kt=kt, ktl=ktl, sB=sB, qtt=qtt: e.matmul(
                    sB[:, 256 * ktl:256 * (ktl + 1)], kres[64:128, j, 128 * kt:128 * (kt + 1)],
                    qtt[64:128, j, :], start=True, stop=True, tile_position=(64, 0)),
                    reads=[("kres",), rqtt], writes=[rsB],
                    wmode="w" if ktl == 0 else "acc")
            eA = eAslot[sl]
            eB = eBslot[sl]
            S.op("act", lambda e, sA=sA, eA=eA: e.activation(eA, sA, AF.Exp, scale=CON_SCALE_A),
                 reads=[rsA, ("con",)], writes=[("eA", sl)])
            S.op("act", lambda e, sB=sB, eB=eB: e.activation(eB, sB, AF.Exp, scale=CON_SCALE_A),
                 reads=[rsB, ("con",)], writes=[("eB", sl)])

        def a0_av(snum, j, ktB2):
            sl = snum % 3
            for hs, ee, ree in ((0, eAslot[sl], ("eA", sl)), (1, eBslot[sl], ("eB", sl))):
                hh = 2 * j + hs
                for qsub in range(2):
                    for ktl in range(2):
                        kt = 2 * ktB2 + ktl
                        first = (ktB2 == 0 and ktl == 0)
                        S.op("pe", lambda e, hs=hs, hh=hh, kt=kt, ktl=ktl, qsub=qsub, ee=ee, ktB2=ktB2: e.matmul(
                            avac[hs][qsub],
                            ee[:, 256 * ktl + 128 * qsub:256 * ktl + 128 * qsub + 128],
                            Vp[:, kt, hh, :],
                            start=(ktB2 == 0 and ktl == 0), stop=(ktB2 == KB2 - 1 and ktl == 1)),
                            reads=[ree, ("Vv", si)],
                            writes=[avkey[hs][qsub]], wmode="w" if first else "acc")

        def a0_norm(j):
            for qsub in range(2):
                S.op("dve", lambda e, qsub=qsub: e.tensor_copy(
                    sm[:, 2 * qsub:2 * qsub + 1], avac[0][qsub][:, 64:65]),
                    reads=[avkey[0][qsub]], writes=[("small", 0)],
                    wmode="w" if qsub == 0 else "acc")
                S.op("dve", lambda e, qsub=qsub: e.tensor_copy(
                    sm[:, 2 * qsub + 1:2 * qsub + 2], avac[1][qsub][:, 64:65]),
                    reads=[avkey[1][qsub]], writes=[("small", 0)], wmode="acc")
            S.op("dve", lambda e: e.reciprocal(sm[:, 4:8], sm[:, 0:4]),
                 reads=[("small", 0)], writes=[("small", 0)], wmode="acc")
            for qsub in range(2):
                yb = y0 if qsub == 0 else C.ybuf[1]
                ryb = ("yb", si, qsub)
                for hs in range(2):
                    h = 2 * j + hs
                    S.op("dve", lambda e, h=h, hs=hs, qsub=qsub, yb=yb: e.tensor_scalar_mul(
                        yb[:, 64 * h:64 * (h + 1)], avac[hs][qsub][:, 0:64],
                        sm[:, 4 + 2 * qsub + hs:5 + 2 * qsub + hs]),
                        reads=[avkey[hs][qsub], ("small", 0)], writes=[ryb],
                        wmode="w" if (j == 0 and hs == 0) else "acc")

        steps = [(j, ktB2) for j in range(8) for ktB2 in range(KB2)]
        LOOKA = 2
        for snum, (j, ktB2) in enumerate(steps):
            a0_scores(snum, j, ktB2)
            if snum >= LOOKA:
                pj, pktB2 = steps[snum - LOOKA]
                a0_av(snum - LOOKA, pj, pktB2)
                if pktB2 == KB2 - 1:
                    a0_norm(pj)
        for snum in range(len(steps) - LOOKA, len(steps)):
            pj, pktB2 = steps[snum]
            a0_av(snum, pj, pktB2)
            if pktB2 == KB2 - 1:
                a0_norm(pj)

        for qsub in range(2):
            qt = 2 * QP + qsub
            yb = y0 if qsub == 0 else C.ybuf[1]
            ryb = ("yb", si, qsub)
            S.op("sp", lambda e, qt=qt, xa=xa: e.dma_start(
                xa[:], x_src[128 * qt:128 * (qt + 1), :]),
                reads=[("x", si)], writes=[rxa], kind="d")
            for c in range(8):
                S.op("pe", lambda e, c=c, yb=yb: e.transpose(
                    ps["ptr"][:, 128 * c:128 * (c + 1)], yb[:, 128 * c:128 * (c + 1)], sb["ident"]),
                    reads=[ryb, ("ident",)], writes=[("ptr",)],
                    wmode="w" if c == 0 else "acc")
            S.op("act", lambda e: e.copy(aoT[:, :, :], ps["ptr"].rearrange("p (c q) -> p c q", c=8)),
                 reads=[("ptr",)], writes=[("aoT",)])
            for h2 in range(2):
                cl = slice(512 * h2, 512 * (h2 + 1))
                for c in range(8):
                    S.op("pe", lambda e, c=c, cl=cl: e.matmul(
                        ps["big0"][:, cl], aoT[:, c, :], wov[:, c, cl],
                        start=(c == 0), stop=False),
                        reads=[("aoT",), ("a0wo",)], writes=[("big0", 0), ("big0", 1)],
                        wmode="w" if (c == 0 and h2 == 0) else "acc")
                S.op("pe", lambda e, cl=cl: e.matmul(
                    ps["big0"][:, cl], sb["ones_col"], wob[0:1, cl],
                    start=False, stop=True),
                    reads=[("ones_col",), ("rowbuf",)], writes=[("big0", 0), ("big0", 1)], wmode="acc")
            xm = sb["xmix0"]
            S.op("dve", lambda e, xa=xa, xm=xm: e.tensor_add(xm[:], ps["big0"][:], xa[:]),
                 reads=[("big0", 0), ("big0", 1), rxa], writes=[("xmix",)])
            S.op("sp", lambda e, qt=qt, xm=xm: e.dma_start(
                x_dst[128 * qt:128 * (qt + 1), :], xm[:]),
                reads=[("xmix",)], writes=[("x", si + 1)],
                wmode="w" if qt == 0 else "acc", kind="d")


def emit_att1(C, x_src, x_dst, si):
    S, sb, ps, dr, nc = C.S, C.sb, C.ps, C.dr, C.nc
    ysc = dr["ysc"]
    BIG1, BIG2, MEDA, MEDB = sb["BIG1"], sb["BIG2"], sb["MEDA"], sb["MEDB"]
    PB, PF, ROW = sb["POOLB"], sb["POOLF"], sb["ROW"]
    qrT = BIG1.rearrange("p (c t) -> p c t", c=8)
    Woutv = BIG2[:, 0:8192].rearrange("p (c E) -> p c E", c=8)
    Vp = BIG2[:, 8192:16512].rearrange("p (t h j) -> p t h j", t=32, h=4)
    krT = MEDA.rearrange("p (k t) -> p k t", k=4)
    Wqav = MEDB[:, 0:512].rearrange("p (c j) -> p c j", c=8)
    Wkvav = MEDB[:, 512:768].rearrange("p (c j) -> p c j", c=8)
    tri = MEDB[:, 768:896]
    qr = MEDB[:, 896:1920]
    qa_g = MEDB[0:64, 1920:2432]
    Wqb = MEDB[0:64, 2432:3456]
    ytg = MEDB[:, 3456:7552].rearrange("p (c q) -> p c q", c=8)
    kva_g = PB[0:32, 0:512]
    Wkvb = PB[0:32, 512:1024]
    dgb = PB[:, 1024:1152]
    aoT = PB[:, 1152:2176].rearrange("p (c q) -> p c q", c=8)
    kr = sb["y0"][:, 0:256]
    cosT = PF[:, 0:32]
    sinT = PF[:, 32:64]
    bout = ROW[:, 176:1200]
    eT0, eT1 = sb["eT0"], sb["eT1"]
    C.ybuf = [sb["y0"], BIG2[:, 16512:17536]]

    S.op("pool", lambda g: g.dma_start(Wqav, dr["a1_Wqa"].rearrange("(c p) j -> p c j", p=128)),
         writes=[("a1wqa",)], kind="d")
    S.op("pool", lambda g: g.dma_start(Wkvav, dr["a1_Wkva"].rearrange("(c p) j -> p c j", p=128)),
         writes=[("a1wkva",)], kind="d")
    S.op("pool", lambda g: g.dma_start(Wqb, dr["a1_Wqb"]), writes=[("a1wqb",)], kind="d")
    S.op("pool", lambda g: g.dma_start(Wkvb, dr["a1_Wkvb"]), writes=[("a1wkvb",)], kind="d")
    S.op("pool", lambda g: g.dma_start(tri, dr["tri"]), writes=[("tri",)], kind="d")
    S.op("pool", lambda g: g.dma_start(
        Woutv, dr["a1_Wout"].rearrange("(c p) E -> p c E", p=128)),
        writes=[("a1wout",)], kind="d")
    S.op("pool", lambda g: g.dma_start(bout, dr["a1_bout"]), writes=[("rowbuf",)], kind="d")
    S.op("dve", lambda e: e.memset(Vp[:, :, :, 64:65], 1.0), writes=[("Vv", si)])

    def a1_ln(t, p, ry):
        ytt = sb["ytt0"]
        rytt = ("ytt", si)
        transpose_y(C, ry, p, t, si,
                    lambda e, ytt=ytt: e.copy(ytt[:, :, :],
                                              ps["ptr"].rearrange("p (c q) -> p c q", c=8)),
                    [rytt])
        S.op("sp", lambda e, ytt=ytt, t=t: e.dma_start(ysc[:, :, 128 * t:128 * (t + 1)], ytt[:, :, :]),
             reads=[rytt], writes=[("ysc", si, t // 4)], kind="d",
             wmode="w" if t % 4 == 0 else "acc")

    ln_loop(C, x_src, si, a1_ln)

    for g in range(NG):
        S.op("sp", lambda e, g=g: e.dma_start(ytg, ysc[:, :, 512 * g:512 * (g + 1)]),
             reads=[("ysc", si, g)], writes=[("ytg", 0)], kind="d")
        for c in range(8):
            S.op("pe", lambda e, c=c: e.matmul(
                ps["med0"][0:64, :], Wqav[:, c, :], ytg[:, c, :],
                start=(c == 0), stop=(c == 7)),
                reads=[("a1wqa",), ("ytg", 0)], writes=[("big1", 0)],
                wmode="w" if c == 0 else "acc")
        S.op("dve", lambda e: e.tensor_copy(qa_g, ps["med0"][0:64, :]),
             reads=[("big1", 0)], writes=[("qa_g",)])
        for c in range(8):
            S.op("pe", lambda e, c=c: e.matmul(
                ps["med1"][0:32, :], Wkvav[:, c, :], ytg[:, c, :],
                start=(c == 0), stop=(c == 7)),
                reads=[("a1wkva",), ("ytg", 0)], writes=[("big1", 1)],
                wmode="w" if c == 0 else "acc")
        S.op("dve", lambda e: e.tensor_copy(kva_g, ps["med1"][0:32, :]),
             reads=[("big1", 1)], writes=[("kva_g",)])
        for tt in range(4):
            t = 4 * g + tt
            S.op("pool", lambda e, t=t: e.dma_start(
                cosT, dr["a1_cos"].rearrange("(t p) j -> t p j", p=128)[t]),
                writes=[("cosT",)], kind="d")
            S.op("pool", lambda e, t=t: e.dma_start(
                sinT, dr["a1_sin"].rearrange("(t p) j -> t p j", p=128)[t]),
                writes=[("sinT",)], kind="d")
            for h2 in range(2):
                cl = slice(512 * h2, 512 * (h2 + 1))
                S.op("pe", lambda e, tt=tt, cl=cl: e.matmul(
                    ps["big0"][:, cl], qa_g[:, 128 * tt:128 * (tt + 1)], Wqb[:, cl],
                    start=True, stop=True),
                    reads=[("qa_g",), ("a1wqb",)], writes=[("big0", h2)],
                    wmode="w")
            x1 = ps["big0"].rearrange("p (h d) -> p h d", h=16)[:, :, 0:64:2]
            x2 = ps["big0"].rearrange("p (h d) -> p h d", h=16)[:, :, 1:64:2]
            cB = cosT[:, None, :].broadcast_to([128, 16, 32])
            sB = sinT[:, None, :].broadcast_to([128, 16, 32])
            t1 = eT0[:, 0:512].rearrange("p (h d) -> p h d", h=16)
            t2 = eT0[:, 512:1024].rearrange("p (h d) -> p h d", h=16)
            qre = qr.rearrange("p (h d) -> p h d", h=16)[:, :, 0:64:2]
            qro = qr.rearrange("p (h d) -> p h d", h=16)[:, :, 1:64:2]
            S.op("dve", lambda e, x1=x1, cB=cB, t1=t1: e.tensor_mul(t1, x1, cB),
                 reads=[("big0", 0), ("big0", 1), ("cosT",)], writes=[("eT", 0)])
            S.op("dve", lambda e, x2=x2, sB=sB, t2=t2: e.tensor_mul(t2, x2, sB),
                 reads=[("big0", 0), ("big0", 1), ("sinT",)], writes=[("eT", 0)], wmode="acc")
            S.op("pool", lambda e, t1=t1, t2=t2, qre=qre: e.tensor_sub(qre, t1, t2),
                 reads=[("eT", 0)], writes=[("qr",)])
            S.op("dve", lambda e, x1=x1, sB=sB, t1=t1: e.tensor_mul(t1, x1, sB),
                 reads=[("big0", 0), ("big0", 1), ("sinT",)], writes=[("eT", 0)])
            S.op("dve", lambda e, x2=x2, cB=cB, t2=t2: e.tensor_mul(t2, x2, cB),
                 reads=[("big0", 0), ("big0", 1), ("cosT",)], writes=[("eT", 0)], wmode="acc")
            S.op("pool", lambda e, t1=t1, t2=t2, qro=qro: e.tensor_add(qro, t1, t2),
                 reads=[("eT", 0)], writes=[("qr",)], wmode="acc")
            for c in range(8):
                S.op("pe", lambda e, c=c: e.transpose(
                    ps["ptr"][:, 128 * c:128 * (c + 1)], qr[:, 128 * c:128 * (c + 1)], sb["ident"]),
                    reads=[("qr",), ("ident",)], writes=[("ptr",)],
                    wmode="w" if c == 0 else "acc")
            S.op("act", lambda e, t=t: e.copy(
                qrT[:, :, 128 * t:128 * (t + 1)], ps["ptr"].rearrange("p (c q) -> p c q", c=8)),
                reads=[("ptr",)], writes=[("qrT",)],
                wmode="w" if t == 0 else "acc")
            S.op("pe", lambda e, tt=tt: e.matmul(
                ps["med0"][:], kva_g[:, 128 * tt:128 * (tt + 1)], Wkvb[:],
                start=True, stop=True),
                reads=[("kva_g",), ("a1wkvb",)], writes=[("big1", 0)], wmode="w")
            kx1 = ps["med0"].rearrange("p (h d) -> p h d", h=8)[:, 0:4, 0:64:2]
            kx2 = ps["med0"].rearrange("p (h d) -> p h d", h=8)[:, 0:4, 1:64:2]
            kcB = cosT[:, None, :].broadcast_to([128, 4, 32])
            ksB = sinT[:, None, :].broadcast_to([128, 4, 32])
            kt1 = eT0[:, 0:128].rearrange("p (h d) -> p h d", h=4)
            kt2 = eT0[:, 128:256].rearrange("p (h d) -> p h d", h=4)
            kre = kr.rearrange("p (h d) -> p h d", h=4)[:, :, 0:64:2]
            kro = kr.rearrange("p (h d) -> p h d", h=4)[:, :, 1:64:2]
            S.op("dve", lambda e, kx1=kx1, kcB=kcB, kt1=kt1: e.tensor_mul(kt1, kx1, kcB),
                 reads=[("big1", 0), ("cosT",)], writes=[("eT", 0)])
            S.op("dve", lambda e, kx2=kx2, ksB=ksB, kt2=kt2: e.tensor_mul(kt2, kx2, ksB),
                 reads=[("big1", 0), ("sinT",)], writes=[("eT", 0)], wmode="acc")
            S.op("pool", lambda e, kt1=kt1, kt2=kt2, kre=kre: e.tensor_sub(kre, kt1, kt2),
                 reads=[("eT", 0)], writes=[("y", si)])
            S.op("dve", lambda e, kx1=kx1, ksB=ksB, kt1=kt1: e.tensor_mul(kt1, kx1, ksB),
                 reads=[("big1", 0), ("sinT",)], writes=[("eT", 0)])
            S.op("dve", lambda e, kx2=kx2, kcB=kcB, kt2=kt2: e.tensor_mul(kt2, kx2, kcB),
                 reads=[("big1", 0), ("cosT",)], writes=[("eT", 0)], wmode="acc")
            S.op("pool", lambda e, kt1=kt1, kt2=kt2, kro=kro: e.tensor_add(kro, kt1, kt2),
                 reads=[("eT", 0)], writes=[("y", si)], wmode="acc")
            S.op("act", lambda e, t=t: e.copy(
                Vp[:, t, :, 0:64], ps["med0"][:, 256:512].rearrange("p (h j) -> p h j", h=4)),
                reads=[("big1", 0)], writes=[("Vv", si)], wmode="acc")
            for k4 in range(4):
                S.op("pe", lambda e, k4=k4: e.transpose(
                    ps["ptr"][0:64, 128 * k4:128 * k4 + 128], kr[:, 64 * k4:64 * (k4 + 1)], sb["ident"]),
                    reads=[("y", si), ("ident",)], writes=[("ptr",)],
                    wmode="w" if k4 == 0 else "acc")
                S.op("pe", lambda e, k4=k4: e.transpose(
                    ps["ptr"][64:128, 128 * k4:128 * k4 + 128], kr[:, 64 * k4:64 * (k4 + 1)], sb["ident"]),
                    reads=[("y", si), ("ident",)], writes=[("ptr",)], wmode="acc")
            S.op("act", lambda e, t=t: e.copy(
                krT[:, :, 128 * t:128 * (t + 1)], ps["ptr"][:, 0:512].rearrange("p (k q) -> p k q", k=4)),
                reads=[("ptr",)], writes=[("krT",)],
                wmode="w" if t == 0 else "acc")

    ptrf = ps["ptr"].bitcast(F32)
    eAslot = [eT0[:, 0:512], eT0[:, 512:1024], PB[:, 2176:2688]]
    eBslot = [eT1[:, 0:512], eT1[:, 512:1024], PB[:, 2688:3200]]
    avac = [[ps["pd"][:, 0:65], ps["pd"][:, 1024:1089]],
            [ps["pd"][:, 512:577], ptrf[:, 0:65]]]
    avkey = [[("pd",), ("pd", 2)], [("pd", 1), ("ptr",)]]
    aoT = sb["ytt0"]
    sm = sb["small"]
    y0 = sb["y0"]
    xa = sb["xa0"]
    rxa = ("xa", si, 0)
    for QP in range(16):
        nkt = 2 * QP + 2
        KB2 = (nkt + 1) // 2

        def a1_scores(snum, j, ktB2):
            par = snum % 2
            sl = snum % 3
            kts = [k for k in (2 * ktB2, 2 * ktB2 + 1) if k < nkt]
            sA = ps["big0"][:, 512 * par:512 * (par + 1)]
            sB = ps["big1"][:, 512 * par:512 * (par + 1)]
            rsA = ("big0", par)
            rsB = ("big1", par)
            for ktl, kt in enumerate(kts):
                S.op("pe", lambda e, j=j, kt=kt, ktl=ktl, sA=sA, QP=QP: e.matmul(
                    sA[:, 256 * ktl:256 * (ktl + 1)], krT[0:64, j // 2, 128 * kt:128 * (kt + 1)],
                    qrT[0:64, j, 256 * QP:256 * (QP + 1)], start=True, stop=True,
                    tile_position=(0, 0)),
                    reads=[("krT",), ("qrT",)], writes=[rsA],
                    wmode="w" if ktl == 0 else "acc")
                S.op("pe", lambda e, j=j, kt=kt, ktl=ktl, sB=sB, QP=QP: e.matmul(
                    sB[:, 256 * ktl:256 * (ktl + 1)], krT[64:128, j // 2, 128 * kt:128 * (kt + 1)],
                    qrT[64:128, j, 256 * QP:256 * (QP + 1)], start=True, stop=True,
                    tile_position=(64, 0)),
                    reads=[("krT",), ("qrT",)], writes=[rsB],
                    wmode="w" if ktl == 0 else "acc")
            eA = eAslot[sl]
            eB = eBslot[sl]
            nc4 = 256 * len(kts)
            S.op("act", lambda e, sA=sA, eA=eA, nc4=nc4: e.activation(
                eA[:, 0:nc4], sA[:, 0:nc4], AF.Exp, scale=CON_SCALE_A),
                reads=[rsA, ("con",)], writes=[("eA", sl)])
            S.op("act", lambda e, sB=sB, eB=eB, nc4=nc4: e.activation(
                eB[:, 0:nc4], sB[:, 0:nc4], AF.Exp, scale=CON_SCALE_A),
                reads=[rsB, ("con",)], writes=[("eB", sl)])

        def a1_av(snum, j, ktB2):
            sl = snum % 3
            kts = [k for k in (2 * ktB2, 2 * ktB2 + 1) if k < nkt]
            for hs, ee, ree in ((0, eAslot[sl], ("eA", sl)), (1, eBslot[sl], ("eB", sl))):
                hh = 2 * j + hs
                kvh = hh // 4
                for qsub in range(2):
                    qglob = 2 * QP + qsub
                    for ktl, kt in enumerate(kts):
                        if kt > qglob:
                            continue
                        first = (ktB2 == 0 and ktl == 0)
                        stop = (kt == qglob)
                        diag = (kt == qglob)
                        src_ap = ee[:, 256 * ktl + 128 * qsub:256 * ktl + 128 * qsub + 128]
                        if diag:
                            S.op("dve", lambda e, src_ap=src_ap: e.tensor_mul(
                                dgb[:], src_ap, tri[:]),
                                reads=[ree, ("tri",)], writes=[("dgb",)])
                            rl = ("dgb",)
                        else:
                            rl = ree
                        S.op("pe", (lambda e, hs=hs, hh=hh, kt=kt, ktl=ktl, qsub=qsub,
                                    ee=ee, kvh=kvh, diag=diag, src_ap=src_ap:
                                    e.matmul(avac[hs][qsub],
                                             dgb[:] if diag else src_ap,
                                             Vp[:, kt, kvh, :],
                                             start=(kt == 0), stop=diag)),
                            reads=[rl, ("Vv", si)],
                            writes=[avkey[hs][qsub]] if not first else
                            [avkey[hs][qsub], ("pd",), ("pd", 1), ("pd", 2), ("ptr",)],
                            wmode="w" if first else "acc")

        def a1_norm(j):
            for qsub in range(2):
                S.op("dve", lambda e, qsub=qsub: e.tensor_copy(
                    sm[:, 2 * qsub:2 * qsub + 1], avac[0][qsub][:, 64:65]),
                    reads=[avkey[0][qsub]], writes=[("small", 0)],
                    wmode="w" if qsub == 0 else "acc")
                S.op("dve", lambda e, qsub=qsub: e.tensor_copy(
                    sm[:, 2 * qsub + 1:2 * qsub + 2], avac[1][qsub][:, 64:65]),
                    reads=[avkey[1][qsub]], writes=[("small", 0)], wmode="acc")
            S.op("dve", lambda e: e.reciprocal(sm[:, 4:8], sm[:, 0:4]),
                 reads=[("small", 0)], writes=[("small", 0)], wmode="acc")
            for qsub in range(2):
                yb = y0 if qsub == 0 else C.ybuf[1]
                ryb = ("yb", si, qsub)
                for hs in range(2):
                    h = 2 * j + hs
                    S.op("dve", lambda e, h=h, hs=hs, qsub=qsub, yb=yb: e.tensor_scalar_mul(
                        yb[:, 64 * h:64 * (h + 1)], avac[hs][qsub][:, 0:64],
                        sm[:, 4 + 2 * qsub + hs:5 + 2 * qsub + hs]),
                        reads=[avkey[hs][qsub], ("small", 0)], writes=[ryb],
                        wmode="w" if (j == 0 and hs == 0) else "acc")

        steps = [(j, ktB2) for j in range(8) for ktB2 in range(KB2)]
        LOOKA = 2 if len(steps) >= 2 else 1
        for snum, (j, ktB2) in enumerate(steps):
            a1_scores(snum, j, ktB2)
            if snum >= LOOKA:
                pj, pktB2 = steps[snum - LOOKA]
                a1_av(snum - LOOKA, pj, pktB2)
                if pktB2 == KB2 - 1:
                    a1_norm(pj)
        for snum in range(len(steps) - LOOKA, len(steps)):
            pj, pktB2 = steps[snum]
            a1_av(snum, pj, pktB2)
            if pktB2 == KB2 - 1:
                a1_norm(pj)

        for qsub in range(2):
            qt = 2 * QP + qsub
            yb = y0 if qsub == 0 else C.ybuf[1]
            ryb = ("yb", si, qsub)
            S.op("sp", lambda e, qt=qt, xa=xa: e.dma_start(
                xa[:], x_src[128 * qt:128 * (qt + 1), :]),
                reads=[("x", si)], writes=[rxa], kind="d")
            for c in range(8):
                S.op("pe", lambda e, c=c, yb=yb: e.transpose(
                    ps["ptr"][:, 128 * c:128 * (c + 1)], yb[:, 128 * c:128 * (c + 1)], sb["ident"]),
                    reads=[ryb, ("ident",)], writes=[("ptr",)],
                    wmode="w" if c == 0 else "acc")
            S.op("act", lambda e: e.copy(aoT[:, :, :], ps["ptr"].rearrange("p (c q) -> p c q", c=8)),
                 reads=[("ptr",)], writes=[("aoT",)])
            for h2 in range(2):
                cl = slice(512 * h2, 512 * (h2 + 1))
                for c in range(8):
                    S.op("pe", lambda e, c=c, cl=cl: e.matmul(
                        ps["big0"][:, cl], aoT[:, c, :], Woutv[:, c, cl],
                        start=(c == 0), stop=False),
                        reads=[("aoT",), ("a1wout",)], writes=[("big0", 0), ("big0", 1)],
                        wmode="w" if (c == 0 and h2 == 0) else "acc")
                S.op("pe", lambda e, cl=cl: e.matmul(
                    ps["big0"][:, cl], sb["ones_col"], bout[0:1, cl],
                    start=False, stop=True),
                    reads=[("ones_col",), ("rowbuf",)], writes=[("big0", 0), ("big0", 1)], wmode="acc")
            xm = sb["xmix0"]
            S.op("dve", lambda e, xa=xa, xm=xm: e.tensor_add(xm[:], ps["big0"][:], xa[:]),
                 reads=[("big0", 0), ("big0", 1), rxa], writes=[("xmix",)])
            S.op("sp", lambda e, qt=qt, xm=xm: e.dma_start(
                x_dst[128 * qt:128 * (qt + 1), :], xm[:]),
                reads=[("xmix",)], writes=[("x", si + 1)],
                wmode="w" if qt == 0 else "acc", kind="d")


def emit_moe(C, x_src, x_dst, si):
    S, sb, ps, dr, nc = C.S, C.sb, C.ps, C.dr, C.nc
    ysc = dr["ysc"]
    BIG1, BIG2, MEDA, MEDB = sb["BIG1"], sb["BIG2"], sb["MEDA"], sb["MEDB"]
    PB, PF, ROW = sb["POOLB"], sb["POOLF"], sb["ROW"]
    w1sb = BIG1.rearrange("p (c f q) -> p c f q", c=8, f=32)
    w2sb = BIG2[:, 0:32768].rearrange("p (f h) -> p f h", f=32)
    gT = MEDA[:, 0:16384].rearrange("p (f q) -> p f q", f=32)
    ytg = MEDB.rearrange("p (b c q) -> p b c q", b=2, c=8)
    routerT = PB[:, 0:64].rearrange("p (c e) -> p c e", c=8)
    c05b = PB[:, 64:320]
    equ = PB[:, 320:328].bitcast(U8)
    eq1 = equ[:, 0:8]; eq2 = equ[:, 8:16]
    b1sb = PF[:, 0:256].rearrange("p (e f) -> p e f", e=8)
    logitsAll = PF[:, 256:512]
    dAll = PF[:, 512:544]
    w1All = PF[:, 544:576]
    w2All = PF[:, 576:608]
    m1c = PF[:, 608:609]; m2c = PF[:, 609:610]
    l2buf = PF[:, 616:624]
    neg30 = PF[:, 624:632]
    b2row = ROW[:, 176:1200]
    C.ybuf = [sb["y0"], PB[:, 1024:2048]]

    S.op("pool", lambda g: g.dma_start(
        routerT, dr["m_routerT"].rearrange("(c p) e -> p c e", p=128)),
        writes=[("mrt",)], kind="d")
    S.op("sp", lambda e: e.dma_start(
        b1sb, dr["m_b1"].rearrange("e p f -> p e f")),
        writes=[("mb1",)], kind="d")
    S.op("dve", lambda e: e.memset(neg30, -1e30), writes=[("neg30",)])
    S.op("pool", lambda e: e.dma_start(x_dst, x_src),
         reads=[("x", si)], writes=[("x", si + 1), ("xinit", si)], kind="d")

    def moe_ln(t, p, ry):
        ytt = sb["ytt0"]
        rytt = ("ytt", si)
        transpose_y(C, ry, p, t, si,
                    lambda e, ytt=ytt: e.copy(ytt[:, :, :],
                                              ps["ptr"].rearrange("p (c q) -> p c q", c=8)),
                    [rytt])
        S.op("sp", lambda e, ytt=ytt, t=t: e.dma_start(ysc[:, :, 128 * t:128 * (t + 1)], ytt[:, :, :]),
             reads=[rytt], writes=[("ysc", si, t // 4)], kind="d",
             wmode="w" if t % 4 == 0 else "acc")
        mp = ps["med%d" % p]
        rmp = ("big1", p)
        for c in range(8):
            S.op("pe", lambda e, c=c, ytt=ytt, mp=mp: e.matmul(
                mp[:, 0:8], ytt[:, c, :], routerT[:, c, :],
                start=(c == 0), stop=(c == 7)),
                reads=[rytt, ("mrt",)], writes=[rmp],
                wmode="w" if c == 0 else "acc")
        S.op("dve", lambda e, mp=mp, t=t: e.tensor_copy(
            logitsAll[:, 8 * t:8 * (t + 1)], mp[:, 0:8]),
            reads=[rmp], writes=[("logits",)],
            wmode="w" if t < 2 else "acc")

    ln_loop(C, x_src, si, moe_ln)

    for t in range(NT):
        lt = logitsAll[:, 8 * t:8 * (t + 1)]
        S.op("dve", lambda e, lt=lt: e.reduce_max(m1c, lt, axis=AX.X),
             reads=[("logits",)], writes=[("rt",)])
        S.op("dve", lambda e, lt=lt: e.tensor_single_scalar(eq1, lt, m1c, ALU.is_equal),
             reads=[("logits",), ("rt",)], writes=[("equ", 1)])
        S.op("dve", lambda e, lt=lt: e.select(l2buf, eq1, neg30, lt, add_drain=True),
             reads=[("logits",), ("equ", 1), ("neg30",)], writes=[("l2",)])
        S.op("dve", lambda e: e.reduce_max(m2c, l2buf, axis=AX.X),
             reads=[("l2",)], writes=[("rt",)], wmode="acc")
        S.op("dve", lambda e: e.tensor_single_scalar(eq2, l2buf, m2c, ALU.is_equal),
             reads=[("l2",), ("rt",)], writes=[("equ", 2)])
        S.op("dve", lambda e, t=t: e.scalar_tensor_tensor(
            dAll[:, t:t + 1], m1c, 1.0, m2c, ALU.mult, ALU.subtract),
            reads=[("rt",)], writes=[("dAll",)],
            wmode="w" if t == 0 else "acc")
        S.op("dve", lambda e, t=t: e.tensor_copy(c05b[:, 8 * t:8 * (t + 1)], eq2),
             reads=[("equ", 2)], writes=[("c05",)],
             wmode="w" if t == 0 else "acc")
        S.op("dve", lambda e, t=t: e.tensor_copy(logitsAll[:, 8 * t:8 * (t + 1)], eq1),
             reads=[("equ", 1)], writes=[("logits",)], wmode="acc")
    S.op("act", lambda e: e.activation(w1All, dAll, AF.Sigmoid),
         reads=[("dAll",)], writes=[("w1All",)])
    S.op("dve", lambda e: e.tensor_scalar(w2All, w1All, -0.5, 0.5, ALU.mult, ALU.add),
         reads=[("w1All",)], writes=[("w2All",)])
    for t in range(NT):
        S.op("dve", lambda e, t=t: e.tensor_scalar(
            l2buf, logitsAll[:, 8 * t:8 * (t + 1)], w1All[:, t:t + 1], 0.5, ALU.mult, ALU.mult),
            reads=[("logits",), ("w1All",)], writes=[("l2",)])
        S.op("dve", lambda e, t=t: e.scalar_tensor_tensor(
            logitsAll[:, 8 * t:8 * (t + 1)], c05b[:, 8 * t:8 * (t + 1)],
            w2All[:, t:t + 1], l2buf, ALU.mult, ALU.add),
            reads=[("c05",), ("w2All",), ("l2",)], writes=[("logits",)], wmode="acc")

    for e_ in range(8):
        for fq in range(4):
            S.op("pool", lambda g, e_=e_, fq=fq: g.dma_start(
                w1sb[:, :, 8 * fq:8 * (fq + 1), :],
                dr["m_w1"][e_].rearrange("(c p) (f q) -> p c f q", p=128, q=128)[:, :, 8 * fq:8 * (fq + 1), :]),
                writes=[("w1m", fq)], kind="d")
        S.op("pool", lambda g, e_=e_: g.dma_start(
            w2sb, dr["m_w2"][e_].rearrange("(f p) h -> p f h", p=128)),
            writes=[("w2m",)], kind="d")
        S.op("pool", lambda g, e_=e_: g.dma_start(
            b2row, dr["m_b2"][0:1, D * e_:D * (e_ + 1)]),
            writes=[("rowbuf",)], kind="d")
        for g in range(NG):
            gp = g % 2
            rytg = ("ytg", gp)
            S.op("sp", lambda e, g=g, gp=gp: e.dma_start(
                ytg[:, gp], ysc[:, :, 512 * g:512 * (g + 1)]),
                reads=[("ysc", si, g)], writes=[rytg], kind="d")
            for F in range(32):
                par = F % 2
                mp = ps["med%d" % par]
                rmp = ("big1", par)
                for c in range(8):
                    S.op("pe", lambda e, c=c, F=F, gp=gp, mp=mp: e.matmul(
                        mp[:], w1sb[:, c, F, :], ytg[:, gp, c, :],
                        start=(c == 0), stop=(c == 7)),
                        reads=[rytg, ("w1m", F // 8)], writes=[rmp],
                        wmode="w" if c == 0 else "acc")
                S.op("act", lambda e, F=F, e_=e_, mp=mp: e.activation(
                    gT[:, F, :], mp[:], AF.Gelu, bias=b1sb[:, e_, F:F + 1]),
                    reads=[rmp, ("mb1",)], writes=[("gT", F)])
            for tt in range(4):
                t = 4 * g + tt
                bp = tt % 2
                bigp = ps["big0"] if bp == 0 else ps["pd"][:, 0:1024]
                rbig = ("big0",) if bp == 0 else ("pd",)
                for h2 in range(2):
                    cl = slice(512 * h2, 512 * (h2 + 1))
                    for fi, F in enumerate(range(31, -1, -1)):
                        S.op("pe", lambda e, F=F, tt=tt, cl=cl, bigp=bigp, fi=fi: e.matmul(
                            bigp[:, cl], gT[:, F, 128 * tt:128 * (tt + 1)], w2sb[:, F, cl],
                            start=(fi == 0), stop=False),
                            reads=[("gT", F), ("w2m",)], writes=[rbig],
                            wmode="w" if (fi == 0 and h2 == 0) else "acc")
                    S.op("pe", lambda e, cl=cl, bigp=bigp: e.matmul(
                        bigp[:, cl], sb["ones_col"], b2row[0:1, cl],
                        start=False, stop=True),
                        reads=[("ones_col",), ("rowbuf",)], writes=[rbig], wmode="acc")
                xm = sb["xmix0"]
                S.op("act", lambda e, t=t, e_=e_, xm=xm, bigp=bigp: e.mul(
                    xm[:], bigp[:], logitsAll[:, 8 * t + e_:8 * t + e_ + 1]),
                    reads=[rbig, ("logits",)], writes=[("xmix",)])
                S.op("pool", lambda e, t=t, xm=xm: e.dma_start(
                    x_dst[128 * t:128 * (t + 1), :], xm[:], accum_op=ALU.add),
                    reads=[("xmix",), ("xinit", si)], writes=[("x", si + 1)],
                    kind="d", wmode="acc")


def kernel(**inputs):
    w = prep_weights(inputs)
    x = np.ascontiguousarray(inputs["L0_attn_x"], dtype=np.float32)
    cur = [np.ascontiguousarray(x[b]) for b in range(8)]
    for stages in STAGE_GROUPS:
        nc, innames = build_nc(list(stages))
        in_maps = []
        for b in range(8):
            m = {}
            for n in innames:
                if n == "xin":
                    m[n] = cur[b]
                else:
                    m[n] = w[n]
            in_maps.append(m)
        res = run_bass_kernel_spmd(nc, in_maps, core_ids=list(range(8)))
        cur = [np.asarray(res.results[b]["xout"], dtype=np.float32) for b in range(8)]
    return np.stack(cur, axis=0)


# version 14
# speedup vs baseline: 1.0119x; 1.0119x over previous
import numpy as np
import concourse.bass as bass
import concourse.mybir as mybir
from concourse.bass_utils import run_bass_kernel_spmd

F32 = mybir.dt.float32
BF16 = mybir.dt.bfloat16
U8 = mybir.dt.uint8
AF = mybir.ActivationFunctionType
ALU = mybir.AluOpType
AX = mybir.AxisListType

T = 4096
D = 1024
NT = 32
NG = 8

ALL_STAGES = ("fff1", "att0", "fff2", "moe1", "att1", "moe2")
STAGE_GROUPS = [ALL_STAGES]
DEBUG_TAPS = False


class _Op:
    __slots__ = ("stream", "emit", "kind", "pos", "sem", "val", "waits", "know")

    def __init__(self, stream, emit, kind, pos):
        self.stream = stream
        self.emit = emit
        self.kind = kind
        self.pos = pos
        self.sem = None
        self.val = None
        self.waits = []
        self.know = None


class Sched:

    CSTREAMS = ("pe", "act", "dve", "pool")
    DSTREAMS = ("sp", "pool")
    NDSEM = 12

    def __init__(self):
        self.ops = {s: [] for s in ("pe", "act", "dve", "sp", "pool")}
        self.ccnt = {s: 0 for s in self.CSTREAMS}
        self.last_assigned_pos = {s: -1 for s in self.CSTREAMS}
        self.didx = {s: 0 for s in self.DSTREAMS}
        self.dsem_last = {}
        self.know = {s: {} for s in ("pe", "act", "dve", "sp", "pool")}
        self.snap = {}
        self.reg = {}
        self.sem_totals = {}
        self._barrier_deps = []
        self._barrier_absorbed = set()

    def _completion(self, op):
        if op.sem is not None:
            if op.kind == "d":
                return op.sem, self.dsem_last[op.sem]
            return op.sem, op.val
        s = op.stream
        assert op.kind == "c"
        if op.pos > self.last_assigned_pos[s]:
            self.ccnt[s] += 1
            op.sem = "c_" + s
            op.val = self.ccnt[s]
            self.last_assigned_pos[s] = op.pos
            know = dict(op.know)
            know["c_" + s] = op.val
            self.snap[(op.sem, op.val)] = know
            self.sem_totals[op.sem] = op.val
            return op.sem, op.val
        for later in self.ops[s][op.pos + 1:]:
            if later.sem is not None and later.kind == "c":
                return later.sem, later.val
        raise AssertionError("inconsistent inc assignment")

    def _add_wait(self, op, dep_op):
        sem, val = self._completion(dep_op)
        know = self.know[op.stream]
        if know.get(sem, 0) >= val:
            return
        op.waits.append((sem, val))
        snap = self.snap.get((sem, val))
        if snap:
            for k, v in snap.items():
                if know.get(k, 0) < v:
                    know[k] = v
        if know.get(sem, 0) < val:
            know[sem] = val

    def op(self, stream, emit, reads=(), writes=(), kind="c", wmode="w"):
        lst = self.ops[stream]
        o = _Op(stream, emit, kind, len(lst))
        if self._barrier_deps and stream not in self._barrier_absorbed:
            self._barrier_absorbed.add(stream)
            for d in self._barrier_deps:
                if d is not None:
                    self._add_wait(o, d)
        deps = []
        for r in reads:
            e = self.reg.get(r)
            if e and e[0] is not None:
                deps.append(e[0])
        if wmode == "w":
            for w in writes:
                e = self.reg.get(w)
                if e:
                    if e[0] is not None:
                        deps.append(e[0])
                    deps.extend(e[1])
        seen = set()
        for d in deps:
            if d is o or id(d) in seen:
                continue
            seen.add(id(d))
            self._add_wait(o, d)
        if kind == "d":
            i = self.didx[stream]
            self.didx[stream] = i + 1
            o.sem = "d_%s_%d" % (stream, i % self.NDSEM)
            o.val = 16 * (i // self.NDSEM + 1)
            if i >= self.NDSEM:
                know = self.know[stream]
                if know.get(o.sem, 0) < o.val - 16:
                    o.waits.append((o.sem, o.val - 16))
                    snap = self.snap.get((o.sem, o.val - 16))
                    if snap:
                        for k, v in snap.items():
                            if know.get(k, 0) < v:
                                know[k] = v
                    know[o.sem] = max(know.get(o.sem, 0), o.val - 16)
            self.dsem_last[o.sem] = o.val
            self.sem_totals[o.sem] = max(self.sem_totals.get(o.sem, 0), o.val)
        o.know = dict(self.know[stream])
        if o.sem is not None:
            know = dict(o.know)
            know[o.sem] = o.val
            self.snap[(o.sem, o.val)] = know
        lst.append(o)
        for r in reads:
            e = self.reg.setdefault(r, [None, [], "w"])
            e[1].append(o)
        for w in writes:
            e = self.reg.setdefault(w, [None, [], "w"])
            e[0] = o
            if wmode == "w":
                e[1] = []
        return o

    def soft_barrier(self):
        deps = []
        for s in ("pe", "act", "dve", "sp", "pool"):
            if self.ops[s]:
                deps.append(self.ops[s][-1])
        self._barrier_deps = deps
        self._barrier_absorbed = set()

    def barrier_regions(self, keys, dep_keys):
        ops = []
        for k in dep_keys:
            e = self.reg.get(k)
            if e:
                if e[0] is not None:
                    ops.append(e[0])
                ops.extend(e[1])
                del self.reg[k]
        if not ops:
            return
        for k in keys:
            e = self.reg.setdefault(k, [None, [], "w"])
            e[1].extend(ops)

    def finalize(self, nc, stack):
        sems = {}
        for s in self.CSTREAMS:
            sems["c_" + s] = stack.enter_context(nc.semaphore("c_" + s))
        for s in self.DSTREAMS:
            for j in range(self.NDSEM):
                sems["d_%s_%d" % (s, j)] = stack.enter_context(
                    nc.semaphore("d_%s_%d" % (s, j)))
        block = stack.enter_context(nc.Block())
        sched = self

        def run_stream(eng, name):
            for o in sched.ops[name]:
                for sem, val in o.waits:
                    eng.wait_ge(sems[sem], val)
                instr = o.emit(eng)
                if o.sem is not None:
                    instr.then_inc(sems[o.sem], 1 if o.kind == "c" else 16)

        @block.tensor
        def _(t):
            run_stream(t, "pe")

        @block.scalar
        def _(s):
            run_stream(s, "act")

        @block.vector
        def _(v):
            run_stream(v, "dve")

        @block.gpsimd
        def _(g):
            run_stream(g, "pool")

        @block.sync
        def _(sy):
            run_stream(sy, "sp")
            for sem, val in sorted(sched.sem_totals.items()):
                sy.wait_ge(sems[sem], val)


_INPUT_NAMES = (
    "L0_attn_x", "L0_attn_wq", "L0_attn_wk", "L0_attn_wv", "L0_attn_wo_w",
    "L0_attn_wo_b", "L0_ffn_q_node_w", "L0_ffn_q_node_b", "L0_ffn_q_w1",
    "L0_ffn_q_b1", "L0_ffn_q_w2", "L0_ffn_q_b2", "L0_ffn_k_node_w",
    "L0_ffn_k_node_b", "L0_ffn_k_w1", "L0_ffn_k_b1", "L0_ffn_k_w2",
    "L0_ffn_k_b2", "L0_ffn_v_node_w", "L0_ffn_v_node_b", "L0_ffn_v_w1",
    "L0_ffn_v_b1", "L0_ffn_v_w2", "L0_ffn_v_b2", "L1_attn_Wqa", "L1_attn_Wqb",
    "L1_attn_Wkva", "L1_attn_Wkvb", "L1_attn_Wout", "L1_attn_bout",
    "L1_ffn_router_w", "L1_ffn_w1", "L1_ffn_b1", "L1_ffn_w2", "L1_ffn_b2",
)


def prep_weights(inp):
    for _n in _INPUT_NAMES:
        assert _n in inp, _n
    w = {}
    nw = np.concatenate([inp["L0_ffn_%s_node_w" % i].T for i in "qkv"], axis=1)
    w["f_nodeW"] = np.ascontiguousarray(nw)
    nb = np.concatenate([inp["L0_ffn_%s_node_b" % i] for i in "qkv"])
    w["f_nodeB"] = np.ascontiguousarray(nb.reshape(1, 45))
    for i in "qkv":
        w["f_w1_" + i] = np.ascontiguousarray(
            inp["L0_ffn_%s_w1" % i].transpose(1, 0, 2).reshape(D, D))
        w["f_b1_" + i] = np.ascontiguousarray(
            inp["L0_ffn_%s_b1" % i].reshape(1, D))
        w["f_w2_" + i] = np.ascontiguousarray(
            inp["L0_ffn_%s_w2" % i].reshape(D, D))
        w["f_b2_" + i] = np.ascontiguousarray(inp["L0_ffn_%s_b2" % i])
    perm = np.concatenate([np.arange(0, D, 2), np.arange(1, D, 2)])
    w["a0_wqT"] = np.ascontiguousarray(inp["L0_attn_wq"][perm, :].T)
    w["a0_wkT"] = np.ascontiguousarray(inp["L0_attn_wk"][perm, :].T)
    w["a0_wvT"] = np.ascontiguousarray(inp["L0_attn_wv"].T)
    w["a0_woT"] = np.ascontiguousarray(inp["L0_attn_wo_w"].T)
    w["a0_wob"] = np.ascontiguousarray(inp["L0_attn_wo_b"].reshape(1, D))
    r = np.arange(512, dtype=np.float64)
    fr = np.exp(r * 2 * (-np.log(10000.0) / D))
    ang = np.arange(T, dtype=np.float64)[None, :] * fr[:, None]
    w["a0_cos"] = np.cos(ang).astype(np.float32)
    w["a0_sin"] = np.sin(ang).astype(np.float32)
    w["a1_Wqa"] = np.ascontiguousarray(inp["L1_attn_Wqa"])
    w["a1_Wqb"] = np.ascontiguousarray(inp["L1_attn_Wqb"])
    w["a1_Wkva"] = np.ascontiguousarray(inp["L1_attn_Wkva"])
    kvb = inp["L1_attn_Wkvb"].reshape(32, 4, 64, 2)
    w["a1_Wkvb"] = np.ascontiguousarray(
        np.concatenate([kvb[..., 0].reshape(32, 256),
                        kvb[..., 1].reshape(32, 256)], axis=1))
    w["a1_Wout"] = np.ascontiguousarray(inp["L1_attn_Wout"])
    w["a1_bout"] = np.ascontiguousarray(inp["L1_attn_bout"].reshape(1, D))
    j = np.arange(32, dtype=np.float64)
    invf = 1.0 / 10000.0 ** (2 * j / 64)
    ang1 = np.arange(T, dtype=np.float64)[:, None] * invf[None, :]
    w["a1_cos"] = np.cos(ang1).astype(np.float32)
    w["a1_sin"] = np.sin(ang1).astype(np.float32)
    w["m_routerT"] = np.ascontiguousarray(inp["L1_ffn_router_w"].T)
    w["m_w1"] = np.ascontiguousarray(inp["L1_ffn_w1"].transpose(0, 2, 1))
    w["m_b1"] = np.ascontiguousarray(
        inp["L1_ffn_b1"].reshape(8, 32, 128).transpose(0, 2, 1))
    w["m_w2"] = np.ascontiguousarray(inp["L1_ffn_w2"].transpose(0, 2, 1))
    w["m_b2"] = np.ascontiguousarray(inp["L1_ffn_b2"].reshape(1, 8 * D))
    w["ident"] = np.eye(128, dtype=np.float32)
    w["ones_col"] = np.ones((1, 128), np.float32)
    w["ones_k"] = np.ones((128, 1), np.float32)
    con = np.zeros((128, 16), np.float32)
    con[:, 0] = 0.125
    con[:, 1] = 0.03125
    con[:, 2] = 1e-5
    w["con"] = con
    tri = (np.arange(128)[:, None] <= np.arange(128)[None, :]).astype(np.float32)
    w["tri"] = tri
    return w


CON_SCALE_A = 0.125
CON_SCALE_F = 0.03125
CON_EPS = 1e-5


class Ctx:
    pass


def build_nc(stages):
    import contextlib
    stack = contextlib.ExitStack()
    nc = bass.Bass("TRN2", target_bir_lowering=False)
    S = Sched()
    C = Ctx()
    C.nc = nc
    C.S = S
    C.stages = stages
    need = set(stages)

    dr = {}
    dr["xin"] = nc.dram_tensor("xin", [T, D], F32, kind="ExternalInput").ap()
    dr["xout"] = nc.dram_tensor("xout", [T, D], F32, kind="ExternalOutput").ap()
    innames = ["xin"]

    def din(name, shape, dt=F32):
        dr[name] = nc.dram_tensor(name, list(shape), dt, kind="ExternalInput").ap()
        innames.append(name)

    din("con", (128, 16))
    din("ident", (128, 128))
    din("ones_col", (1, 128))
    din("ones_k", (128, 1))
    if {"fff1", "fff2"} & need:
        din("f_nodeW", (D, 45)); din("f_nodeB", (1, 45))
        for i in "qkv":
            din("f_w1_" + i, (D, D)); din("f_b1_" + i, (1, D))
            din("f_w2_" + i, (D, D)); din("f_b2_" + i, (16, D))
    if "att0" in need:
        for n in ("a0_wqT", "a0_wkT", "a0_wvT", "a0_woT"):
            din(n, (D, D))
        din("a0_wob", (1, D))
        din("a0_cos", (512, T)); din("a0_sin", (512, T))
    if "att1" in need:
        din("a1_Wqa", (D, 64)); din("a1_Wqb", (64, D))
        din("a1_Wkva", (D, 32)); din("a1_Wkvb", (32, 512))
        din("a1_Wout", (D, D)); din("a1_bout", (1, D))
        din("a1_cos", (T, 32)); din("a1_sin", (T, 32))
        din("tri", (128, 128))
    if {"moe1", "moe2"} & need:
        din("m_routerT", (D, 8))
        din("m_w1", (8, D, 4 * D)); din("m_b1", (8, 128, 32))
        din("m_w2", (8, 4 * D, D)); din("m_b2", (1, 8 * D))
    nscratch = len(stages) - 1
    xs = [dr["xin"]]
    for k in range(nscratch):
        dr["xs%d" % k] = nc.dram_tensor("xs%d" % k, [T, D], F32, kind="Internal").ap()
        xs.append(dr["xs%d" % k])
    xs.append(dr["xout"])
    C.xs = xs
    if {"fff1", "fff2", "att0", "moe1", "moe2", "att1"} & need:
        dr["ysc"] = nc.dram_tensor("ysc", [128, 8, T], BF16, kind="Internal").ap()
    if {"fff1", "fff2", "att0"} & need:
        dr["qsc"] = nc.dram_tensor("qsc", [128, 8, T], BF16, kind="Internal").ap()
        dr["ksc"] = nc.dram_tensor("ksc", [128, 8, T], BF16, kind="Internal").ap()
    if DEBUG_TAPS:
        dr["dbg_y"] = nc.dram_tensor("dbg_y", [128, 8, T], BF16, kind="ExternalOutput").ap()
        dr["dbg_q"] = nc.dram_tensor("dbg_q", [128, 8, T], BF16, kind="ExternalOutput").ap()
        dr["dbg_k"] = nc.dram_tensor("dbg_k", [128, 8, T], BF16, kind="ExternalOutput").ap()
        dr["dbg_v"] = nc.dram_tensor("dbg_v", [128, 32, 1040], BF16, kind="ExternalOutput").ap()
        dr["dbg_s"] = nc.dram_tensor("dbg_s", [128, NT * 45], F32, kind="ExternalOutput").ap()
        dr["dbg_e"] = nc.dram_tensor("dbg_e", [128, 3 * NT * 16], BF16, kind="ExternalOutput").ap()
    C.dr = dr

    sb = {}

    def sbuf(name, shape, dt):
        sb[name] = stack.enter_context(nc.sbuf_tensor("s_" + name, list(shape), dt))

    sbuf("BIG1", (128, 32768), BF16)
    sbuf("BIG2", (128, 33280), BF16)
    sbuf("MEDA", (128, 16384), BF16)
    sbuf("MEDB", (128, 8192), BF16)
    sbuf("FA", (128, 2144), F32)
    sbuf("BA", (128, 4226), BF16)
    sbuf("POOLB", (128, 4208), BF16)
    sbuf("POOLF", (128, 640), F32)
    sbuf("ROW", (1, 1232), BF16)
    sbuf("con", (128, 16), F32)

    fa, ba = sb["FA"], sb["BA"]
    sb["xa0"] = fa[:, 0:1024]
    sb["xmix0"] = fa[:, 1024:2048]
    sb["stats"] = fa[:, 2048:2080]
    sb["small"] = fa[:, 2080:2144]
    sb["ident"] = ba[:, 0:128]
    sb["y0"] = ba[:, 128:1152]
    sb["ytt0"] = ba[:, 1152:2176].rearrange("p (c q) -> p c q", c=8)
    sb["eT0"] = ba[:, 2176:3200]
    sb["eT1"] = ba[:, 3200:4224]
    sb["junk"] = sb["eT0"]
    sb["ones_k"] = ba[:, 4224:4225]
    sb["ones_col"] = sb["ROW"][:, 0:128]
    C.sb = sb

    ps = {}

    def psum(name, shape, dt):
        ps[name] = stack.enter_context(nc.psum_tensor(name, list(shape), dt))

    psum("big0", (128, 1024), F32)
    psum("big1", (128, 1024), F32)
    psum("pd", (128, 1536), F32)
    psum("ptr", (128, 1024), BF16)
    ps["med0"] = ps["big1"][:, 0:512]
    ps["med1"] = ps["big1"][:, 512:1024]
    C.ps = ps

    nc.const_aps.register(F32, CON_SCALE_A, sb["con"][:, 0:1])
    nc.const_aps.register(F32, CON_SCALE_F, sb["con"][:, 1:2])
    nc.const_aps.register(F32, CON_EPS, sb["con"][:, 2:3])

    S.op("pool", lambda g: g.dma_start(sb["con"][:], dr["con"]),
         writes=[("con",)], kind="d")
    S.op("pool", lambda g: g.dma_start(sb["ident"], dr["ident"]),
         writes=[("ident",)], kind="d")
    S.op("pool", lambda g: g.dma_start(sb["ones_col"][:], dr["ones_col"]),
         writes=[("ones_col",)], kind="d")
    S.op("pool", lambda g: g.dma_start(sb["ones_k"], dr["ones_k"]),
         writes=[("ones_k",)], kind="d")

    for si, st in enumerate(stages):
        x_src, x_dst = C.xs[si], C.xs[si + 1]
        S.soft_barrier()
        if st in ("fff1", "fff2"):
            emit_fff(C, x_src, x_dst, si)
        elif st == "att0":
            emit_att0(C, x_src, x_dst, si)
        elif st == "att1":
            emit_att1(C, x_src, x_dst, si)
        else:
            emit_moe(C, x_src, x_dst, si)

    S.finalize(nc, stack)
    stack.close()
    return nc, innames


def ln_stageA(C, x_src, t, si, extra_reads=()):
    S, sb = C.S, C.sb
    p = t % 2
    xa = sb["xa0"] if p == 0 else sb["xmix0"]
    st = sb["stats"]
    base = p * 8
    s1 = st[:, base + 0:base + 1]
    s2 = st[:, base + 1:base + 2]
    rxa = ("xa", si, p) if p == 0 else ("xmix",)
    rst = ("st", si, p)
    rs2 = ("st2", si, p)
    S.op("sp", lambda e, xa=xa, t=t: e.dma_start(xa[:], x_src[128 * t:128 * (t + 1), :]),
         reads=[("x", si)] + list(extra_reads), writes=[rxa], kind="d")
    S.op("act", lambda e, xa=xa, s2=s2: e.activation(sb["junk"][:], xa[:], AF.Square, accum_out=s2),
         reads=[rxa], writes=[("eT", 0), rs2])
    S.op("dve", lambda e, xa=xa, s1=s1: e.reduce_sum(s1, xa[:], axis=AX.X),
         reads=[rxa], writes=[rst])


def ln_stageB(C, t, si):
    S, sb = C.S, C.sb
    p = t % 2
    xa = sb["xa0"] if p == 0 else sb["xmix0"]
    y = C.ybuf[p]
    st = sb["stats"]
    base = p * 8
    s1 = st[:, base + 0:base + 1]
    s2 = st[:, base + 1:base + 2]
    mean = st[:, base + 2:base + 3]
    msq = st[:, base + 3:base + 4]
    var = st[:, base + 4:base + 5]
    sd = st[:, base + 5:base + 6]
    rstd = st[:, base + 6:base + 7]
    aa = st[:, base + 7:base + 8]
    rxa = ("xa", si, p) if p == 0 else ("xmix",)
    ry = ("yb", si, p)
    rst = ("st", si, p)
    rs2 = ("st2", si, p)
    S.op("dve", lambda e, s1=s1, mean=mean: e.tensor_scalar_mul(mean, s1, 1.0 / D),
         reads=[rst], writes=[rst], wmode="acc")
    S.op("dve", lambda e, mean=mean, msq=msq: e.tensor_mul(msq, mean, mean),
         reads=[rst], writes=[rst], wmode="acc")
    S.op("dve", lambda e, s2=s2, msq=msq, var=var: e.scalar_tensor_tensor(
        var, s2, 1.0 / D, msq, ALU.mult, ALU.subtract),
        reads=[rst, rs2], writes=[rst], wmode="acc")
    S.op("act", lambda e, sd=sd, var=var: e.activation(sd, var, AF.Sqrt, bias=CON_EPS),
         reads=[rst, ("con",)], writes=[rst], wmode="acc")
    S.op("dve", lambda e, rstd=rstd, sd=sd: e.reciprocal(rstd, sd),
         reads=[rst], writes=[rst], wmode="acc")
    S.op("dve", lambda e, aa=aa, mean=mean, rstd=rstd: e.scalar_tensor_tensor(
        aa, mean, -1.0, rstd, ALU.mult, ALU.mult),
        reads=[rst], writes=[rst], wmode="acc")
    S.op("pool", lambda e, xa=xa, y=y, rstd=rstd, aa=aa: e.tensor_scalar(
        y[:], xa[:], rstd, aa, ALU.mult, ALU.add),
        reads=[rxa, rst], writes=[ry])
    return p, rxa, ry


def ln_loop(C, x_src, si, per_tile_fn):
    ln_stageA(C, x_src, 0, si)
    for t in range(NT):
        if t + 1 < NT:
            ln_stageA(C, x_src, t + 1, si)
        p, rxa, ry = ln_stageB(C, t, si)
        per_tile_fn(t, p, ry)


def transpose_y(C, ry, p, t, si, dest_fn, dest_regs):
    S, sb, ps = C.S, C.sb, C.ps
    y = C.ybuf[p]
    for c in range(8):
        S.op("pe",
             lambda e, c=c, y=y: e.transpose(ps["ptr"][:, 128 * c:128 * (c + 1)],
                                             y[:, 128 * c:128 * (c + 1)], sb["ident"]),
             reads=[ry, ("ident",)], writes=[("ptr",)],
             wmode="w" if c == 0 else "acc")
    S.op("act", dest_fn, reads=[("ptr",)], writes=list(dest_regs))


def emit_fff(C, x_src, x_dst, si):
    S, sb, ps, dr, nc = C.S, C.sb, C.ps, C.dr, C.nc
    ysc, qsc, ksc = dr["ysc"], dr["qsc"], dr["ksc"]
    BIG1, BIG2, MEDA, MEDB = sb["BIG1"], sb["BIG2"], sb["MEDA"], sb["MEDB"]
    PB, PF, ROW = sb["POOLB"], sb["POOLF"], sb["ROW"]
    nodeW = PB[:, 0:360].rearrange("p (c j) -> p c j", c=8)
    hms = [PB[:, 368:1392], PB[:, 1392:2416]]
    eqb = PB[:, 2416:3952].rearrange("p (i t l) -> p i t l", i=3, t=NT)
    bits = PB[:, 3952:4144].bitcast(U8).rearrange("p (i b t) -> p i b t", i=3, b=4)
    nodeB = ROW[:, 128:173]
    b1row = ROW[:, 176:1200]
    treetmp = PF[:, 0:256].rearrange("p (k t) -> p k t", k=8)
    leafI = PF[:, 256:352].rearrange("p (i t) -> p i t", i=3)
    C.ybuf = [sb["y0"], sb["eT1"]]
    scoresAll = MEDA.bitcast(F32)[:, 6656:6656 + NT * 45]
    w1v = BIG1[:, 0:16384].rearrange("p (b c F) -> p b c F", b=2, c=8)
    w2v = BIG1[:, 16384:].rearrange("p (b f E) -> p b f E", b=2, f=8)
    Vv = BIG2[:, :32768].rearrange("p (t e) -> p t e", t=32)
    eqT = MEDA[0:16, 0:4096]
    b2s = MEDA[0:16, 4096:5120]
    hmT = MEDA[:, 5120:9216].rearrange("p (c q) -> p c q", c=8)
    qTg = MEDA[:, 9216:13312].rearrange("p (c q) -> p c q", c=8)
    ytg = MEDB.rearrange("p (b c q) -> p b c q", b=2, c=8)

    S.op("pool", lambda g: g.dma_start(
        nodeW, dr["f_nodeW"].rearrange("(c p) j -> p c j", p=128)),
        writes=[("f_nodeW",)], kind="d")
    S.op("pool", lambda g: g.dma_start(nodeB, dr["f_nodeB"]),
        writes=[("rowbuf2",)], kind="d")

    def fff_passA(t, p, ry):
        ytt = sb["ytt0"]
        rytt = ("ytt", si)
        transpose_y(C, ry, p, t, si,
                    lambda e, ytt=ytt: e.copy(ytt[:, :, :],
                                              ps["ptr"].rearrange("p (c q) -> p c q", c=8)),
                    [rytt])
        S.op("sp", lambda e, ytt=ytt, t=t: e.dma_start(ysc[:, :, 128 * t:128 * (t + 1)], ytt[:, :, :]),
             reads=[rytt], writes=[("ysc", si, t // 4)], kind="d",
             wmode="w" if t % 4 == 0 else "acc")
        mp = ps["med%d" % p]
        rmp = ("med%d" % p,)
        for c in range(8):
            S.op("pe", lambda e, c=c, ytt=ytt, mp=mp: e.matmul(
                mp[:, 0:45], ytt[:, c, :], nodeW[:, c, :],
                start=(c == 0), stop=False),
                reads=[rytt, ("f_nodeW",)], writes=[rmp],
                wmode="w" if c == 0 else "acc")
        S.op("pe", lambda e, mp=mp: e.matmul(
            mp[:, 0:45], sb["ones_col"], nodeB,
            start=False, stop=True),
            reads=[("ones_col",), ("rowbuf2",)], writes=[rmp], wmode="acc")
        S.op("dve", lambda e, mp=mp, t=t, scoresAll=scoresAll: e.tensor_copy(
            scoresAll[:, 45 * t:45 * (t + 1)], mp[:, 0:45]),
            reads=[rmp], writes=[("scoresAll",)],
            wmode="w" if t < 2 else "acc")

    ln_loop(C, x_src, si, fff_passA)

    sA = scoresAll
    tt = treetmp

    def sj(i, j):
        return sA[:, i * 15 + j: NT * 45: 45]

    for i in range(3):
        rb = ("bits",)
        rt = ("treetmp",)
        rl = ("leafI",)
        rs = ("scoresAll",)

        def V(fn, reads, writes, wmode="w"):
            S.op("dve", fn, reads=reads, writes=writes, wmode=wmode)
        b0, b1, b2, b3 = (bits[:, i, k, :] for k in range(4))
        V(lambda e, i=i, b0=b0: e.tensor_single_scalar(b0, sj(i, 0), 0.0, ALU.is_ge), [rs], [rb], "acc")
        V(lambda e, i=i, b0=b0: e.select(tt[:, 0, :], b0, sj(i, 2), sj(i, 1), add_drain=True), [rs, rb], [rt], "acc")
        V(lambda e, b1=b1: e.tensor_single_scalar(b1, tt[:, 0, :], 0.0, ALU.is_ge), [rt, rb], [rb], "acc")
        V(lambda e, i=i, b1=b1: e.select(tt[:, 1, :], b1, sj(i, 4), sj(i, 3), add_drain=True), [rs, rb], [rt], "acc")
        V(lambda e, i=i, b1=b1: e.select(tt[:, 2, :], b1, sj(i, 6), sj(i, 5), add_drain=True), [rs, rb], [rt], "acc")
        V(lambda e, b0=b0: e.select(tt[:, 3, :], b0, tt[:, 2, :], tt[:, 1, :], add_drain=True), [rt, rb], [rt], "acc")
        V(lambda e, b2=b2: e.tensor_single_scalar(b2, tt[:, 3, :], 0.0, ALU.is_ge), [rt], [rb], "acc")
        V(lambda e, i=i, b2=b2: e.select(tt[:, 0, :], b2, sj(i, 8), sj(i, 7), add_drain=True), [rs, rb], [rt], "acc")
        V(lambda e, i=i, b2=b2: e.select(tt[:, 1, :], b2, sj(i, 10), sj(i, 9), add_drain=True), [rs, rb], [rt], "acc")
        V(lambda e, i=i, b2=b2: e.select(tt[:, 2, :], b2, sj(i, 12), sj(i, 11), add_drain=True), [rs, rb], [rt], "acc")
        V(lambda e, i=i, b2=b2: e.select(tt[:, 3, :], b2, sj(i, 14), sj(i, 13), add_drain=True), [rs, rb], [rt], "acc")
        V(lambda e, b1=b1: e.select(tt[:, 4, :], b1, tt[:, 1, :], tt[:, 0, :], add_drain=True), [rt, rb], [rt], "acc")
        V(lambda e, b1=b1: e.select(tt[:, 5, :], b1, tt[:, 3, :], tt[:, 2, :], add_drain=True), [rt, rb], [rt], "acc")
        V(lambda e, b0=b0: e.select(tt[:, 6, :], b0, tt[:, 5, :], tt[:, 4, :], add_drain=True), [rt, rb], [rt], "acc")
        V(lambda e, b3=b3: e.tensor_single_scalar(b3, tt[:, 6, :], 0.0, ALU.is_ge), [rt], [rb], "acc")
        li = leafI[:, i, :]
        V(lambda e, li=li, b0=b0, b1=b1: e.scalar_tensor_tensor(li, b0, 2.0, b1, ALU.mult, ALU.add), [rb], [rl], "acc")
        V(lambda e, li=li, b2=b2: e.scalar_tensor_tensor(li, li, 2.0, b2, ALU.mult, ALU.add), [rb, rl], [rl], "acc")
        V(lambda e, li=li, b3=b3: e.scalar_tensor_tensor(li, li, 2.0, b3, ALU.mult, ALU.add), [rb, rl], [rl], "acc")
        for l in range(16):
            V(lambda e, i=i, l=l, li=li: e.tensor_single_scalar(
                eqb[:, i, :, l], li, float(l), ALU.is_equal), [rl], [("eqb",)], "acc")

    for ii, inst in enumerate("qkv"):
        bi = ii % 2
        rw1 = ("fw1", si, ii)
        rw2 = ("fw2", si, ii)
        S.op("pool", lambda g, inst=inst, bi=bi: g.dma_start(
            w1v[:, bi], dr["f_w1_" + inst].rearrange("(c p) F -> p c F", p=128)),
            writes=[rw1, ("w1v", bi)], kind="d")
        S.op("pool", lambda g, inst=inst, bi=bi: g.dma_start(
            w2v[:, bi], dr["f_w2_" + inst].rearrange("(f p) E -> p f E", p=128)),
            writes=[rw2, ("w2v", bi)], kind="d")
        S.op("pool", lambda g, inst=inst: g.dma_start(b1row, dr["f_b1_" + inst]),
             writes=[("rowbuf",)], kind="d")
        S.op("pool", lambda g, inst=inst: g.dma_start(b2s[:, :], dr["f_b2_" + inst]),
             writes=[("b2s",)], kind="d")
        for t in range(NT):
            S.op("pe", lambda e, t=t, ii=ii: e.transpose(
                ps["ptr"][0:16, 0:128], eqb[:, ii, t, :], sb["ident"]),
                reads=[("eqb",), ("ident",)], writes=[("ptr",)])
            S.op("act", lambda e, t=t: e.copy(eqT[:, 128 * t:128 * (t + 1)],
                                              ps["ptr"][0:16, 0:128]),
                 reads=[("ptr",)], writes=[("eqT",)],
                 wmode="w" if t == 0 else "acc")
        for g in range(NG):
            gp = g % 2
            rytg = ("ytg", gp)
            S.op("sp", lambda e, g=g, gp=gp: e.dma_start(
                ytg[:, gp], ysc[:, :, 512 * g:512 * (g + 1)]),
                reads=[("ysc", si, g)], writes=[rytg], kind="d")
            def c_mm1(tloc):
                t = 4 * g + tloc
                hp = tloc % 2
                hm = hms[hp]
                rhm = ("hm", hp)
                bp = ps["big0"] if hp == 0 else ps["pd"][:, 0:1024]
                rbig = ("big0",) if hp == 0 else ("pd",)
                for h2 in range(2):
                    cl = slice(512 * h2, 512 * (h2 + 1))
                    for c in range(8):
                        S.op("pe", lambda e, c=c, gp=gp, tloc=tloc, bi=bi, cl=cl, bp=bp: e.matmul(
                            bp[:, cl], ytg[:, gp, c, 128 * tloc:128 * (tloc + 1)],
                            w1v[:, bi, c, cl], start=(c == 0), stop=False),
                            reads=[rytg, rw1], writes=[rbig],
                            wmode="w" if (c == 0 and h2 == 0) else "acc")
                    S.op("pe", lambda e, cl=cl, bp=bp: e.matmul(
                        bp[:, cl], sb["ones_col"], b1row[0:1, cl],
                        start=False, stop=True),
                        reads=[("ones_col",), ("rowbuf",)], writes=[rbig], wmode="acc")
                S.op("dve", lambda e, hm=hm, ii=ii, t=t, bp=bp: e.scalar_tensor_tensor(
                    hm.rearrange("p (l h) -> p l h", l=16), bp.rearrange("p (l h) -> p l h", l=16),
                    0.0, eqb[:, ii, t, :, None].broadcast_to([128, 16, 64]),
                    ALU.max, ALU.mult),
                    reads=[rbig, ("eqb",)], writes=[rhm])

            def c_tr(tloc):
                hp = tloc % 2
                hm = hms[hp]
                rhm = ("hm", hp)
                for c in range(8):
                    S.op("pe", lambda e, c=c, hm=hm: e.transpose(
                        ps["ptr"][:, 128 * c:128 * (c + 1)],
                        hm[:, 128 * c:128 * (c + 1)], sb["ident"]),
                        reads=[rhm, ("ident",)], writes=[("ptr",)],
                        wmode="w" if c == 0 else "acc")
                S.op("act", lambda e, tloc=tloc: e.copy(
                    hmT[:, :, 128 * tloc:128 * (tloc + 1)],
                    ps["ptr"].rearrange("p (c q) -> p c q", c=8)),
                    reads=[("ptr",)], writes=[("hmT", tloc)])

            for tloc in range(4):
                c_mm1(tloc)
                if tloc > 0:
                    c_tr(tloc - 1)
            c_tr(3)
            if inst in ("q", "k"):
                for E in range(8):
                    mp = ps["med%d" % (E % 2)]
                    rmp = ("med%d" % (E % 2),)
                    for f in range(8):
                        S.op("pe", lambda e, E=E, f=f, bi=bi, mp=mp: e.matmul(
                            mp[:], w2v[:, bi, f, 128 * E:128 * (E + 1)], hmT[:, f, :],
                            start=(f == 0), stop=False),
                            reads=[rw2, ("hmT", 0), ("hmT", 1), ("hmT", 2), ("hmT", 3)],
                            writes=[rmp], wmode="w" if f == 0 else "acc")
                    S.op("pe", lambda e, E=E, g=g, mp=mp: e.matmul(
                        mp[:], b2s[:, 128 * E:128 * (E + 1)], eqT[:, 512 * g:512 * (g + 1)],
                        start=False, stop=True),
                        reads=[("b2s",), ("eqT",)], writes=[rmp], wmode="acc")
                    S.op("dve", lambda e, E=E, mp=mp: e.tensor_copy(qTg[:, E, :], mp[:]),
                         reads=[rmp], writes=[("qTg", E)])
                dst = qsc if inst == "q" else ksc
                S.op("sp", lambda e, g=g, dst=dst: e.dma_start(
                    dst[:, :, 512 * g:512 * (g + 1)], qTg[:, :, :]),
                    reads=[("qTg", E) for E in range(8)],
                    writes=[("qsc" if inst == "q" else "ksc", si)], kind="d",
                    wmode="w" if g == 0 else "acc")
            else:
                for tloc in range(4):
                    t = 4 * g + tloc
                    hp = tloc % 2
                    bp = ps["big0"] if hp == 0 else ps["pd"][:, 0:1024]
                    rbig = ("big0",) if hp == 0 else ("pd",)
                    for h2 in range(2):
                        cl = slice(512 * h2, 512 * (h2 + 1))
                        for f in range(8):
                            S.op("pe", lambda e, f=f, tloc=tloc, bi=bi, cl=cl, bp=bp: e.matmul(
                                bp[:, cl], hmT[:, f, 128 * tloc:128 * (tloc + 1)],
                                w2v[:, bi, f, cl], start=(f == 0), stop=False),
                                reads=[("hmT", tloc), rw2], writes=[rbig],
                                wmode="w" if (f == 0 and h2 == 0) else "acc")
                        S.op("pe", lambda e, t=t, cl=cl, bp=bp: e.matmul(
                            bp[:, cl], eqT[:, 128 * t:128 * (t + 1)], b2s[:, cl],
                            start=False, stop=True),
                            reads=[("eqT",), ("b2s",)], writes=[rbig], wmode="acc")
                    S.op("act", lambda e, t=t, bp=bp: e.copy(Vv[:, t, :], bp[:]),
                         reads=[rbig], writes=[("Vv", si)],
                         wmode="w" if t == 0 else "acc")

    if DEBUG_TAPS:
        S.op("sp", lambda e: e.dma_start(dr["dbg_y"], ysc[:, :, :]),
             reads=[("ysc", si)], kind="d")
        S.op("sp", lambda e: e.dma_start(dr["dbg_q"], qsc[:, :, :]),
             reads=[("qsc", si)], kind="d")
        S.op("sp", lambda e: e.dma_start(dr["dbg_k"], ksc[:, :, :]),
             reads=[("ksc", si)], kind="d")
        S.op("sp", lambda e: e.dma_start(dr["dbg_v"][:, :, 0:1024], Vv[:, :, :]),
             reads=[("Vv", si)], kind="d")
        S.op("sp", lambda e: e.dma_start(dr["dbg_s"], scoresAll[:, :]),
             reads=[("scoresAll",)], kind="d")
        S.op("sp", lambda e: e.dma_start(
            dr["dbg_e"], eqb.rearrange("p i t l -> p (i t l)")),
            reads=[("eqb",)], kind="d")

    kres = BIG1.rearrange("p (c t) -> p c t", c=8)
    S.barrier_regions([("kres",)], [("fw1", si, 0), ("fw1", si, 1), ("fw1", si, 2),
                                    ("fw2", si, 0), ("fw2", si, 1), ("fw2", si, 2),
                                    ("w1v", 0), ("w1v", 1), ("w2v", 0), ("w2v", 1)])
    S.op("sp", lambda e: e.dma_start(kres[:, :, :], ksc[:, :, :]),
         reads=[("ksc", si)], writes=[("kres",)], kind="d")
    qttv = [sb["ytt0"], hms[0].rearrange("p (c q) -> p c q", c=8)]
    for qt in range(NT):
        qp = qt % 2
        qtt = qttv[qp]
        rqtt = ("qtt", si, qp)
        S.op("sp", lambda e, qt=qt, qtt=qtt: e.dma_start(
            qtt[:, :, :], qsc[:, :, 128 * qt:128 * (qt + 1)]),
            reads=[("qsc", si)], writes=[rqtt], kind="d")
        xa = sb["xa0"]
        rxa = ("xa", si, 0)
        S.op("sp", lambda e, qt=qt, xa=xa: e.dma_start(
            xa[:], x_src[128 * qt:128 * (qt + 1), :]),
            reads=[("x", si)], writes=[rxa], kind="d")
        rpd = ("pd",)

        def fff_av(kt):
            kp = kt % 2
            eT = sb["eT%d" % kp]
            reT = ("eT", kp)
            for h2 in range(2):
                cl = slice(512 * h2, 512 * (h2 + 1))
                S.op("pe", lambda e, kt=kt, eT=eT, cl=cl: e.matmul(
                    ps["pd"][:, cl], eT[:, 0:128], Vv[:, kt, cl],
                    start=(kt == 0), stop=(kt == NT - 1)),
                    reads=[reT, ("Vv", si)], writes=[rpd],
                    wmode="w" if (kt == 0 and h2 == 0) else "acc")
            S.op("pe", lambda e, kt=kt, eT=eT: e.matmul(
                ps["pd"][:, 1024:1025], eT[:, 0:128], sb["ones_k"],
                start=(kt == 0), stop=(kt == NT - 1)),
                reads=[reT, ("ones_k",)], writes=[rpd], wmode="acc")

        for kt in range(NT):
            kp = kt % 2
            mp = ps["med%d" % kp]
            rmp = ("med%d" % kp,)
            for c in range(8):
                S.op("pe", lambda e, c=c, kt=kt, qtt=qtt, mp=mp: e.matmul(
                    mp[:, 0:128], kres[:, c, 128 * kt:128 * (kt + 1)], qtt[:, c, :],
                    start=(c == 0), stop=(c == 7)),
                    reads=[("kres",), rqtt], writes=[rmp],
                    wmode="w" if c == 0 else "acc")
            eT = sb["eT%d" % kp]
            reT = ("eT", kp)
            S.op("act", lambda e, eT=eT, mp=mp: e.activation(
                eT[:, 0:128], mp[:, 0:128], AF.Exp, scale=CON_SCALE_F),
                reads=[rmp, ("con",)], writes=[reT])
            if kt > 0:
                fff_av(kt - 1)
        fff_av(NT - 1)
        sm = sb["small"]
        S.op("dve", lambda e, qp=qp: e.tensor_scalar_mul(
            sm[:, qp:qp + 1], ps["pd"][:, 1024:1025], 2.0),
            reads=[rpd], writes=[("small", qp)])
        S.op("dve", lambda e, qp=qp: e.reciprocal(sm[:, 2 + qp:3 + qp], sm[:, qp:qp + 1]),
             reads=[("small", qp)], writes=[("small", qp)], wmode="acc")
        xm = sb["xmix0"]
        rxm = ("xmix",)
        S.op("dve", lambda e, qp=qp, xa=xa, xm=xm: e.scalar_tensor_tensor(
            xm[:], ps["pd"][:, 0:1024], sm[:, 2 + qp:3 + qp], xa[:], ALU.mult, ALU.add),
            reads=[rpd, ("small", qp), rxa], writes=[rxm])
        S.op("sp", lambda e, qt=qt, xm=xm: e.dma_start(
            x_dst[128 * qt:128 * (qt + 1), :], xm[:]),
            reads=[rxm], writes=[("x", si + 1)],
            wmode="w" if qt == 0 else "acc", kind="d")


def emit_att0(C, x_src, x_dst, si):
    S, sb, ps, dr, nc = C.S, C.sb, C.ps, C.dr, C.nc
    ysc, qsc, ksc = dr["ysc"], dr["qsc"], dr["ksc"]
    BIG1, BIG2, MEDA, MEDB = sb["BIG1"], sb["BIG2"], sb["MEDA"], sb["MEDB"]
    PB, ROW = sb["POOLB"], sb["ROW"]
    wqv = BIG1[:, 0:8192].rearrange("p (c E) -> p c E", c=8)
    wkv = BIG1[:, 8192:16384].rearrange("p (c E) -> p c E", c=8)
    wvv = BIG1[:, 16384:24576].rearrange("p (c E) -> p c E", c=8)
    wov = MEDA[:, 0:8192].rearrange("p (c E) -> p c E", c=8)
    wob = ROW[:, 176:1200]
    Vp = BIG2[:, 0:33280].rearrange("p (t h j) -> p t h j", t=32, h=16)
    cosb = MEDB[:, 0:2048].rearrange("p (c q) -> p c q", c=4)
    sinb = MEDB[:, 2048:4096].rearrange("p (c q) -> p c q", c=4)
    ytg = PB[:, 0:4096].rearrange("p (c q) -> p c q", c=8)
    eT0, eT1 = sb["eT0"], sb["eT1"]
    eT0f = eT0.bitcast(F32)
    C.ybuf = [sb["y0"], MEDB[:, 4096:5120]]

    for nm, dst_ in (("a0_wqT", wqv), ("a0_wkT", wkv), ("a0_wvT", wvv)):
        S.op("pool", lambda g, nm=nm, dst_=dst_: g.dma_start(
            dst_, dr[nm].rearrange("(c p) E -> p c E", p=128)),
            writes=[("a0w", nm)], kind="d")
    S.op("pool", lambda g: g.dma_start(
        wov, dr["a0_woT"].rearrange("(c p) E -> p c E", p=128)),
        writes=[("a0wo",)], kind="d")
    S.op("pool", lambda g: g.dma_start(wob, dr["a0_wob"]),
        writes=[("rowbuf",)], kind="d")
    S.op("dve", lambda e: e.memset(Vp[:, :, :, 64:65], 1.0), writes=[("Vv", si)])

    def a0_ln(t, p, ry):
        ytt = sb["ytt0"]
        rytt = ("ytt", si)
        transpose_y(C, ry, p, t, si,
                    lambda e, ytt=ytt: e.copy(ytt[:, :, :],
                                              ps["ptr"].rearrange("p (c q) -> p c q", c=8)),
                    [rytt])
        S.op("sp", lambda e, ytt=ytt, t=t: e.dma_start(ysc[:, :, 128 * t:128 * (t + 1)], ytt[:, :, :]),
             reads=[rytt], writes=[("ysc", si, t // 4)], kind="d",
             wmode="w" if t % 4 == 0 else "acc")

    ln_loop(C, x_src, si, a0_ln)

    for g in range(NG):
        S.op("pool", lambda e, g=g: e.dma_start(
            cosb, dr["a0_cos"].rearrange("(c p) t -> p c t", p=128)[:, :, 512 * g:512 * (g + 1)]),
            writes=[("cosb",)], kind="d")
        S.op("pool", lambda e, g=g: e.dma_start(
            sinb, dr["a0_sin"].rearrange("(c p) t -> p c t", p=128)[:, :, 512 * g:512 * (g + 1)]),
            writes=[("sinb",)], kind="d")
        S.op("sp", lambda e, g=g: e.dma_start(ytg, ysc[:, :, 512 * g:512 * (g + 1)]),
             reads=[("ysc", si, g)], writes=[("ytg", 0)], kind="d")
        for nm, wv_, dst in (("q", wqv, qsc), ("k", wkv, ksc)):
            for c in range(4):
                cp = c % 2
                if cp == 0:
                    mA, mB = ps["med0"], ps["med1"]
                    rA, rB = ("big1", 0), ("big1", 1)
                else:
                    mA, mB = ps["pd"][:, 512:1024], ps["pd"][:, 1024:1536]
                    rA, rB = ("pd", 1), ("pd", 2)
                for mi, rc, mp, rm in ((0, c, mA, rA), (1, c + 4, mB, rB)):
                    for cc in range(8):
                        S.op("pe", lambda e, cc=cc, rc=rc, wv_=wv_, mp=mp: e.matmul(
                            mp[:], wv_[:, cc, 128 * rc:128 * (rc + 1)], ytg[:, cc, :],
                            start=(cc == 0), stop=(cc == 7)),
                            reads=[("ytg", 0), ("a0w", "a0_" + ("wqT" if nm == "q" else "wkT"))],
                            writes=[rm],
                            wmode="w" if cc == 0 else "acc")
                t1 = sb["ytt0"].rearrange("p c q -> p (c q)").bitcast(F32)[:, 0:512]
                t2 = eT0f
                oE = eT1[:, 0:512]; oO = eT1[:, 512:1024]
                S.op("dve", lambda e, c=c, t1=t1, mA=mA: e.tensor_mul(t1, mA, cosb[:, c, :]),
                     reads=[rA, ("cosb",)], writes=[("ytt", si)])
                S.op("dve", lambda e, c=c, t2=t2, mB=mB: e.tensor_mul(t2, mB, sinb[:, c, :]),
                     reads=[rB, ("sinb",)], writes=[("eT", 0)])
                S.op("pool", lambda e, t1=t1, t2=t2, oE=oE: e.tensor_sub(oE, t1, t2),
                     reads=[("ytt", si), ("eT", 0)], writes=[("eT", 1)])
                S.op("dve", lambda e, c=c, t1=t1, mA=mA: e.tensor_mul(t1, mA, sinb[:, c, :]),
                     reads=[rA, ("sinb",)], writes=[("ytt", si)])
                S.op("dve", lambda e, c=c, t2=t2, mB=mB: e.tensor_mul(t2, mB, cosb[:, c, :]),
                     reads=[rB, ("cosb",)], writes=[("eT", 0)])
                S.op("pool", lambda e, t1=t1, t2=t2, oO=oO: e.tensor_add(oO, t1, t2),
                     reads=[("ytt", si), ("eT", 0)], writes=[("eT", 1)], wmode="acc")
                S.op("sp", lambda e, c=c, g=g, dst=dst: e.dma_start(
                    dst[:, c:c + 5:4, 512 * g:512 * (g + 1)],
                    eT1.rearrange("p (a q) -> p a q", a=2)),
                    reads=[("eT", 1)], writes=[("qsc" if dst is qsc else "ksc", si)],
                    kind="d", wmode="w" if (g == 0 and c == 0) else "acc")
        for tloc in range(4):
            t = 4 * g + tloc
            for h2 in range(2):
                cl = slice(512 * h2, 512 * (h2 + 1))
                for cc in range(8):
                    S.op("pe", lambda e, cc=cc, tloc=tloc, cl=cl: e.matmul(
                        ps["big0"][:, cl], ytg[:, cc, 128 * tloc:128 * (tloc + 1)],
                        wvv[:, cc, cl], start=(cc == 0), stop=(cc == 7)),
                        reads=[("ytg", 0), ("a0w", "a0_wvT")], writes=[("big0", 0), ("big0", 1)],
                        wmode="w" if (cc == 0 and h2 == 0) else "acc")
            S.op("act", lambda e, t=t: e.copy(
                Vp[:, t, :, 0:64], ps["big0"].rearrange("p (h j) -> p h j", h=16)),
                reads=[("big0", 0), ("big0", 1)], writes=[("Vv", si)], wmode="acc")

    if DEBUG_TAPS:
        S.op("sp", lambda e: e.dma_start(dr["dbg_y"], ysc[:, :, :]),
             reads=[("ysc", si)], kind="d")
        S.op("sp", lambda e: e.dma_start(dr["dbg_q"], qsc[:, :, :]),
             reads=[("qsc", si)], kind="d")
        S.op("sp", lambda e: e.dma_start(dr["dbg_k"], ksc[:, :, :]),
             reads=[("ksc", si)], kind="d")
        S.op("sp", lambda e: e.dma_start(dr["dbg_v"], BIG2[:, 0:33280].rearrange("p (t j) -> p t j", t=32)),
             reads=[("Vv", si)], kind="d")

    kres = BIG1[:, 0:32768].rearrange("p (c t) -> p c t", c=8)
    S.barrier_regions([("kres",)], [("a0w", "a0_wqT"), ("a0w", "a0_wkT"), ("a0w", "a0_wvT")])
    S.op("sp", lambda e: e.dma_start(kres[:, :, :], ksc[:, :, :]),
         reads=[("ksc", si)], writes=[("kres",)], kind="d")
    qttv = [PB[:, 0:2048].rearrange("p (c q) -> p c q", c=8),
            PB[:, 2048:4096].rearrange("p (c q) -> p c q", c=8)]
    aoT = sb["ytt0"]
    ptrf = ps["ptr"].bitcast(F32)
    eAslot = [eT0[:, 0:512], eT0[:, 512:1024], MEDB[:, 5120:5632]]
    eBslot = [eT1[:, 0:512], eT1[:, 512:1024], MEDB[:, 5632:6144]]
    avac = [[ps["pd"][:, 0:65], ps["pd"][:, 1024:1089]],
            [ps["pd"][:, 512:577], ptrf[:, 0:65]]]
    avkey = [[("pd",), ("pd", 2)], [("pd", 1), ("ptr",)]]
    sm = sb["small"]
    y0 = sb["y0"]
    xa = sb["xa0"]
    rxa = ("xa", si, 0)
    KB2 = 16
    for QP in range(16):
        qp = QP % 2
        qtt = qttv[qp]
        rqtt = ("qtt", si, qp)
        S.op("sp", lambda e, QP=QP, qtt=qtt: e.dma_start(
            qtt[:, :, :], qsc[:, :, 256 * QP:256 * (QP + 1)]),
            reads=[("qsc", si)],
            writes=[rqtt] + ([("ytg", 0)] if QP < 2 else []), kind="d")

        def a0_scores(snum, j, ktB2):
            par = snum % 2
            sl = snum % 3
            sA = ps["big0"][:, 512 * par:512 * (par + 1)]
            sB = ps["big1"][:, 512 * par:512 * (par + 1)]
            rsA = ("big0", par)
            rsB = ("big1", par)
            for ktl in range(2):
                kt = 2 * ktB2 + ktl
                S.op("pe", lambda e, j=j, kt=kt, ktl=ktl, sA=sA, qtt=qtt: e.matmul(
                    sA[:, 256 * ktl:256 * (ktl + 1)], kres[0:64, j, 128 * kt:128 * (kt + 1)],
                    qtt[0:64, j, :], start=True, stop=True, tile_position=(0, 0)),
                    reads=[("kres",), rqtt], writes=[rsA],
                    wmode="w" if ktl == 0 else "acc")
                S.op("pe", lambda e, j=j, kt=kt, ktl=ktl, sB=sB, qtt=qtt: e.matmul(
                    sB[:, 256 * ktl:256 * (ktl + 1)], kres[64:128, j, 128 * kt:128 * (kt + 1)],
                    qtt[64:128, j, :], start=True, stop=True, tile_position=(64, 0)),
                    reads=[("kres",), rqtt], writes=[rsB],
                    wmode="w" if ktl == 0 else "acc")
            eA = eAslot[sl]
            eB = eBslot[sl]
            S.op("act", lambda e, sA=sA, eA=eA: e.activation(eA, sA, AF.Exp, scale=CON_SCALE_A),
                 reads=[rsA, ("con",)], writes=[("eA", sl)])
            S.op("act", lambda e, sB=sB, eB=eB: e.activation(eB, sB, AF.Exp, scale=CON_SCALE_A),
                 reads=[rsB, ("con",)], writes=[("eB", sl)])

        def a0_av(snum, j, ktB2):
            sl = snum % 3
            for hs, ee, ree in ((0, eAslot[sl], ("eA", sl)), (1, eBslot[sl], ("eB", sl))):
                hh = 2 * j + hs
                for qsub in range(2):
                    for ktl in range(2):
                        kt = 2 * ktB2 + ktl
                        first = (ktB2 == 0 and ktl == 0)
                        S.op("pe", lambda e, hs=hs, hh=hh, kt=kt, ktl=ktl, qsub=qsub, ee=ee, ktB2=ktB2: e.matmul(
                            avac[hs][qsub],
                            ee[:, 256 * ktl + 128 * qsub:256 * ktl + 128 * qsub + 128],
                            Vp[:, kt, hh, :],
                            start=(ktB2 == 0 and ktl == 0), stop=(ktB2 == KB2 - 1 and ktl == 1)),
                            reads=[ree, ("Vv", si)],
                            writes=[avkey[hs][qsub]], wmode="w" if first else "acc")

        def a0_norm(j):
            for qsub in range(2):
                S.op("dve", lambda e, qsub=qsub: e.tensor_copy(
                    sm[:, 2 * qsub:2 * qsub + 1], avac[0][qsub][:, 64:65]),
                    reads=[avkey[0][qsub]], writes=[("small", 0)],
                    wmode="w" if qsub == 0 else "acc")
                S.op("dve", lambda e, qsub=qsub: e.tensor_copy(
                    sm[:, 2 * qsub + 1:2 * qsub + 2], avac[1][qsub][:, 64:65]),
                    reads=[avkey[1][qsub]], writes=[("small", 0)], wmode="acc")
            S.op("dve", lambda e: e.reciprocal(sm[:, 4:8], sm[:, 0:4]),
                 reads=[("small", 0)], writes=[("small", 0)], wmode="acc")
            for qsub in range(2):
                yb = y0 if qsub == 0 else C.ybuf[1]
                ryb = ("yb", si, qsub)
                for hs in range(2):
                    h = 2 * j + hs
                    S.op("dve", lambda e, h=h, hs=hs, qsub=qsub, yb=yb: e.tensor_scalar_mul(
                        yb[:, 64 * h:64 * (h + 1)], avac[hs][qsub][:, 0:64],
                        sm[:, 4 + 2 * qsub + hs:5 + 2 * qsub + hs]),
                        reads=[avkey[hs][qsub], ("small", 0)], writes=[ryb],
                        wmode="w" if (j == 0 and hs == 0) else "acc")

        steps = [(j, ktB2) for j in range(8) for ktB2 in range(KB2)]
        LOOKA = 2
        for snum, (j, ktB2) in enumerate(steps):
            a0_scores(snum, j, ktB2)
            if snum >= LOOKA:
                pj, pktB2 = steps[snum - LOOKA]
                a0_av(snum - LOOKA, pj, pktB2)
                if pktB2 == KB2 - 1:
                    a0_norm(pj)
        for snum in range(len(steps) - LOOKA, len(steps)):
            pj, pktB2 = steps[snum]
            a0_av(snum, pj, pktB2)
            if pktB2 == KB2 - 1:
                a0_norm(pj)

        for qsub in range(2):
            qt = 2 * QP + qsub
            yb = y0 if qsub == 0 else C.ybuf[1]
            ryb = ("yb", si, qsub)
            S.op("sp", lambda e, qt=qt, xa=xa: e.dma_start(
                xa[:], x_src[128 * qt:128 * (qt + 1), :]),
                reads=[("x", si)], writes=[rxa], kind="d")
            for c in range(8):
                S.op("pe", lambda e, c=c, yb=yb: e.transpose(
                    ps["ptr"][:, 128 * c:128 * (c + 1)], yb[:, 128 * c:128 * (c + 1)], sb["ident"]),
                    reads=[ryb, ("ident",)], writes=[("ptr",)],
                    wmode="w" if c == 0 else "acc")
            S.op("act", lambda e: e.copy(aoT[:, :, :], ps["ptr"].rearrange("p (c q) -> p c q", c=8)),
                 reads=[("ptr",)], writes=[("aoT",)])
            for h2 in range(2):
                cl = slice(512 * h2, 512 * (h2 + 1))
                for c in range(8):
                    S.op("pe", lambda e, c=c, cl=cl: e.matmul(
                        ps["big0"][:, cl], aoT[:, c, :], wov[:, c, cl],
                        start=(c == 0), stop=False),
                        reads=[("aoT",), ("a0wo",)], writes=[("big0", 0), ("big0", 1)],
                        wmode="w" if (c == 0 and h2 == 0) else "acc")
                S.op("pe", lambda e, cl=cl: e.matmul(
                    ps["big0"][:, cl], sb["ones_col"], wob[0:1, cl],
                    start=False, stop=True),
                    reads=[("ones_col",), ("rowbuf",)], writes=[("big0", 0), ("big0", 1)], wmode="acc")
            xm = sb["xmix0"]
            S.op("dve", lambda e, xa=xa, xm=xm: e.tensor_add(xm[:], ps["big0"][:], xa[:]),
                 reads=[("big0", 0), ("big0", 1), rxa], writes=[("xmix",)])
            S.op("sp", lambda e, qt=qt, xm=xm: e.dma_start(
                x_dst[128 * qt:128 * (qt + 1), :], xm[:]),
                reads=[("xmix",)], writes=[("x", si + 1)],
                wmode="w" if qt == 0 else "acc", kind="d")


def emit_att1(C, x_src, x_dst, si):
    S, sb, ps, dr, nc = C.S, C.sb, C.ps, C.dr, C.nc
    ysc = dr["ysc"]
    BIG1, BIG2, MEDA, MEDB = sb["BIG1"], sb["BIG2"], sb["MEDA"], sb["MEDB"]
    PB, PF, ROW = sb["POOLB"], sb["POOLF"], sb["ROW"]
    qrT = BIG1.rearrange("p (c t) -> p c t", c=8)
    Woutv = BIG2[:, 0:8192].rearrange("p (c E) -> p c E", c=8)
    Vp = BIG2[:, 8192:16512].rearrange("p (t h j) -> p t h j", t=32, h=4)
    krT = MEDA.rearrange("p (k t) -> p k t", k=4)
    Wqav = MEDB[:, 0:512].rearrange("p (c j) -> p c j", c=8)
    Wkvav = MEDB[:, 512:768].rearrange("p (c j) -> p c j", c=8)
    tri = MEDB[:, 768:896]
    qr = MEDB[:, 896:1920]
    qa_g = MEDB[0:64, 1920:2432]
    Wqb = MEDB[0:64, 2432:3456]
    ytg = MEDB[:, 3456:7552].rearrange("p (c q) -> p c q", c=8)
    kva_g = PB[0:32, 0:512]
    Wkvb = PB[0:32, 512:1024]
    dgb = PB[:, 1024:1152]
    aoT = PB[:, 1152:2176].rearrange("p (c q) -> p c q", c=8)
    kr = sb["y0"][:, 0:256]
    cosT = PF[:, 0:32]
    sinT = PF[:, 32:64]
    bout = ROW[:, 176:1200]
    eT0, eT1 = sb["eT0"], sb["eT1"]
    C.ybuf = [sb["y0"], BIG2[:, 16512:17536]]

    S.op("pool", lambda g: g.dma_start(Wqav, dr["a1_Wqa"].rearrange("(c p) j -> p c j", p=128)),
         writes=[("a1wqa",)], kind="d")
    S.op("pool", lambda g: g.dma_start(Wkvav, dr["a1_Wkva"].rearrange("(c p) j -> p c j", p=128)),
         writes=[("a1wkva",)], kind="d")
    S.op("pool", lambda g: g.dma_start(Wqb, dr["a1_Wqb"]), writes=[("a1wqb",)], kind="d")
    S.op("pool", lambda g: g.dma_start(Wkvb, dr["a1_Wkvb"]), writes=[("a1wkvb",)], kind="d")
    S.op("pool", lambda g: g.dma_start(tri, dr["tri"]), writes=[("tri",)], kind="d")
    S.op("pool", lambda g: g.dma_start(
        Woutv, dr["a1_Wout"].rearrange("(c p) E -> p c E", p=128)),
        writes=[("a1wout",)], kind="d")
    S.op("pool", lambda g: g.dma_start(bout, dr["a1_bout"]), writes=[("rowbuf",)], kind="d")
    S.op("dve", lambda e: e.memset(Vp[:, :, :, 64:65], 1.0), writes=[("Vv", si)])

    def a1_ln(t, p, ry):
        ytt = sb["ytt0"]
        rytt = ("ytt", si)
        transpose_y(C, ry, p, t, si,
                    lambda e, ytt=ytt: e.copy(ytt[:, :, :],
                                              ps["ptr"].rearrange("p (c q) -> p c q", c=8)),
                    [rytt])
        S.op("sp", lambda e, ytt=ytt, t=t: e.dma_start(ysc[:, :, 128 * t:128 * (t + 1)], ytt[:, :, :]),
             reads=[rytt], writes=[("ysc", si, t // 4)], kind="d",
             wmode="w" if t % 4 == 0 else "acc")

    ln_loop(C, x_src, si, a1_ln)

    for g in range(NG):
        S.op("sp", lambda e, g=g: e.dma_start(ytg, ysc[:, :, 512 * g:512 * (g + 1)]),
             reads=[("ysc", si, g)], writes=[("ytg", 0)], kind="d")
        for c in range(8):
            S.op("pe", lambda e, c=c: e.matmul(
                ps["med0"][0:64, :], Wqav[:, c, :], ytg[:, c, :],
                start=(c == 0), stop=(c == 7)),
                reads=[("a1wqa",), ("ytg", 0)], writes=[("big1", 0)],
                wmode="w" if c == 0 else "acc")
        S.op("dve", lambda e: e.tensor_copy(qa_g, ps["med0"][0:64, :]),
             reads=[("big1", 0)], writes=[("qa_g",)])
        for c in range(8):
            S.op("pe", lambda e, c=c: e.matmul(
                ps["med1"][0:32, :], Wkvav[:, c, :], ytg[:, c, :],
                start=(c == 0), stop=(c == 7)),
                reads=[("a1wkva",), ("ytg", 0)], writes=[("big1", 1)],
                wmode="w" if c == 0 else "acc")
        S.op("dve", lambda e: e.tensor_copy(kva_g, ps["med1"][0:32, :]),
             reads=[("big1", 1)], writes=[("kva_g",)])
        for tt in range(4):
            t = 4 * g + tt
            S.op("pool", lambda e, t=t: e.dma_start(
                cosT, dr["a1_cos"].rearrange("(t p) j -> t p j", p=128)[t]),
                writes=[("cosT",)], kind="d")
            S.op("pool", lambda e, t=t: e.dma_start(
                sinT, dr["a1_sin"].rearrange("(t p) j -> t p j", p=128)[t]),
                writes=[("sinT",)], kind="d")
            for h2 in range(2):
                cl = slice(512 * h2, 512 * (h2 + 1))
                S.op("pe", lambda e, tt=tt, cl=cl: e.matmul(
                    ps["big0"][:, cl], qa_g[:, 128 * tt:128 * (tt + 1)], Wqb[:, cl],
                    start=True, stop=True),
                    reads=[("qa_g",), ("a1wqb",)], writes=[("big0", h2)],
                    wmode="w")
            x1 = ps["big0"].rearrange("p (h d) -> p h d", h=16)[:, :, 0:64:2]
            x2 = ps["big0"].rearrange("p (h d) -> p h d", h=16)[:, :, 1:64:2]
            cB = cosT[:, None, :].broadcast_to([128, 16, 32])
            sB = sinT[:, None, :].broadcast_to([128, 16, 32])
            t1 = eT0[:, 0:512].rearrange("p (h d) -> p h d", h=16)
            t2 = eT0[:, 512:1024].rearrange("p (h d) -> p h d", h=16)
            qre = qr.rearrange("p (h d) -> p h d", h=16)[:, :, 0:64:2]
            qro = qr.rearrange("p (h d) -> p h d", h=16)[:, :, 1:64:2]
            S.op("dve", lambda e, x1=x1, cB=cB, t1=t1: e.tensor_mul(t1, x1, cB),
                 reads=[("big0", 0), ("big0", 1), ("cosT",)], writes=[("eT", 0)])
            S.op("dve", lambda e, x2=x2, sB=sB, t2=t2: e.tensor_mul(t2, x2, sB),
                 reads=[("big0", 0), ("big0", 1), ("sinT",)], writes=[("eT", 0)], wmode="acc")
            S.op("pool", lambda e, t1=t1, t2=t2, qre=qre: e.tensor_sub(qre, t1, t2),
                 reads=[("eT", 0)], writes=[("qr",)])
            S.op("dve", lambda e, x1=x1, sB=sB, t1=t1: e.tensor_mul(t1, x1, sB),
                 reads=[("big0", 0), ("big0", 1), ("sinT",)], writes=[("eT", 0)])
            S.op("dve", lambda e, x2=x2, cB=cB, t2=t2: e.tensor_mul(t2, x2, cB),
                 reads=[("big0", 0), ("big0", 1), ("cosT",)], writes=[("eT", 0)], wmode="acc")
            S.op("pool", lambda e, t1=t1, t2=t2, qro=qro: e.tensor_add(qro, t1, t2),
                 reads=[("eT", 0)], writes=[("qr",)], wmode="acc")
            for c in range(8):
                S.op("pe", lambda e, c=c: e.transpose(
                    ps["ptr"][:, 128 * c:128 * (c + 1)], qr[:, 128 * c:128 * (c + 1)], sb["ident"]),
                    reads=[("qr",), ("ident",)], writes=[("ptr",)],
                    wmode="w" if c == 0 else "acc")
            S.op("act", lambda e, t=t: e.copy(
                qrT[:, :, 128 * t:128 * (t + 1)], ps["ptr"].rearrange("p (c q) -> p c q", c=8)),
                reads=[("ptr",)], writes=[("qrT",)],
                wmode="w" if t == 0 else "acc")
            S.op("pe", lambda e, tt=tt: e.matmul(
                ps["med0"][:], kva_g[:, 128 * tt:128 * (tt + 1)], Wkvb[:],
                start=True, stop=True),
                reads=[("kva_g",), ("a1wkvb",)], writes=[("big1", 0)], wmode="w")
            kx1 = ps["med0"].rearrange("p (h d) -> p h d", h=8)[:, 0:4, 0:64:2]
            kx2 = ps["med0"].rearrange("p (h d) -> p h d", h=8)[:, 0:4, 1:64:2]
            kcB = cosT[:, None, :].broadcast_to([128, 4, 32])
            ksB = sinT[:, None, :].broadcast_to([128, 4, 32])
            kt1 = eT0[:, 0:128].rearrange("p (h d) -> p h d", h=4)
            kt2 = eT0[:, 128:256].rearrange("p (h d) -> p h d", h=4)
            kre = kr.rearrange("p (h d) -> p h d", h=4)[:, :, 0:64:2]
            kro = kr.rearrange("p (h d) -> p h d", h=4)[:, :, 1:64:2]
            S.op("dve", lambda e, kx1=kx1, kcB=kcB, kt1=kt1: e.tensor_mul(kt1, kx1, kcB),
                 reads=[("big1", 0), ("cosT",)], writes=[("eT", 0)])
            S.op("dve", lambda e, kx2=kx2, ksB=ksB, kt2=kt2: e.tensor_mul(kt2, kx2, ksB),
                 reads=[("big1", 0), ("sinT",)], writes=[("eT", 0)], wmode="acc")
            S.op("pool", lambda e, kt1=kt1, kt2=kt2, kre=kre: e.tensor_sub(kre, kt1, kt2),
                 reads=[("eT", 0)], writes=[("y", si)])
            S.op("dve", lambda e, kx1=kx1, ksB=ksB, kt1=kt1: e.tensor_mul(kt1, kx1, ksB),
                 reads=[("big1", 0), ("sinT",)], writes=[("eT", 0)])
            S.op("dve", lambda e, kx2=kx2, kcB=kcB, kt2=kt2: e.tensor_mul(kt2, kx2, kcB),
                 reads=[("big1", 0), ("cosT",)], writes=[("eT", 0)], wmode="acc")
            S.op("pool", lambda e, kt1=kt1, kt2=kt2, kro=kro: e.tensor_add(kro, kt1, kt2),
                 reads=[("eT", 0)], writes=[("y", si)], wmode="acc")
            S.op("act", lambda e, t=t: e.copy(
                Vp[:, t, :, 0:64], ps["med0"][:, 256:512].rearrange("p (h j) -> p h j", h=4)),
                reads=[("big1", 0)], writes=[("Vv", si)], wmode="acc")
            for k4 in range(4):
                S.op("pe", lambda e, k4=k4: e.transpose(
                    ps["ptr"][0:64, 128 * k4:128 * k4 + 128], kr[:, 64 * k4:64 * (k4 + 1)], sb["ident"]),
                    reads=[("y", si), ("ident",)], writes=[("ptr",)],
                    wmode="w" if k4 == 0 else "acc")
                S.op("pe", lambda e, k4=k4: e.transpose(
                    ps["ptr"][64:128, 128 * k4:128 * k4 + 128], kr[:, 64 * k4:64 * (k4 + 1)], sb["ident"]),
                    reads=[("y", si), ("ident",)], writes=[("ptr",)], wmode="acc")
            S.op("act", lambda e, t=t: e.copy(
                krT[:, :, 128 * t:128 * (t + 1)], ps["ptr"][:, 0:512].rearrange("p (k q) -> p k q", k=4)),
                reads=[("ptr",)], writes=[("krT",)],
                wmode="w" if t == 0 else "acc")

    ptrf = ps["ptr"].bitcast(F32)
    eAslot = [eT0[:, 0:512], eT0[:, 512:1024], PB[:, 2176:2688]]
    eBslot = [eT1[:, 0:512], eT1[:, 512:1024], PB[:, 2688:3200]]
    avac = [[ps["pd"][:, 0:65], ps["pd"][:, 1024:1089]],
            [ps["pd"][:, 512:577], ptrf[:, 0:65]]]
    avkey = [[("pd",), ("pd", 2)], [("pd", 1), ("ptr",)]]
    aoT = sb["ytt0"]
    sm = sb["small"]
    y0 = sb["y0"]
    xa = sb["xa0"]
    rxa = ("xa", si, 0)
    for QP in range(16):
        nkt = 2 * QP + 2
        KB2 = (nkt + 1) // 2

        def a1_scores(snum, j, ktB2):
            par = snum % 2
            sl = snum % 3
            kts = [k for k in (2 * ktB2, 2 * ktB2 + 1) if k < nkt]
            sA = ps["big0"][:, 512 * par:512 * (par + 1)]
            sB = ps["big1"][:, 512 * par:512 * (par + 1)]
            rsA = ("big0", par)
            rsB = ("big1", par)
            for ktl, kt in enumerate(kts):
                S.op("pe", lambda e, j=j, kt=kt, ktl=ktl, sA=sA, QP=QP: e.matmul(
                    sA[:, 256 * ktl:256 * (ktl + 1)], krT[0:64, j // 2, 128 * kt:128 * (kt + 1)],
                    qrT[0:64, j, 256 * QP:256 * (QP + 1)], start=True, stop=True,
                    tile_position=(0, 0)),
                    reads=[("krT",), ("qrT",)], writes=[rsA],
                    wmode="w" if ktl == 0 else "acc")
                S.op("pe", lambda e, j=j, kt=kt, ktl=ktl, sB=sB, QP=QP: e.matmul(
                    sB[:, 256 * ktl:256 * (ktl + 1)], krT[64:128, j // 2, 128 * kt:128 * (kt + 1)],
                    qrT[64:128, j, 256 * QP:256 * (QP + 1)], start=True, stop=True,
                    tile_position=(64, 0)),
                    reads=[("krT",), ("qrT",)], writes=[rsB],
                    wmode="w" if ktl == 0 else "acc")
            eA = eAslot[sl]
            eB = eBslot[sl]
            nc4 = 256 * len(kts)
            S.op("act", lambda e, sA=sA, eA=eA, nc4=nc4: e.activation(
                eA[:, 0:nc4], sA[:, 0:nc4], AF.Exp, scale=CON_SCALE_A),
                reads=[rsA, ("con",)], writes=[("eA", sl)])
            S.op("act", lambda e, sB=sB, eB=eB, nc4=nc4: e.activation(
                eB[:, 0:nc4], sB[:, 0:nc4], AF.Exp, scale=CON_SCALE_A),
                reads=[rsB, ("con",)], writes=[("eB", sl)])

        def a1_av(snum, j, ktB2):
            sl = snum % 3
            kts = [k for k in (2 * ktB2, 2 * ktB2 + 1) if k < nkt]
            for hs, ee, ree in ((0, eAslot[sl], ("eA", sl)), (1, eBslot[sl], ("eB", sl))):
                hh = 2 * j + hs
                kvh = hh // 4
                for qsub in range(2):
                    qglob = 2 * QP + qsub
                    for ktl, kt in enumerate(kts):
                        if kt > qglob:
                            continue
                        first = (ktB2 == 0 and ktl == 0)
                        stop = (kt == qglob)
                        diag = (kt == qglob)
                        src_ap = ee[:, 256 * ktl + 128 * qsub:256 * ktl + 128 * qsub + 128]
                        if diag:
                            S.op("dve", lambda e, src_ap=src_ap: e.tensor_mul(
                                dgb[:], src_ap, tri[:]),
                                reads=[ree, ("tri",)], writes=[("dgb",)])
                            rl = ("dgb",)
                        else:
                            rl = ree
                        S.op("pe", (lambda e, hs=hs, hh=hh, kt=kt, ktl=ktl, qsub=qsub,
                                    ee=ee, kvh=kvh, diag=diag, src_ap=src_ap:
                                    e.matmul(avac[hs][qsub],
                                             dgb[:] if diag else src_ap,
                                             Vp[:, kt, kvh, :],
                                             start=(kt == 0), stop=diag)),
                            reads=[rl, ("Vv", si)],
                            writes=[avkey[hs][qsub]] if not first else
                            [avkey[hs][qsub], ("pd",), ("pd", 1), ("pd", 2), ("ptr",)],
                            wmode="w" if first else "acc")

        def a1_norm(j):
            for qsub in range(2):
                S.op("dve", lambda e, qsub=qsub: e.tensor_copy(
                    sm[:, 2 * qsub:2 * qsub + 1], avac[0][qsub][:, 64:65]),
                    reads=[avkey[0][qsub]], writes=[("small", 0)],
                    wmode="w" if qsub == 0 else "acc")
                S.op("dve", lambda e, qsub=qsub: e.tensor_copy(
                    sm[:, 2 * qsub + 1:2 * qsub + 2], avac[1][qsub][:, 64:65]),
                    reads=[avkey[1][qsub]], writes=[("small", 0)], wmode="acc")
            S.op("dve", lambda e: e.reciprocal(sm[:, 4:8], sm[:, 0:4]),
                 reads=[("small", 0)], writes=[("small", 0)], wmode="acc")
            for qsub in range(2):
                yb = y0 if qsub == 0 else C.ybuf[1]
                ryb = ("yb", si, qsub)
                for hs in range(2):
                    h = 2 * j + hs
                    S.op("dve", lambda e, h=h, hs=hs, qsub=qsub, yb=yb: e.tensor_scalar_mul(
                        yb[:, 64 * h:64 * (h + 1)], avac[hs][qsub][:, 0:64],
                        sm[:, 4 + 2 * qsub + hs:5 + 2 * qsub + hs]),
                        reads=[avkey[hs][qsub], ("small", 0)], writes=[ryb],
                        wmode="w" if (j == 0 and hs == 0) else "acc")

        steps = [(j, ktB2) for j in range(8) for ktB2 in range(KB2)]
        LOOKA = 2 if len(steps) >= 2 else 1
        for snum, (j, ktB2) in enumerate(steps):
            a1_scores(snum, j, ktB2)
            if snum >= LOOKA:
                pj, pktB2 = steps[snum - LOOKA]
                a1_av(snum - LOOKA, pj, pktB2)
                if pktB2 == KB2 - 1:
                    a1_norm(pj)
        for snum in range(len(steps) - LOOKA, len(steps)):
            pj, pktB2 = steps[snum]
            a1_av(snum, pj, pktB2)
            if pktB2 == KB2 - 1:
                a1_norm(pj)

        for qsub in range(2):
            qt = 2 * QP + qsub
            yb = y0 if qsub == 0 else C.ybuf[1]
            ryb = ("yb", si, qsub)
            S.op("sp", lambda e, qt=qt, xa=xa: e.dma_start(
                xa[:], x_src[128 * qt:128 * (qt + 1), :]),
                reads=[("x", si)], writes=[rxa], kind="d")
            for c in range(8):
                S.op("pe", lambda e, c=c, yb=yb: e.transpose(
                    ps["ptr"][:, 128 * c:128 * (c + 1)], yb[:, 128 * c:128 * (c + 1)], sb["ident"]),
                    reads=[ryb, ("ident",)], writes=[("ptr",)],
                    wmode="w" if c == 0 else "acc")
            S.op("act", lambda e: e.copy(aoT[:, :, :], ps["ptr"].rearrange("p (c q) -> p c q", c=8)),
                 reads=[("ptr",)], writes=[("aoT",)])
            for h2 in range(2):
                cl = slice(512 * h2, 512 * (h2 + 1))
                for c in range(8):
                    S.op("pe", lambda e, c=c, cl=cl: e.matmul(
                        ps["big0"][:, cl], aoT[:, c, :], Woutv[:, c, cl],
                        start=(c == 0), stop=False),
                        reads=[("aoT",), ("a1wout",)], writes=[("big0", 0), ("big0", 1)],
                        wmode="w" if (c == 0 and h2 == 0) else "acc")
                S.op("pe", lambda e, cl=cl: e.matmul(
                    ps["big0"][:, cl], sb["ones_col"], bout[0:1, cl],
                    start=False, stop=True),
                    reads=[("ones_col",), ("rowbuf",)], writes=[("big0", 0), ("big0", 1)], wmode="acc")
            xm = sb["xmix0"]
            S.op("dve", lambda e, xa=xa, xm=xm: e.tensor_add(xm[:], ps["big0"][:], xa[:]),
                 reads=[("big0", 0), ("big0", 1), rxa], writes=[("xmix",)])
            S.op("sp", lambda e, qt=qt, xm=xm: e.dma_start(
                x_dst[128 * qt:128 * (qt + 1), :], xm[:]),
                reads=[("xmix",)], writes=[("x", si + 1)],
                wmode="w" if qt == 0 else "acc", kind="d")


def emit_moe(C, x_src, x_dst, si):
    S, sb, ps, dr, nc = C.S, C.sb, C.ps, C.dr, C.nc
    ysc = dr["ysc"]
    BIG1, BIG2, MEDA, MEDB = sb["BIG1"], sb["BIG2"], sb["MEDA"], sb["MEDB"]
    PB, PF, ROW = sb["POOLB"], sb["POOLF"], sb["ROW"]
    w1sb = BIG1.rearrange("p (c f q) -> p c f q", c=8, f=32)
    w2sb = BIG2[:, 0:32768].rearrange("p (f h) -> p f h", f=32)
    gT = MEDA[:, 0:16384].rearrange("p (f q) -> p f q", f=32)
    ytg = MEDB.rearrange("p (b c q) -> p b c q", b=2, c=8)
    routerT = PB[:, 0:64].rearrange("p (c e) -> p c e", c=8)
    c05b = PB[:, 64:320]
    equ = PB[:, 320:328].bitcast(U8)
    eq1 = equ[:, 0:8]; eq2 = equ[:, 8:16]
    b1sb = PF[:, 0:256].rearrange("p (e f) -> p e f", e=8)
    logitsAll = PF[:, 256:512]
    dAll = PF[:, 512:544]
    w1All = PF[:, 544:576]
    w2All = PF[:, 576:608]
    m1c = PF[:, 608:609]; m2c = PF[:, 609:610]
    l2buf = PF[:, 616:624]
    neg30 = PF[:, 624:632]
    b2row = ROW[:, 176:1200]
    C.ybuf = [sb["y0"], PB[:, 1024:2048]]

    S.op("pool", lambda g: g.dma_start(
        routerT, dr["m_routerT"].rearrange("(c p) e -> p c e", p=128)),
        writes=[("mrt",)], kind="d")
    S.op("sp", lambda e: e.dma_start(
        b1sb, dr["m_b1"].rearrange("e p f -> p e f")),
        writes=[("mb1",)], kind="d")
    S.op("dve", lambda e: e.memset(neg30, -1e30), writes=[("neg30",)])
    S.op("pool", lambda e: e.dma_start(x_dst, x_src),
         reads=[("x", si)], writes=[("x", si + 1), ("xinit", si)], kind="d")

    def moe_ln(t, p, ry):
        ytt = sb["ytt0"]
        rytt = ("ytt", si)
        transpose_y(C, ry, p, t, si,
                    lambda e, ytt=ytt: e.copy(ytt[:, :, :],
                                              ps["ptr"].rearrange("p (c q) -> p c q", c=8)),
                    [rytt])
        S.op("sp", lambda e, ytt=ytt, t=t: e.dma_start(ysc[:, :, 128 * t:128 * (t + 1)], ytt[:, :, :]),
             reads=[rytt], writes=[("ysc", si, t // 4)], kind="d",
             wmode="w" if t % 4 == 0 else "acc")
        mp = ps["med%d" % p]
        rmp = ("big1", p)
        for c in range(8):
            S.op("pe", lambda e, c=c, ytt=ytt, mp=mp: e.matmul(
                mp[:, 0:8], ytt[:, c, :], routerT[:, c, :],
                start=(c == 0), stop=(c == 7)),
                reads=[rytt, ("mrt",)], writes=[rmp],
                wmode="w" if c == 0 else "acc")
        S.op("dve", lambda e, mp=mp, t=t: e.tensor_copy(
            logitsAll[:, 8 * t:8 * (t + 1)], mp[:, 0:8]),
            reads=[rmp], writes=[("logits",)],
            wmode="w" if t < 2 else "acc")

    ln_loop(C, x_src, si, moe_ln)

    for t in range(NT):
        lt = logitsAll[:, 8 * t:8 * (t + 1)]
        S.op("dve", lambda e, lt=lt: e.reduce_max(m1c, lt, axis=AX.X),
             reads=[("logits",)], writes=[("rt",)])
        S.op("dve", lambda e, lt=lt: e.tensor_single_scalar(eq1, lt, m1c, ALU.is_equal),
             reads=[("logits",), ("rt",)], writes=[("equ", 1)])
        S.op("dve", lambda e, lt=lt: e.select(l2buf, eq1, neg30, lt, add_drain=True),
             reads=[("logits",), ("equ", 1), ("neg30",)], writes=[("l2",)])
        S.op("dve", lambda e: e.reduce_max(m2c, l2buf, axis=AX.X),
             reads=[("l2",)], writes=[("rt",)], wmode="acc")
        S.op("dve", lambda e: e.tensor_single_scalar(eq2, l2buf, m2c, ALU.is_equal),
             reads=[("l2",), ("rt",)], writes=[("equ", 2)])
        S.op("dve", lambda e, t=t: e.scalar_tensor_tensor(
            dAll[:, t:t + 1], m1c, 1.0, m2c, ALU.mult, ALU.subtract),
            reads=[("rt",)], writes=[("dAll",)],
            wmode="w" if t == 0 else "acc")
        S.op("dve", lambda e, t=t: e.tensor_copy(c05b[:, 8 * t:8 * (t + 1)], eq2),
             reads=[("equ", 2)], writes=[("c05",)],
             wmode="w" if t == 0 else "acc")
        S.op("dve", lambda e, t=t: e.tensor_copy(logitsAll[:, 8 * t:8 * (t + 1)], eq1),
             reads=[("equ", 1)], writes=[("logits",)], wmode="acc")
    S.op("act", lambda e: e.activation(w1All, dAll, AF.Sigmoid),
         reads=[("dAll",)], writes=[("w1All",)])
    S.op("dve", lambda e: e.tensor_scalar(w2All, w1All, -0.5, 0.5, ALU.mult, ALU.add),
         reads=[("w1All",)], writes=[("w2All",)])
    for t in range(NT):
        S.op("dve", lambda e, t=t: e.tensor_scalar(
            l2buf, logitsAll[:, 8 * t:8 * (t + 1)], w1All[:, t:t + 1], 0.5, ALU.mult, ALU.mult),
            reads=[("logits",), ("w1All",)], writes=[("l2",)])
        S.op("dve", lambda e, t=t: e.scalar_tensor_tensor(
            logitsAll[:, 8 * t:8 * (t + 1)], c05b[:, 8 * t:8 * (t + 1)],
            w2All[:, t:t + 1], l2buf, ALU.mult, ALU.add),
            reads=[("c05",), ("w2All",), ("l2",)], writes=[("logits",)], wmode="acc")

    for e_ in range(8):
        S.op("pool", lambda g, e_=e_: g.dma_start(
            w1sb, dr["m_w1"][e_].rearrange("(c p) (f q) -> p c f q", p=128, q=128)),
            writes=[("w1m", 0), ("w1m", 1), ("w1m", 2), ("w1m", 3)], kind="d")
        S.op("pool", lambda g, e_=e_: g.dma_start(
            w2sb, dr["m_w2"][e_].rearrange("(f p) h -> p f h", p=128)),
            writes=[("w2m",)], kind="d")
        S.op("pool", lambda g, e_=e_: g.dma_start(
            b2row, dr["m_b2"][0:1, D * e_:D * (e_ + 1)]),
            writes=[("rowbuf",)], kind="d")
        for g in range(NG):
            gp = g % 2
            rytg = ("ytg", gp)
            S.op("sp", lambda e, g=g, gp=gp: e.dma_start(
                ytg[:, gp], ysc[:, :, 512 * g:512 * (g + 1)]),
                reads=[("ysc", si, g)], writes=[rytg], kind="d")
            for F in range(32):
                par = F % 2
                mp = ps["med%d" % par]
                rmp = ("big1", par)
                for c in range(8):
                    S.op("pe", lambda e, c=c, F=F, gp=gp, mp=mp: e.matmul(
                        mp[:], w1sb[:, c, F, :], ytg[:, gp, c, :],
                        start=(c == 0), stop=(c == 7)),
                        reads=[rytg, ("w1m", F // 8)], writes=[rmp],
                        wmode="w" if c == 0 else "acc")
                S.op("act", lambda e, F=F, e_=e_, mp=mp: e.activation(
                    gT[:, F, :], mp[:], AF.Gelu, bias=b1sb[:, e_, F:F + 1]),
                    reads=[rmp, ("mb1",)], writes=[("gT", F)])
            for tt in range(4):
                t = 4 * g + tt
                bp = tt % 2
                bigp = ps["big0"] if bp == 0 else ps["pd"][:, 0:1024]
                rbig = ("big0",) if bp == 0 else ("pd",)
                for h2 in range(2):
                    cl = slice(512 * h2, 512 * (h2 + 1))
                    for fi, F in enumerate(range(31, -1, -1)):
                        S.op("pe", lambda e, F=F, tt=tt, cl=cl, bigp=bigp, fi=fi: e.matmul(
                            bigp[:, cl], gT[:, F, 128 * tt:128 * (tt + 1)], w2sb[:, F, cl],
                            start=(fi == 0), stop=False),
                            reads=[("gT", F), ("w2m",)], writes=[rbig],
                            wmode="w" if (fi == 0 and h2 == 0) else "acc")
                    S.op("pe", lambda e, cl=cl, bigp=bigp: e.matmul(
                        bigp[:, cl], sb["ones_col"], b2row[0:1, cl],
                        start=False, stop=True),
                        reads=[("ones_col",), ("rowbuf",)], writes=[rbig], wmode="acc")
                xm = sb["xmix0"]
                S.op("act", lambda e, t=t, e_=e_, xm=xm, bigp=bigp: e.mul(
                    xm[:], bigp[:], logitsAll[:, 8 * t + e_:8 * t + e_ + 1]),
                    reads=[rbig, ("logits",)], writes=[("xmix",)])
                S.op("pool", lambda e, t=t, xm=xm: e.dma_start(
                    x_dst[128 * t:128 * (t + 1), :], xm[:], accum_op=ALU.add),
                    reads=[("xmix",), ("xinit", si)], writes=[("x", si + 1)],
                    kind="d", wmode="acc")


def kernel(**inputs):
    w = prep_weights(inputs)
    x = np.ascontiguousarray(inputs["L0_attn_x"], dtype=np.float32)
    cur = [np.ascontiguousarray(x[b]) for b in range(8)]
    for stages in STAGE_GROUPS:
        nc, innames = build_nc(list(stages))
        in_maps = []
        for b in range(8):
            m = {}
            for n in innames:
                if n == "xin":
                    m[n] = cur[b]
                else:
                    m[n] = w[n]
            in_maps.append(m)
        res = run_bass_kernel_spmd(nc, in_maps, core_ids=list(range(8)))
        cur = [np.asarray(res.results[b]["xout"], dtype=np.float32) for b in range(8)]
    return np.stack(cur, axis=0)


# version 15 (final)
# speedup vs baseline: 1.0924x; 1.0795x over previous
import numpy as np
import concourse.bass as bass
import concourse.mybir as mybir
from concourse.bass_utils import run_bass_kernel_spmd

F32 = mybir.dt.float32
BF16 = mybir.dt.bfloat16
U8 = mybir.dt.uint8
AF = mybir.ActivationFunctionType
ALU = mybir.AluOpType
AX = mybir.AxisListType

T = 4096
D = 1024
NT = 32
NG = 8

ALL_STAGES = ("fff1", "att0", "fff2", "moe1", "att1", "moe2")
STAGE_GROUPS = [ALL_STAGES]
DEBUG_TAPS = False


class _Op:
    __slots__ = ("stream", "emit", "kind", "pos", "sem", "val", "waits", "know")

    def __init__(self, stream, emit, kind, pos):
        self.stream = stream
        self.emit = emit
        self.kind = kind
        self.pos = pos
        self.sem = None
        self.val = None
        self.waits = []
        self.know = None


class Sched:

    CSTREAMS = ("pe", "act", "dve", "pool")
    DSTREAMS = ("sp", "pool")
    NDSEM = 12

    def __init__(self):
        self.ops = {s: [] for s in ("pe", "act", "dve", "sp", "pool")}
        self.ccnt = {s: 0 for s in self.CSTREAMS}
        self.last_assigned_pos = {s: -1 for s in self.CSTREAMS}
        self.didx = {s: 0 for s in self.DSTREAMS}
        self.dsem_last = {}
        self.know = {s: {} for s in ("pe", "act", "dve", "sp", "pool")}
        self.snap = {}
        self.reg = {}
        self.sem_totals = {}
        self._barrier_deps = []
        self._barrier_absorbed = set()

    def _completion(self, op):
        if op.sem is not None:
            if op.kind == "d":
                return op.sem, self.dsem_last[op.sem]
            return op.sem, op.val
        s = op.stream
        assert op.kind == "c"
        if op.pos > self.last_assigned_pos[s]:
            self.ccnt[s] += 1
            op.sem = "c_" + s
            op.val = self.ccnt[s]
            self.last_assigned_pos[s] = op.pos
            know = dict(op.know)
            know["c_" + s] = op.val
            self.snap[(op.sem, op.val)] = know
            self.sem_totals[op.sem] = op.val
            return op.sem, op.val
        for later in self.ops[s][op.pos + 1:]:
            if later.sem is not None and later.kind == "c":
                return later.sem, later.val
        raise AssertionError("inconsistent inc assignment")

    def _add_wait(self, op, dep_op):
        sem, val = self._completion(dep_op)
        know = self.know[op.stream]
        if know.get(sem, 0) >= val:
            return
        op.waits.append((sem, val))
        snap = self.snap.get((sem, val))
        if snap:
            for k, v in snap.items():
                if know.get(k, 0) < v:
                    know[k] = v
        if know.get(sem, 0) < val:
            know[sem] = val

    def op(self, stream, emit, reads=(), writes=(), kind="c", wmode="w"):
        lst = self.ops[stream]
        o = _Op(stream, emit, kind, len(lst))
        if self._barrier_deps and stream not in self._barrier_absorbed:
            self._barrier_absorbed.add(stream)
            for d in self._barrier_deps:
                if d is not None:
                    self._add_wait(o, d)
        deps = []
        for r in reads:
            e = self.reg.get(r)
            if e and e[0] is not None:
                deps.append(e[0])
        if wmode == "w":
            for w in writes:
                e = self.reg.get(w)
                if e:
                    if e[0] is not None:
                        deps.append(e[0])
                    deps.extend(e[1])
        seen = set()
        for d in deps:
            if d is o or id(d) in seen:
                continue
            seen.add(id(d))
            self._add_wait(o, d)
        if kind == "d":
            i = self.didx[stream]
            self.didx[stream] = i + 1
            o.sem = "d_%s_%d" % (stream, i % self.NDSEM)
            o.val = 16 * (i // self.NDSEM + 1)
            if i >= self.NDSEM:
                know = self.know[stream]
                if know.get(o.sem, 0) < o.val - 16:
                    o.waits.append((o.sem, o.val - 16))
                    snap = self.snap.get((o.sem, o.val - 16))
                    if snap:
                        for k, v in snap.items():
                            if know.get(k, 0) < v:
                                know[k] = v
                    know[o.sem] = max(know.get(o.sem, 0), o.val - 16)
            self.dsem_last[o.sem] = o.val
            self.sem_totals[o.sem] = max(self.sem_totals.get(o.sem, 0), o.val)
        o.know = dict(self.know[stream])
        if o.sem is not None:
            know = dict(o.know)
            know[o.sem] = o.val
            self.snap[(o.sem, o.val)] = know
        lst.append(o)
        for r in reads:
            e = self.reg.setdefault(r, [None, [], "w"])
            e[1].append(o)
        for w in writes:
            e = self.reg.setdefault(w, [None, [], "w"])
            e[0] = o
            if wmode == "w":
                e[1] = []
        return o

    def soft_barrier(self):
        deps = []
        for s in ("pe", "act", "dve", "sp", "pool"):
            if self.ops[s]:
                deps.append(self.ops[s][-1])
        self._barrier_deps = deps
        self._barrier_absorbed = set()

    def barrier_regions(self, keys, dep_keys):
        ops = []
        for k in dep_keys:
            e = self.reg.get(k)
            if e:
                if e[0] is not None:
                    ops.append(e[0])
                ops.extend(e[1])
                del self.reg[k]
        if not ops:
            return
        for k in keys:
            e = self.reg.setdefault(k, [None, [], "w"])
            e[1].extend(ops)

    def finalize(self, nc, stack):
        sems = {}
        for s in self.CSTREAMS:
            sems["c_" + s] = stack.enter_context(nc.semaphore("c_" + s))
        for s in self.DSTREAMS:
            for j in range(self.NDSEM):
                sems["d_%s_%d" % (s, j)] = stack.enter_context(
                    nc.semaphore("d_%s_%d" % (s, j)))
        block = stack.enter_context(nc.Block())
        sched = self

        def run_stream(eng, name):
            for o in sched.ops[name]:
                for sem, val in o.waits:
                    eng.wait_ge(sems[sem], val)
                instr = o.emit(eng)
                if o.sem is not None:
                    instr.then_inc(sems[o.sem], 1 if o.kind == "c" else 16)

        @block.tensor
        def _(t):
            run_stream(t, "pe")

        @block.scalar
        def _(s):
            run_stream(s, "act")

        @block.vector
        def _(v):
            run_stream(v, "dve")

        @block.gpsimd
        def _(g):
            run_stream(g, "pool")

        @block.sync
        def _(sy):
            run_stream(sy, "sp")
            for sem, val in sorted(sched.sem_totals.items()):
                sy.wait_ge(sems[sem], val)


_INPUT_NAMES = (
    "L0_attn_x", "L0_attn_wq", "L0_attn_wk", "L0_attn_wv", "L0_attn_wo_w",
    "L0_attn_wo_b", "L0_ffn_q_node_w", "L0_ffn_q_node_b", "L0_ffn_q_w1",
    "L0_ffn_q_b1", "L0_ffn_q_w2", "L0_ffn_q_b2", "L0_ffn_k_node_w",
    "L0_ffn_k_node_b", "L0_ffn_k_w1", "L0_ffn_k_b1", "L0_ffn_k_w2",
    "L0_ffn_k_b2", "L0_ffn_v_node_w", "L0_ffn_v_node_b", "L0_ffn_v_w1",
    "L0_ffn_v_b1", "L0_ffn_v_w2", "L0_ffn_v_b2", "L1_attn_Wqa", "L1_attn_Wqb",
    "L1_attn_Wkva", "L1_attn_Wkvb", "L1_attn_Wout", "L1_attn_bout",
    "L1_ffn_router_w", "L1_ffn_w1", "L1_ffn_b1", "L1_ffn_w2", "L1_ffn_b2",
)


def prep_weights(inp):
    for _n in _INPUT_NAMES:
        assert _n in inp, _n
    w = {}
    nw = np.concatenate([inp["L0_ffn_%s_node_w" % i].T for i in "qkv"], axis=1)
    w["f_nodeW"] = np.ascontiguousarray(nw)
    nb = np.concatenate([inp["L0_ffn_%s_node_b" % i] for i in "qkv"])
    w["f_nodeB"] = np.ascontiguousarray(nb.reshape(1, 45))
    for i in "qkv":
        w["f_w1_" + i] = np.ascontiguousarray(
            inp["L0_ffn_%s_w1" % i].transpose(1, 0, 2).reshape(D, D))
        w["f_b1_" + i] = np.ascontiguousarray(
            inp["L0_ffn_%s_b1" % i].reshape(1, D))
        w["f_w2_" + i] = np.ascontiguousarray(
            inp["L0_ffn_%s_w2" % i].reshape(D, D))
        w["f_b2_" + i] = np.ascontiguousarray(inp["L0_ffn_%s_b2" % i])
    perm = np.concatenate([np.arange(0, D, 2), np.arange(1, D, 2)])
    w["a0_wqT"] = np.ascontiguousarray(inp["L0_attn_wq"][perm, :].T)
    w["a0_wkT"] = np.ascontiguousarray(inp["L0_attn_wk"][perm, :].T)
    w["a0_wvT"] = np.ascontiguousarray(inp["L0_attn_wv"].T)
    w["a0_woT"] = np.ascontiguousarray(inp["L0_attn_wo_w"].T)
    w["a0_wob"] = np.ascontiguousarray(inp["L0_attn_wo_b"].reshape(1, D))
    r = np.arange(512, dtype=np.float64)
    fr = np.exp(r * 2 * (-np.log(10000.0) / D))
    ang = np.arange(T, dtype=np.float64)[None, :] * fr[:, None]
    w["a0_cos"] = np.cos(ang).astype(np.float32)
    w["a0_sin"] = np.sin(ang).astype(np.float32)
    w["a1_Wqa"] = np.ascontiguousarray(inp["L1_attn_Wqa"])
    w["a1_Wqb"] = np.ascontiguousarray(inp["L1_attn_Wqb"])
    w["a1_Wkva"] = np.ascontiguousarray(inp["L1_attn_Wkva"])
    kvb = inp["L1_attn_Wkvb"].reshape(32, 4, 64, 2)
    w["a1_Wkvb"] = np.ascontiguousarray(
        np.concatenate([kvb[..., 0].reshape(32, 256),
                        kvb[..., 1].reshape(32, 256)], axis=1))
    w["a1_Wout"] = np.ascontiguousarray(inp["L1_attn_Wout"])
    w["a1_bout"] = np.ascontiguousarray(inp["L1_attn_bout"].reshape(1, D))
    j = np.arange(32, dtype=np.float64)
    invf = 1.0 / 10000.0 ** (2 * j / 64)
    ang1 = np.arange(T, dtype=np.float64)[:, None] * invf[None, :]
    w["a1_cos"] = np.cos(ang1).astype(np.float32)
    w["a1_sin"] = np.sin(ang1).astype(np.float32)
    w["m_routerT"] = np.ascontiguousarray(inp["L1_ffn_router_w"].T)
    w["m_w1"] = np.ascontiguousarray(inp["L1_ffn_w1"].transpose(0, 2, 1))
    w["m_b1"] = np.ascontiguousarray(
        inp["L1_ffn_b1"].reshape(8, 32, 128).transpose(0, 2, 1))
    w["m_w2"] = np.ascontiguousarray(inp["L1_ffn_w2"].transpose(0, 2, 1))
    w["m_b2"] = np.ascontiguousarray(inp["L1_ffn_b2"].reshape(1, 8 * D))
    w["ident"] = np.eye(128, dtype=np.float32)
    w["ones_col"] = np.ones((1, 128), np.float32)
    w["ones_k"] = np.ones((128, 1), np.float32)
    con = np.zeros((128, 16), np.float32)
    con[:, 0] = 0.125
    con[:, 1] = 0.03125
    con[:, 2] = 1e-5
    w["con"] = con
    tri = (np.arange(128)[:, None] <= np.arange(128)[None, :]).astype(np.float32)
    w["tri"] = tri
    return w


CON_SCALE_A = 0.125
CON_SCALE_F = 0.03125
CON_EPS = 1e-5


class Ctx:
    pass


def build_nc(stages):
    import contextlib
    stack = contextlib.ExitStack()
    nc = bass.Bass("TRN2", target_bir_lowering=False)
    S = Sched()
    C = Ctx()
    C.nc = nc
    C.S = S
    C.stages = stages
    need = set(stages)

    dr = {}
    dr["xin"] = nc.dram_tensor("xin", [T, D], F32, kind="ExternalInput").ap()
    dr["xout"] = nc.dram_tensor("xout", [T, D], F32, kind="ExternalOutput").ap()
    innames = ["xin"]

    def din(name, shape, dt=F32):
        dr[name] = nc.dram_tensor(name, list(shape), dt, kind="ExternalInput").ap()
        innames.append(name)

    din("con", (128, 16))
    din("ident", (128, 128))
    din("ones_col", (1, 128))
    din("ones_k", (128, 1))
    if {"fff1", "fff2"} & need:
        din("f_nodeW", (D, 45)); din("f_nodeB", (1, 45))
        for i in "qkv":
            din("f_w1_" + i, (D, D)); din("f_b1_" + i, (1, D))
            din("f_w2_" + i, (D, D)); din("f_b2_" + i, (16, D))
    if "att0" in need:
        for n in ("a0_wqT", "a0_wkT", "a0_wvT", "a0_woT"):
            din(n, (D, D))
        din("a0_wob", (1, D))
        din("a0_cos", (512, T)); din("a0_sin", (512, T))
    if "att1" in need:
        din("a1_Wqa", (D, 64)); din("a1_Wqb", (64, D))
        din("a1_Wkva", (D, 32)); din("a1_Wkvb", (32, 512))
        din("a1_Wout", (D, D)); din("a1_bout", (1, D))
        din("a1_cos", (T, 32)); din("a1_sin", (T, 32))
        din("tri", (128, 128))
    if {"moe1", "moe2"} & need:
        din("m_routerT", (D, 8))
        din("m_w1", (8, D, 4 * D)); din("m_b1", (8, 128, 32))
        din("m_w2", (8, 4 * D, D)); din("m_b2", (1, 8 * D))
    nscratch = len(stages) - 1
    xs = [dr["xin"]]
    for k in range(nscratch):
        dr["xs%d" % k] = nc.dram_tensor("xs%d" % k, [T, D], F32, kind="Internal").ap()
        xs.append(dr["xs%d" % k])
    xs.append(dr["xout"])
    C.xs = xs
    if {"fff1", "fff2", "att0", "moe1", "moe2", "att1"} & need:
        dr["ysc"] = nc.dram_tensor("ysc", [128, 8, T], BF16, kind="Internal").ap()
    if {"fff1", "fff2", "att0"} & need:
        dr["qsc"] = nc.dram_tensor("qsc", [128, 8, T], BF16, kind="Internal").ap()
        dr["ksc"] = nc.dram_tensor("ksc", [128, 8, T], BF16, kind="Internal").ap()
    if DEBUG_TAPS:
        dr["dbg_y"] = nc.dram_tensor("dbg_y", [128, 8, T], BF16, kind="ExternalOutput").ap()
        dr["dbg_q"] = nc.dram_tensor("dbg_q", [128, 8, T], BF16, kind="ExternalOutput").ap()
        dr["dbg_k"] = nc.dram_tensor("dbg_k", [128, 8, T], BF16, kind="ExternalOutput").ap()
        dr["dbg_v"] = nc.dram_tensor("dbg_v", [128, 32, 1040], BF16, kind="ExternalOutput").ap()
        dr["dbg_s"] = nc.dram_tensor("dbg_s", [128, NT * 45], F32, kind="ExternalOutput").ap()
        dr["dbg_e"] = nc.dram_tensor("dbg_e", [128, 3 * NT * 16], BF16, kind="ExternalOutput").ap()
    C.dr = dr

    sb = {}

    def sbuf(name, shape, dt):
        sb[name] = stack.enter_context(nc.sbuf_tensor("s_" + name, list(shape), dt))

    sbuf("BIG1", (128, 32768), BF16)
    sbuf("BIG2", (128, 33280), BF16)
    sbuf("MEDA", (128, 16384), BF16)
    sbuf("MEDB", (128, 8192), BF16)
    sbuf("FA", (128, 2144), F32)
    sbuf("BA", (128, 4226), BF16)
    sbuf("POOLB", (128, 4208), BF16)
    sbuf("POOLF", (128, 640), F32)
    sbuf("ROW", (1, 1232), BF16)
    sbuf("con", (128, 16), F32)

    fa, ba = sb["FA"], sb["BA"]
    sb["xa0"] = fa[:, 0:1024]
    sb["xmix0"] = fa[:, 1024:2048]
    sb["stats"] = fa[:, 2048:2080]
    sb["small"] = fa[:, 2080:2144]
    sb["ident"] = ba[:, 0:128]
    sb["y0"] = ba[:, 128:1152]
    sb["ytt0"] = ba[:, 1152:2176].rearrange("p (c q) -> p c q", c=8)
    sb["eT0"] = ba[:, 2176:3200]
    sb["eT1"] = ba[:, 3200:4224]
    sb["junk"] = sb["eT0"]
    sb["ones_k"] = ba[:, 4224:4225]
    sb["ones_col"] = sb["ROW"][:, 0:128]
    C.sb = sb

    ps = {}

    def psum(name, shape, dt):
        ps[name] = stack.enter_context(nc.psum_tensor(name, list(shape), dt))

    psum("big0", (128, 1024), F32)
    psum("big1", (128, 1024), F32)
    psum("pd", (128, 1536), F32)
    psum("ptr", (128, 1024), BF16)
    ps["med0"] = ps["big1"][:, 0:512]
    ps["med1"] = ps["big1"][:, 512:1024]
    C.ps = ps

    nc.const_aps.register(F32, CON_SCALE_A, sb["con"][:, 0:1])
    nc.const_aps.register(F32, CON_SCALE_F, sb["con"][:, 1:2])
    nc.const_aps.register(F32, CON_EPS, sb["con"][:, 2:3])

    S.op("pool", lambda g: g.dma_start(sb["con"][:], dr["con"]),
         writes=[("con",)], kind="d")
    S.op("pool", lambda g: g.dma_start(sb["ident"], dr["ident"]),
         writes=[("ident",)], kind="d")
    S.op("pool", lambda g: g.dma_start(sb["ones_col"][:], dr["ones_col"]),
         writes=[("ones_col",)], kind="d")
    S.op("pool", lambda g: g.dma_start(sb["ones_k"], dr["ones_k"]),
         writes=[("ones_k",)], kind="d")

    for si, st in enumerate(stages):
        x_src, x_dst = C.xs[si], C.xs[si + 1]
        S.soft_barrier()
        if st in ("fff1", "fff2"):
            emit_fff(C, x_src, x_dst, si)
        elif st == "att0":
            emit_att0(C, x_src, x_dst, si)
        elif st == "att1":
            emit_att1(C, x_src, x_dst, si)
        else:
            emit_moe(C, x_src, x_dst, si)

    S.finalize(nc, stack)
    stack.close()
    return nc, innames


def ln_stageA(C, x_src, t, si, extra_reads=()):
    S, sb = C.S, C.sb
    p = t % 2
    xa = sb["xa0"] if p == 0 else sb["xmix0"]
    st = sb["stats"]
    base = p * 8
    s1 = st[:, base + 0:base + 1]
    s2 = st[:, base + 1:base + 2]
    rxa = ("xa", si, p) if p == 0 else ("xmix",)
    rst = ("st", si, p)
    rs2 = ("st2", si, p)
    S.op("sp", lambda e, xa=xa, t=t: e.dma_start(xa[:], x_src[128 * t:128 * (t + 1), :]),
         reads=[("x", si)] + list(extra_reads), writes=[rxa], kind="d")
    S.op("act", lambda e, xa=xa, s2=s2: e.activation(sb["junk"][:], xa[:], AF.Square, accum_out=s2),
         reads=[rxa], writes=[("eT", 0), rs2])
    S.op("dve", lambda e, xa=xa, s1=s1: e.reduce_sum(s1, xa[:], axis=AX.X),
         reads=[rxa], writes=[rst])


def ln_stageB(C, t, si):
    S, sb = C.S, C.sb
    p = t % 2
    xa = sb["xa0"] if p == 0 else sb["xmix0"]
    y = C.ybuf[p]
    st = sb["stats"]
    base = p * 8
    s1 = st[:, base + 0:base + 1]
    s2 = st[:, base + 1:base + 2]
    mean = st[:, base + 2:base + 3]
    msq = st[:, base + 3:base + 4]
    var = st[:, base + 4:base + 5]
    sd = st[:, base + 5:base + 6]
    rstd = st[:, base + 6:base + 7]
    aa = st[:, base + 7:base + 8]
    rxa = ("xa", si, p) if p == 0 else ("xmix",)
    ry = ("yb", si, p)
    rst = ("st", si, p)
    rs2 = ("st2", si, p)
    S.op("dve", lambda e, s1=s1, mean=mean: e.tensor_scalar_mul(mean, s1, 1.0 / D),
         reads=[rst], writes=[rst], wmode="acc")
    S.op("dve", lambda e, mean=mean, msq=msq: e.tensor_mul(msq, mean, mean),
         reads=[rst], writes=[rst], wmode="acc")
    S.op("dve", lambda e, s2=s2, msq=msq, var=var: e.scalar_tensor_tensor(
        var, s2, 1.0 / D, msq, ALU.mult, ALU.subtract),
        reads=[rst, rs2], writes=[rst], wmode="acc")
    S.op("act", lambda e, sd=sd, var=var: e.activation(sd, var, AF.Sqrt, bias=CON_EPS),
         reads=[rst, ("con",)], writes=[rst], wmode="acc")
    S.op("dve", lambda e, rstd=rstd, sd=sd: e.reciprocal(rstd, sd),
         reads=[rst], writes=[rst], wmode="acc")
    S.op("dve", lambda e, aa=aa, mean=mean, rstd=rstd: e.scalar_tensor_tensor(
        aa, mean, -1.0, rstd, ALU.mult, ALU.mult),
        reads=[rst], writes=[rst], wmode="acc")
    S.op("pool", lambda e, xa=xa, y=y, rstd=rstd, aa=aa: e.tensor_scalar(
        y[:], xa[:], rstd, aa, ALU.mult, ALU.add),
        reads=[rxa, rst], writes=[ry])
    return p, rxa, ry


def ln_loop(C, x_src, si, per_tile_fn):
    ln_stageA(C, x_src, 0, si)
    for t in range(NT):
        if t + 1 < NT:
            ln_stageA(C, x_src, t + 1, si)
        p, rxa, ry = ln_stageB(C, t, si)
        per_tile_fn(t, p, ry)


def transpose_y(C, ry, p, t, si, dest_fn, dest_regs):
    S, sb, ps = C.S, C.sb, C.ps
    y = C.ybuf[p]
    for c in range(8):
        S.op("pe",
             lambda e, c=c, y=y: e.transpose(ps["ptr"][:, 128 * c:128 * (c + 1)],
                                             y[:, 128 * c:128 * (c + 1)], sb["ident"]),
             reads=[ry, ("ident",)], writes=[("ptr",)],
             wmode="w" if c == 0 else "acc")
    S.op("act", dest_fn, reads=[("ptr",)], writes=list(dest_regs))


def emit_fff(C, x_src, x_dst, si):
    S, sb, ps, dr, nc = C.S, C.sb, C.ps, C.dr, C.nc
    ysc, qsc, ksc = dr["ysc"], dr["qsc"], dr["ksc"]
    BIG1, BIG2, MEDA, MEDB = sb["BIG1"], sb["BIG2"], sb["MEDA"], sb["MEDB"]
    PB, PF, ROW = sb["POOLB"], sb["POOLF"], sb["ROW"]
    nodeW = PB[:, 0:360].rearrange("p (c j) -> p c j", c=8)
    hms = [PB[:, 368:1392], PB[:, 1392:2416]]
    eqb = PB[:, 2416:3952].rearrange("p (i t l) -> p i t l", i=3, t=NT)
    bits = PB[:, 3952:4144].bitcast(U8).rearrange("p (i b t) -> p i b t", i=3, b=4)
    nodeB = ROW[:, 128:173]
    b1row = ROW[:, 176:1200]
    treetmp = PF[:, 0:256].rearrange("p (k t) -> p k t", k=8)
    leafI = PF[:, 256:352].rearrange("p (i t) -> p i t", i=3)
    C.ybuf = [sb["y0"], sb["eT1"]]
    scoresAll = MEDA.bitcast(F32)[:, 6656:6656 + NT * 45]
    w1v = BIG1[:, 0:16384].rearrange("p (b c F) -> p b c F", b=2, c=8)
    w2v = BIG1[:, 16384:].rearrange("p (b f E) -> p b f E", b=2, f=8)
    Vv = BIG2[:, :32768].rearrange("p (t e) -> p t e", t=32)
    eqT = MEDA[0:16, 0:4096]
    b2s = MEDA[0:16, 4096:5120]
    hmT = MEDA[:, 5120:9216].rearrange("p (c q) -> p c q", c=8)
    qTg = MEDA[:, 9216:13312].rearrange("p (c q) -> p c q", c=8)
    ytg = MEDB.rearrange("p (b c q) -> p b c q", b=2, c=8)

    S.op("pool", lambda g: g.dma_start(
        nodeW, dr["f_nodeW"].rearrange("(c p) j -> p c j", p=128)),
        writes=[("f_nodeW",)], kind="d")
    S.op("pool", lambda g: g.dma_start(nodeB, dr["f_nodeB"]),
        writes=[("rowbuf2",)], kind="d")

    def fff_passA(t, p, ry):
        ytt = sb["ytt0"]
        rytt = ("ytt", si)
        transpose_y(C, ry, p, t, si,
                    lambda e, ytt=ytt: e.copy(ytt[:, :, :],
                                              ps["ptr"].rearrange("p (c q) -> p c q", c=8)),
                    [rytt])
        S.op("sp", lambda e, ytt=ytt, t=t: e.dma_start(ysc[:, :, 128 * t:128 * (t + 1)], ytt[:, :, :]),
             reads=[rytt], writes=[("ysc", si, t // 4)], kind="d",
             wmode="w" if t % 4 == 0 else "acc")
        mp = ps["med%d" % p]
        rmp = ("med%d" % p,)
        for c in range(8):
            S.op("pe", lambda e, c=c, ytt=ytt, mp=mp: e.matmul(
                mp[:, 0:45], ytt[:, c, :], nodeW[:, c, :],
                start=(c == 0), stop=False),
                reads=[rytt, ("f_nodeW",)], writes=[rmp],
                wmode="w" if c == 0 else "acc")
        S.op("pe", lambda e, mp=mp: e.matmul(
            mp[:, 0:45], sb["ones_col"], nodeB,
            start=False, stop=True),
            reads=[("ones_col",), ("rowbuf2",)], writes=[rmp], wmode="acc")
        S.op("dve", lambda e, mp=mp, t=t, scoresAll=scoresAll: e.tensor_copy(
            scoresAll[:, 45 * t:45 * (t + 1)], mp[:, 0:45]),
            reads=[rmp], writes=[("scoresAll",)],
            wmode="w" if t < 2 else "acc")

    ln_loop(C, x_src, si, fff_passA)

    sA = scoresAll
    tt = treetmp

    def sj(i, j):
        return sA[:, i * 15 + j: NT * 45: 45]

    for i in range(3):
        rb = ("bits",)
        rt = ("treetmp",)
        rl = ("leafI",)
        rs = ("scoresAll",)

        def V(fn, reads, writes, wmode="w"):
            S.op("dve", fn, reads=reads, writes=writes, wmode=wmode)
        b0, b1, b2, b3 = (bits[:, i, k, :] for k in range(4))
        V(lambda e, i=i, b0=b0: e.tensor_single_scalar(b0, sj(i, 0), 0.0, ALU.is_ge), [rs], [rb], "acc")
        V(lambda e, i=i, b0=b0: e.select(tt[:, 0, :], b0, sj(i, 2), sj(i, 1), add_drain=True), [rs, rb], [rt], "acc")
        V(lambda e, b1=b1: e.tensor_single_scalar(b1, tt[:, 0, :], 0.0, ALU.is_ge), [rt, rb], [rb], "acc")
        V(lambda e, i=i, b1=b1: e.select(tt[:, 1, :], b1, sj(i, 4), sj(i, 3), add_drain=True), [rs, rb], [rt], "acc")
        V(lambda e, i=i, b1=b1: e.select(tt[:, 2, :], b1, sj(i, 6), sj(i, 5), add_drain=True), [rs, rb], [rt], "acc")
        V(lambda e, b0=b0: e.select(tt[:, 3, :], b0, tt[:, 2, :], tt[:, 1, :], add_drain=True), [rt, rb], [rt], "acc")
        V(lambda e, b2=b2: e.tensor_single_scalar(b2, tt[:, 3, :], 0.0, ALU.is_ge), [rt], [rb], "acc")
        V(lambda e, i=i, b2=b2: e.select(tt[:, 0, :], b2, sj(i, 8), sj(i, 7), add_drain=True), [rs, rb], [rt], "acc")
        V(lambda e, i=i, b2=b2: e.select(tt[:, 1, :], b2, sj(i, 10), sj(i, 9), add_drain=True), [rs, rb], [rt], "acc")
        V(lambda e, i=i, b2=b2: e.select(tt[:, 2, :], b2, sj(i, 12), sj(i, 11), add_drain=True), [rs, rb], [rt], "acc")
        V(lambda e, i=i, b2=b2: e.select(tt[:, 3, :], b2, sj(i, 14), sj(i, 13), add_drain=True), [rs, rb], [rt], "acc")
        V(lambda e, b1=b1: e.select(tt[:, 4, :], b1, tt[:, 1, :], tt[:, 0, :], add_drain=True), [rt, rb], [rt], "acc")
        V(lambda e, b1=b1: e.select(tt[:, 5, :], b1, tt[:, 3, :], tt[:, 2, :], add_drain=True), [rt, rb], [rt], "acc")
        V(lambda e, b0=b0: e.select(tt[:, 6, :], b0, tt[:, 5, :], tt[:, 4, :], add_drain=True), [rt, rb], [rt], "acc")
        V(lambda e, b3=b3: e.tensor_single_scalar(b3, tt[:, 6, :], 0.0, ALU.is_ge), [rt], [rb], "acc")
        li = leafI[:, i, :]
        V(lambda e, li=li, b0=b0, b1=b1: e.scalar_tensor_tensor(li, b0, 2.0, b1, ALU.mult, ALU.add), [rb], [rl], "acc")
        V(lambda e, li=li, b2=b2: e.scalar_tensor_tensor(li, li, 2.0, b2, ALU.mult, ALU.add), [rb, rl], [rl], "acc")
        V(lambda e, li=li, b3=b3: e.scalar_tensor_tensor(li, li, 2.0, b3, ALU.mult, ALU.add), [rb, rl], [rl], "acc")
        for l in range(16):
            V(lambda e, i=i, l=l, li=li: e.tensor_single_scalar(
                eqb[:, i, :, l], li, float(l), ALU.is_equal), [rl], [("eqb",)], "acc")

    for ii, inst in enumerate("qkv"):
        bi = ii % 2
        rw1 = ("fw1", si, ii)
        rw2 = ("fw2", si, ii)
        S.op("pool", lambda g, inst=inst, bi=bi: g.dma_start(
            w1v[:, bi], dr["f_w1_" + inst].rearrange("(c p) F -> p c F", p=128)),
            writes=[rw1, ("w1v", bi)], kind="d")
        S.op("pool", lambda g, inst=inst, bi=bi: g.dma_start(
            w2v[:, bi], dr["f_w2_" + inst].rearrange("(f p) E -> p f E", p=128)),
            writes=[rw2, ("w2v", bi)], kind="d")
        S.op("pool", lambda g, inst=inst: g.dma_start(b1row, dr["f_b1_" + inst]),
             writes=[("rowbuf",)], kind="d")
        S.op("pool", lambda g, inst=inst: g.dma_start(b2s[:, :], dr["f_b2_" + inst]),
             writes=[("b2s",)], kind="d")
        for t in range(NT):
            S.op("pe", lambda e, t=t, ii=ii: e.transpose(
                ps["ptr"][0:16, 0:128], eqb[:, ii, t, :], sb["ident"]),
                reads=[("eqb",), ("ident",)], writes=[("ptr",)])
            S.op("act", lambda e, t=t: e.copy(eqT[:, 128 * t:128 * (t + 1)],
                                              ps["ptr"][0:16, 0:128]),
                 reads=[("ptr",)], writes=[("eqT",)],
                 wmode="w" if t == 0 else "acc")
        for g in range(NG):
            gp = g % 2
            rytg = ("ytg", gp)
            S.op("sp", lambda e, g=g, gp=gp: e.dma_start(
                ytg[:, gp], ysc[:, :, 512 * g:512 * (g + 1)]),
                reads=[("ysc", si, g)], writes=[rytg], kind="d")
            def c_mm1(tloc):
                t = 4 * g + tloc
                hp = tloc % 2
                hm = hms[hp]
                rhm = ("hm", hp)
                bp = ps["big0"] if hp == 0 else ps["pd"][:, 0:1024]
                rbig = ("big0",) if hp == 0 else ("pd",)
                for h2 in range(2):
                    cl = slice(512 * h2, 512 * (h2 + 1))
                    for c in range(8):
                        S.op("pe", lambda e, c=c, gp=gp, tloc=tloc, bi=bi, cl=cl, bp=bp: e.matmul(
                            bp[:, cl], ytg[:, gp, c, 128 * tloc:128 * (tloc + 1)],
                            w1v[:, bi, c, cl], start=(c == 0), stop=False),
                            reads=[rytg, rw1], writes=[rbig],
                            wmode="w" if (c == 0 and h2 == 0) else "acc")
                    S.op("pe", lambda e, cl=cl, bp=bp: e.matmul(
                        bp[:, cl], sb["ones_col"], b1row[0:1, cl],
                        start=False, stop=True),
                        reads=[("ones_col",), ("rowbuf",)], writes=[rbig], wmode="acc")
                S.op("dve", lambda e, hm=hm, ii=ii, t=t, bp=bp: e.scalar_tensor_tensor(
                    hm.rearrange("p (l h) -> p l h", l=16), bp.rearrange("p (l h) -> p l h", l=16),
                    0.0, eqb[:, ii, t, :, None].broadcast_to([128, 16, 64]),
                    ALU.max, ALU.mult),
                    reads=[rbig, ("eqb",)], writes=[rhm])

            def c_tr(tloc):
                hp = tloc % 2
                hm = hms[hp]
                rhm = ("hm", hp)
                for c in range(8):
                    S.op("pe", lambda e, c=c, hm=hm: e.transpose(
                        ps["ptr"][:, 128 * c:128 * (c + 1)],
                        hm[:, 128 * c:128 * (c + 1)], sb["ident"]),
                        reads=[rhm, ("ident",)], writes=[("ptr",)],
                        wmode="w" if c == 0 else "acc")
                S.op("act", lambda e, tloc=tloc: e.copy(
                    hmT[:, :, 128 * tloc:128 * (tloc + 1)],
                    ps["ptr"].rearrange("p (c q) -> p c q", c=8)),
                    reads=[("ptr",)], writes=[("hmT", tloc)])

            for tloc in range(4):
                c_mm1(tloc)
                if tloc > 0:
                    c_tr(tloc - 1)
            c_tr(3)
            if inst in ("q", "k"):
                for E in range(8):
                    mp = ps["med%d" % (E % 2)]
                    rmp = ("med%d" % (E % 2),)
                    for f in range(8):
                        S.op("pe", lambda e, E=E, f=f, bi=bi, mp=mp: e.matmul(
                            mp[:], w2v[:, bi, f, 128 * E:128 * (E + 1)], hmT[:, f, :],
                            start=(f == 0), stop=False),
                            reads=[rw2, ("hmT", 0), ("hmT", 1), ("hmT", 2), ("hmT", 3)],
                            writes=[rmp], wmode="w" if f == 0 else "acc")
                    S.op("pe", lambda e, E=E, g=g, mp=mp: e.matmul(
                        mp[:], b2s[:, 128 * E:128 * (E + 1)], eqT[:, 512 * g:512 * (g + 1)],
                        start=False, stop=True),
                        reads=[("b2s",), ("eqT",)], writes=[rmp], wmode="acc")
                    S.op("dve", lambda e, E=E, mp=mp: e.tensor_copy(qTg[:, E, :], mp[:]),
                         reads=[rmp], writes=[("qTg", E)])
                dst = qsc if inst == "q" else ksc
                S.op("sp", lambda e, g=g, dst=dst: e.dma_start(
                    dst[:, :, 512 * g:512 * (g + 1)], qTg[:, :, :]),
                    reads=[("qTg", E) for E in range(8)],
                    writes=[("qsc" if inst == "q" else "ksc", si)], kind="d",
                    wmode="w" if g == 0 else "acc")
            else:
                for tloc in range(4):
                    t = 4 * g + tloc
                    hp = tloc % 2
                    bp = ps["big0"] if hp == 0 else ps["pd"][:, 0:1024]
                    rbig = ("big0",) if hp == 0 else ("pd",)
                    for h2 in range(2):
                        cl = slice(512 * h2, 512 * (h2 + 1))
                        for f in range(8):
                            S.op("pe", lambda e, f=f, tloc=tloc, bi=bi, cl=cl, bp=bp: e.matmul(
                                bp[:, cl], hmT[:, f, 128 * tloc:128 * (tloc + 1)],
                                w2v[:, bi, f, cl], start=(f == 0), stop=False),
                                reads=[("hmT", tloc), rw2], writes=[rbig],
                                wmode="w" if (f == 0 and h2 == 0) else "acc")
                        S.op("pe", lambda e, t=t, cl=cl, bp=bp: e.matmul(
                            bp[:, cl], eqT[:, 128 * t:128 * (t + 1)], b2s[:, cl],
                            start=False, stop=True),
                            reads=[("eqT",), ("b2s",)], writes=[rbig], wmode="acc")
                    S.op("act", lambda e, t=t, bp=bp: e.copy(Vv[:, t, :], bp[:]),
                         reads=[rbig], writes=[("Vv", si)],
                         wmode="w" if t == 0 else "acc")

    if DEBUG_TAPS:
        S.op("sp", lambda e: e.dma_start(dr["dbg_y"], ysc[:, :, :]),
             reads=[("ysc", si)], kind="d")
        S.op("sp", lambda e: e.dma_start(dr["dbg_q"], qsc[:, :, :]),
             reads=[("qsc", si)], kind="d")
        S.op("sp", lambda e: e.dma_start(dr["dbg_k"], ksc[:, :, :]),
             reads=[("ksc", si)], kind="d")
        S.op("sp", lambda e: e.dma_start(dr["dbg_v"][:, :, 0:1024], Vv[:, :, :]),
             reads=[("Vv", si)], kind="d")
        S.op("sp", lambda e: e.dma_start(dr["dbg_s"], scoresAll[:, :]),
             reads=[("scoresAll",)], kind="d")
        S.op("sp", lambda e: e.dma_start(
            dr["dbg_e"], eqb.rearrange("p i t l -> p (i t l)")),
            reads=[("eqb",)], kind="d")

    kres = BIG1.rearrange("p (c t) -> p c t", c=8)
    S.barrier_regions([("kres",)], [("fw1", si, 0), ("fw1", si, 1), ("fw1", si, 2),
                                    ("fw2", si, 0), ("fw2", si, 1), ("fw2", si, 2),
                                    ("w1v", 0), ("w1v", 1), ("w2v", 0), ("w2v", 1)])
    S.op("sp", lambda e: e.dma_start(kres[:, :, :], ksc[:, :, :]),
         reads=[("ksc", si)], writes=[("kres",)], kind="d")
    qttv = [sb["ytt0"], hms[0].rearrange("p (c q) -> p c q", c=8)]
    for qt in range(NT):
        qp = qt % 2
        qtt = qttv[qp]
        rqtt = ("qtt", si, qp)
        S.op("sp", lambda e, qt=qt, qtt=qtt: e.dma_start(
            qtt[:, :, :], qsc[:, :, 128 * qt:128 * (qt + 1)]),
            reads=[("qsc", si)], writes=[rqtt], kind="d")
        xa = sb["xa0"]
        rxa = ("xa", si, 0)
        S.op("sp", lambda e, qt=qt, xa=xa: e.dma_start(
            xa[:], x_src[128 * qt:128 * (qt + 1), :]),
            reads=[("x", si)], writes=[rxa], kind="d")
        rpd = ("pd",)
        eslot = [sb["eT0"][:, 0:256], sb["eT0"][:, 256:512], sb["eT0"][:, 512:768]]

        def fff_av(kp2):
            sl = kp2 % 3
            eT = eslot[sl]
            reT = ("eT", sl)
            for ktl in range(2):
                kt = 2 * kp2 + ktl
                for h2 in range(2):
                    cl = slice(512 * h2, 512 * (h2 + 1))
                    S.op("pe", lambda e, kt=kt, ktl=ktl, eT=eT, cl=cl: e.matmul(
                        ps["pd"][:, cl], eT[:, 128 * ktl:128 * (ktl + 1)], Vv[:, kt, cl],
                        start=(kt == 0), stop=(kt == NT - 1)),
                        reads=[reT, ("Vv", si)], writes=[rpd],
                        wmode="w" if (kt == 0 and h2 == 0) else "acc")
                S.op("pe", lambda e, kt=kt, ktl=ktl, eT=eT: e.matmul(
                    ps["pd"][:, 1024:1025], eT[:, 128 * ktl:128 * (ktl + 1)], sb["ones_k"],
                    start=(kt == 0), stop=(kt == NT - 1)),
                    reads=[reT, ("ones_k",)], writes=[rpd], wmode="acc")

        def fff_sc(kp2):
            par = kp2 % 2
            sl = kp2 % 3
            mp = ps["med%d" % par]
            rmp = ("med%d" % par,)
            for ktl in range(2):
                kt = 2 * kp2 + ktl
                for c in range(8):
                    S.op("pe", lambda e, c=c, kt=kt, ktl=ktl, qtt=qtt, mp=mp: e.matmul(
                        mp[:, 128 * ktl:128 * (ktl + 1)], kres[:, c, 128 * kt:128 * (kt + 1)],
                        qtt[:, c, :], start=(c == 0), stop=(c == 7)),
                        reads=[("kres",), rqtt], writes=[rmp],
                        wmode="w" if (c == 0 and ktl == 0) else "acc")
            eT = eslot[sl]
            S.op("act", lambda e, eT=eT, mp=mp: e.activation(
                eT[:], mp[:, 0:256], AF.Exp, scale=CON_SCALE_F),
                reads=[rmp, ("con",)], writes=[("eT", sl)])

        NK2 = NT // 2
        for kp2 in range(NK2):
            fff_sc(kp2)
            if kp2 >= 2:
                fff_av(kp2 - 2)
        fff_av(NK2 - 2)
        fff_av(NK2 - 1)
        sm = sb["small"]
        S.op("dve", lambda e, qp=qp: e.tensor_scalar_mul(
            sm[:, qp:qp + 1], ps["pd"][:, 1024:1025], 2.0),
            reads=[rpd], writes=[("small", qp)])
        S.op("dve", lambda e, qp=qp: e.reciprocal(sm[:, 2 + qp:3 + qp], sm[:, qp:qp + 1]),
             reads=[("small", qp)], writes=[("small", qp)], wmode="acc")
        xm = sb["xmix0"]
        rxm = ("xmix",)
        S.op("dve", lambda e, qp=qp, xa=xa, xm=xm: e.scalar_tensor_tensor(
            xm[:], ps["pd"][:, 0:1024], sm[:, 2 + qp:3 + qp], xa[:], ALU.mult, ALU.add),
            reads=[rpd, ("small", qp), rxa], writes=[rxm])
        S.op("sp", lambda e, qt=qt, xm=xm: e.dma_start(
            x_dst[128 * qt:128 * (qt + 1), :], xm[:]),
            reads=[rxm], writes=[("x", si + 1)],
            wmode="w" if qt == 0 else "acc", kind="d")


def emit_att0(C, x_src, x_dst, si):
    S, sb, ps, dr, nc = C.S, C.sb, C.ps, C.dr, C.nc
    ysc, qsc, ksc = dr["ysc"], dr["qsc"], dr["ksc"]
    BIG1, BIG2, MEDA, MEDB = sb["BIG1"], sb["BIG2"], sb["MEDA"], sb["MEDB"]
    PB, ROW = sb["POOLB"], sb["ROW"]
    wqv = BIG1[:, 0:8192].rearrange("p (c E) -> p c E", c=8)
    wkv = BIG1[:, 8192:16384].rearrange("p (c E) -> p c E", c=8)
    wvv = BIG1[:, 16384:24576].rearrange("p (c E) -> p c E", c=8)
    wov = MEDA[:, 0:8192].rearrange("p (c E) -> p c E", c=8)
    wob = ROW[:, 176:1200]
    Vp = BIG2[:, 0:33280].rearrange("p (t h j) -> p t h j", t=32, h=16)
    cosb = MEDB[:, 0:2048].rearrange("p (c q) -> p c q", c=4)
    sinb = MEDB[:, 2048:4096].rearrange("p (c q) -> p c q", c=4)
    ytg = PB[:, 0:4096].rearrange("p (c q) -> p c q", c=8)
    eT0, eT1 = sb["eT0"], sb["eT1"]
    eT0f = eT0.bitcast(F32)
    C.ybuf = [sb["y0"], MEDB[:, 4096:5120]]

    for nm, dst_ in (("a0_wqT", wqv), ("a0_wkT", wkv), ("a0_wvT", wvv)):
        S.op("pool", lambda g, nm=nm, dst_=dst_: g.dma_start(
            dst_, dr[nm].rearrange("(c p) E -> p c E", p=128)),
            writes=[("a0w", nm)], kind="d")
    S.op("pool", lambda g: g.dma_start(
        wov, dr["a0_woT"].rearrange("(c p) E -> p c E", p=128)),
        writes=[("a0wo",)], kind="d")
    S.op("pool", lambda g: g.dma_start(wob, dr["a0_wob"]),
        writes=[("rowbuf",)], kind="d")
    S.op("dve", lambda e: e.memset(Vp[:, :, :, 64:65], 1.0), writes=[("Vv", si)])

    def a0_ln(t, p, ry):
        ytt = sb["ytt0"]
        rytt = ("ytt", si)
        transpose_y(C, ry, p, t, si,
                    lambda e, ytt=ytt: e.copy(ytt[:, :, :],
                                              ps["ptr"].rearrange("p (c q) -> p c q", c=8)),
                    [rytt])
        S.op("sp", lambda e, ytt=ytt, t=t: e.dma_start(ysc[:, :, 128 * t:128 * (t + 1)], ytt[:, :, :]),
             reads=[rytt], writes=[("ysc", si, t // 4)], kind="d",
             wmode="w" if t % 4 == 0 else "acc")

    ln_loop(C, x_src, si, a0_ln)

    for g in range(NG):
        S.op("pool", lambda e, g=g: e.dma_start(
            cosb, dr["a0_cos"].rearrange("(c p) t -> p c t", p=128)[:, :, 512 * g:512 * (g + 1)]),
            writes=[("cosb",)], kind="d")
        S.op("pool", lambda e, g=g: e.dma_start(
            sinb, dr["a0_sin"].rearrange("(c p) t -> p c t", p=128)[:, :, 512 * g:512 * (g + 1)]),
            writes=[("sinb",)], kind="d")
        S.op("sp", lambda e, g=g: e.dma_start(ytg, ysc[:, :, 512 * g:512 * (g + 1)]),
             reads=[("ysc", si, g)], writes=[("ytg", 0)], kind="d")
        for nm, wv_, dst in (("q", wqv, qsc), ("k", wkv, ksc)):
            for c in range(4):
                cp = c % 2
                if cp == 0:
                    mA, mB = ps["med0"], ps["med1"]
                    rA, rB = ("big1", 0), ("big1", 1)
                else:
                    mA, mB = ps["pd"][:, 512:1024], ps["pd"][:, 1024:1536]
                    rA, rB = ("pd", 1), ("pd", 2)
                for mi, rc, mp, rm in ((0, c, mA, rA), (1, c + 4, mB, rB)):
                    for cc in range(8):
                        S.op("pe", lambda e, cc=cc, rc=rc, wv_=wv_, mp=mp: e.matmul(
                            mp[:], wv_[:, cc, 128 * rc:128 * (rc + 1)], ytg[:, cc, :],
                            start=(cc == 0), stop=(cc == 7)),
                            reads=[("ytg", 0), ("a0w", "a0_" + ("wqT" if nm == "q" else "wkT"))],
                            writes=[rm],
                            wmode="w" if cc == 0 else "acc")
                t1 = sb["ytt0"].rearrange("p c q -> p (c q)").bitcast(F32)[:, 0:512]
                t2 = eT0f
                oE = eT1[:, 0:512]; oO = eT1[:, 512:1024]
                S.op("dve", lambda e, c=c, t1=t1, mA=mA: e.tensor_mul(t1, mA, cosb[:, c, :]),
                     reads=[rA, ("cosb",)], writes=[("ytt", si)])
                S.op("dve", lambda e, c=c, t2=t2, mB=mB: e.tensor_mul(t2, mB, sinb[:, c, :]),
                     reads=[rB, ("sinb",)], writes=[("eT", 0)])
                S.op("pool", lambda e, t1=t1, t2=t2, oE=oE: e.tensor_sub(oE, t1, t2),
                     reads=[("ytt", si), ("eT", 0)], writes=[("eT", 1)])
                S.op("dve", lambda e, c=c, t1=t1, mA=mA: e.tensor_mul(t1, mA, sinb[:, c, :]),
                     reads=[rA, ("sinb",)], writes=[("ytt", si)])
                S.op("dve", lambda e, c=c, t2=t2, mB=mB: e.tensor_mul(t2, mB, cosb[:, c, :]),
                     reads=[rB, ("cosb",)], writes=[("eT", 0)])
                S.op("pool", lambda e, t1=t1, t2=t2, oO=oO: e.tensor_add(oO, t1, t2),
                     reads=[("ytt", si), ("eT", 0)], writes=[("eT", 1)], wmode="acc")
                S.op("sp", lambda e, c=c, g=g, dst=dst: e.dma_start(
                    dst[:, c:c + 5:4, 512 * g:512 * (g + 1)],
                    eT1.rearrange("p (a q) -> p a q", a=2)),
                    reads=[("eT", 1)], writes=[("qsc" if dst is qsc else "ksc", si)],
                    kind="d", wmode="w" if (g == 0 and c == 0) else "acc")
        for tloc in range(4):
            t = 4 * g + tloc
            for h2 in range(2):
                cl = slice(512 * h2, 512 * (h2 + 1))
                for cc in range(8):
                    S.op("pe", lambda e, cc=cc, tloc=tloc, cl=cl: e.matmul(
                        ps["big0"][:, cl], ytg[:, cc, 128 * tloc:128 * (tloc + 1)],
                        wvv[:, cc, cl], start=(cc == 0), stop=(cc == 7)),
                        reads=[("ytg", 0), ("a0w", "a0_wvT")], writes=[("big0", 0), ("big0", 1)],
                        wmode="w" if (cc == 0 and h2 == 0) else "acc")
            S.op("act", lambda e, t=t: e.copy(
                Vp[:, t, :, 0:64], ps["big0"].rearrange("p (h j) -> p h j", h=16)),
                reads=[("big0", 0), ("big0", 1)], writes=[("Vv", si)], wmode="acc")

    if DEBUG_TAPS:
        S.op("sp", lambda e: e.dma_start(dr["dbg_y"], ysc[:, :, :]),
             reads=[("ysc", si)], kind="d")
        S.op("sp", lambda e: e.dma_start(dr["dbg_q"], qsc[:, :, :]),
             reads=[("qsc", si)], kind="d")
        S.op("sp", lambda e: e.dma_start(dr["dbg_k"], ksc[:, :, :]),
             reads=[("ksc", si)], kind="d")
        S.op("sp", lambda e: e.dma_start(dr["dbg_v"], BIG2[:, 0:33280].rearrange("p (t j) -> p t j", t=32)),
             reads=[("Vv", si)], kind="d")

    kres = BIG1[:, 0:32768].rearrange("p (c t) -> p c t", c=8)
    S.barrier_regions([("kres",)], [("a0w", "a0_wqT"), ("a0w", "a0_wkT"), ("a0w", "a0_wvT")])
    S.op("sp", lambda e: e.dma_start(kres[:, :, :], ksc[:, :, :]),
         reads=[("ksc", si)], writes=[("kres",)], kind="d")
    qttv = [PB[:, 0:2048].rearrange("p (c q) -> p c q", c=8),
            PB[:, 2048:4096].rearrange("p (c q) -> p c q", c=8)]
    aoT = sb["ytt0"]
    ptrf = ps["ptr"].bitcast(F32)
    eAslot = [eT0[:, 0:512], eT0[:, 512:1024], MEDB[:, 5120:5632]]
    eBslot = [eT1[:, 0:512], eT1[:, 512:1024], MEDB[:, 5632:6144]]
    avac = [[ps["pd"][:, 0:65], ps["pd"][:, 1024:1089]],
            [ps["pd"][:, 512:577], ptrf[:, 0:65]]]
    avkey = [[("pd",), ("pd", 2)], [("pd", 1), ("ptr",)]]
    sm = sb["small"]
    y0 = sb["y0"]
    xa = sb["xa0"]
    rxa = ("xa", si, 0)
    KB2 = 16
    for QP in range(16):
        qp = QP % 2
        qtt = qttv[qp]
        rqtt = ("qtt", si, qp)
        S.op("sp", lambda e, QP=QP, qtt=qtt: e.dma_start(
            qtt[:, :, :], qsc[:, :, 256 * QP:256 * (QP + 1)]),
            reads=[("qsc", si)],
            writes=[rqtt] + ([("ytg", 0)] if QP < 2 else []), kind="d")

        def a0_scores(snum, j, ktB2):
            par = snum % 2
            sl = snum % 3
            sA = ps["big0"][:, 512 * par:512 * (par + 1)]
            sB = ps["big1"][:, 512 * par:512 * (par + 1)]
            rsA = ("big0", par)
            rsB = ("big1", par)
            for ktl in range(2):
                kt = 2 * ktB2 + ktl
                S.op("pe", lambda e, j=j, kt=kt, ktl=ktl, sA=sA, qtt=qtt: e.matmul(
                    sA[:, 256 * ktl:256 * (ktl + 1)], kres[0:64, j, 128 * kt:128 * (kt + 1)],
                    qtt[0:64, j, :], start=True, stop=True, tile_position=(0, 0)),
                    reads=[("kres",), rqtt], writes=[rsA],
                    wmode="w" if ktl == 0 else "acc")
                S.op("pe", lambda e, j=j, kt=kt, ktl=ktl, sB=sB, qtt=qtt: e.matmul(
                    sB[:, 256 * ktl:256 * (ktl + 1)], kres[64:128, j, 128 * kt:128 * (kt + 1)],
                    qtt[64:128, j, :], start=True, stop=True, tile_position=(64, 0)),
                    reads=[("kres",), rqtt], writes=[rsB],
                    wmode="w" if ktl == 0 else "acc")
            eA = eAslot[sl]
            eB = eBslot[sl]
            S.op("act", lambda e, sA=sA, eA=eA: e.activation(eA, sA, AF.Exp, scale=CON_SCALE_A),
                 reads=[rsA, ("con",)], writes=[("eA", sl)])
            S.op("act", lambda e, sB=sB, eB=eB: e.activation(eB, sB, AF.Exp, scale=CON_SCALE_A),
                 reads=[rsB, ("con",)], writes=[("eB", sl)])

        def a0_av(snum, j, ktB2):
            sl = snum % 3
            for hs, ee, ree in ((0, eAslot[sl], ("eA", sl)), (1, eBslot[sl], ("eB", sl))):
                hh = 2 * j + hs
                for qsub in range(2):
                    for ktl in range(2):
                        kt = 2 * ktB2 + ktl
                        first = (ktB2 == 0 and ktl == 0)
                        S.op("pe", lambda e, hs=hs, hh=hh, kt=kt, ktl=ktl, qsub=qsub, ee=ee, ktB2=ktB2: e.matmul(
                            avac[hs][qsub],
                            ee[:, 256 * ktl + 128 * qsub:256 * ktl + 128 * qsub + 128],
                            Vp[:, kt, hh, :],
                            start=(ktB2 == 0 and ktl == 0), stop=(ktB2 == KB2 - 1 and ktl == 1)),
                            reads=[ree, ("Vv", si)],
                            writes=[avkey[hs][qsub]], wmode="w" if first else "acc")

        def a0_norm(j):
            for qsub in range(2):
                S.op("dve", lambda e, qsub=qsub: e.tensor_copy(
                    sm[:, 2 * qsub:2 * qsub + 1], avac[0][qsub][:, 64:65]),
                    reads=[avkey[0][qsub]], writes=[("small", 0)],
                    wmode="w" if qsub == 0 else "acc")
                S.op("dve", lambda e, qsub=qsub: e.tensor_copy(
                    sm[:, 2 * qsub + 1:2 * qsub + 2], avac[1][qsub][:, 64:65]),
                    reads=[avkey[1][qsub]], writes=[("small", 0)], wmode="acc")
            S.op("dve", lambda e: e.reciprocal(sm[:, 4:8], sm[:, 0:4]),
                 reads=[("small", 0)], writes=[("small", 0)], wmode="acc")
            for qsub in range(2):
                yb = y0 if qsub == 0 else C.ybuf[1]
                ryb = ("yb", si, qsub)
                for hs in range(2):
                    h = 2 * j + hs
                    S.op("dve", lambda e, h=h, hs=hs, qsub=qsub, yb=yb: e.tensor_scalar_mul(
                        yb[:, 64 * h:64 * (h + 1)], avac[hs][qsub][:, 0:64],
                        sm[:, 4 + 2 * qsub + hs:5 + 2 * qsub + hs]),
                        reads=[avkey[hs][qsub], ("small", 0)], writes=[ryb],
                        wmode="w" if (j == 0 and hs == 0) else "acc")

        steps = [(j, ktB2) for j in range(8) for ktB2 in range(KB2)]
        LOOKA = 2
        for snum, (j, ktB2) in enumerate(steps):
            a0_scores(snum, j, ktB2)
            if snum >= LOOKA:
                pj, pktB2 = steps[snum - LOOKA]
                a0_av(snum - LOOKA, pj, pktB2)
                if pktB2 == KB2 - 1:
                    a0_norm(pj)
        for snum in range(len(steps) - LOOKA, len(steps)):
            pj, pktB2 = steps[snum]
            a0_av(snum, pj, pktB2)
            if pktB2 == KB2 - 1:
                a0_norm(pj)

        for qsub in range(2):
            qt = 2 * QP + qsub
            yb = y0 if qsub == 0 else C.ybuf[1]
            ryb = ("yb", si, qsub)
            S.op("sp", lambda e, qt=qt, xa=xa: e.dma_start(
                xa[:], x_src[128 * qt:128 * (qt + 1), :]),
                reads=[("x", si)], writes=[rxa], kind="d")
            for c in range(8):
                S.op("pe", lambda e, c=c, yb=yb: e.transpose(
                    ps["ptr"][:, 128 * c:128 * (c + 1)], yb[:, 128 * c:128 * (c + 1)], sb["ident"]),
                    reads=[ryb, ("ident",)], writes=[("ptr",)],
                    wmode="w" if c == 0 else "acc")
            S.op("act", lambda e: e.copy(aoT[:, :, :], ps["ptr"].rearrange("p (c q) -> p c q", c=8)),
                 reads=[("ptr",)], writes=[("aoT",)])
            for h2 in range(2):
                cl = slice(512 * h2, 512 * (h2 + 1))
                for c in range(8):
                    S.op("pe", lambda e, c=c, cl=cl: e.matmul(
                        ps["big0"][:, cl], aoT[:, c, :], wov[:, c, cl],
                        start=(c == 0), stop=False),
                        reads=[("aoT",), ("a0wo",)], writes=[("big0", 0), ("big0", 1)],
                        wmode="w" if (c == 0 and h2 == 0) else "acc")
                S.op("pe", lambda e, cl=cl: e.matmul(
                    ps["big0"][:, cl], sb["ones_col"], wob[0:1, cl],
                    start=False, stop=True),
                    reads=[("ones_col",), ("rowbuf",)], writes=[("big0", 0), ("big0", 1)], wmode="acc")
            xm = sb["xmix0"]
            S.op("dve", lambda e, xa=xa, xm=xm: e.tensor_add(xm[:], ps["big0"][:], xa[:]),
                 reads=[("big0", 0), ("big0", 1), rxa], writes=[("xmix",)])
            S.op("sp", lambda e, qt=qt, xm=xm: e.dma_start(
                x_dst[128 * qt:128 * (qt + 1), :], xm[:]),
                reads=[("xmix",)], writes=[("x", si + 1)],
                wmode="w" if qt == 0 else "acc", kind="d")


def emit_att1(C, x_src, x_dst, si):
    S, sb, ps, dr, nc = C.S, C.sb, C.ps, C.dr, C.nc
    ysc = dr["ysc"]
    BIG1, BIG2, MEDA, MEDB = sb["BIG1"], sb["BIG2"], sb["MEDA"], sb["MEDB"]
    PB, PF, ROW = sb["POOLB"], sb["POOLF"], sb["ROW"]
    qrT = BIG1.rearrange("p (c t) -> p c t", c=8)
    Woutv = BIG2[:, 0:8192].rearrange("p (c E) -> p c E", c=8)
    Vp = BIG2[:, 8192:16512].rearrange("p (t h j) -> p t h j", t=32, h=4)
    krT = MEDA.rearrange("p (k t) -> p k t", k=4)
    Wqav = MEDB[:, 0:512].rearrange("p (c j) -> p c j", c=8)
    Wkvav = MEDB[:, 512:768].rearrange("p (c j) -> p c j", c=8)
    tri = MEDB[:, 768:896]
    qr = MEDB[:, 896:1920]
    qa_g = MEDB[0:64, 1920:2432]
    Wqb = MEDB[0:64, 2432:3456]
    ytg = MEDB[:, 3456:7552].rearrange("p (c q) -> p c q", c=8)
    kva_g = PB[0:32, 0:512]
    Wkvb = PB[0:32, 512:1024]
    dgb = PB[:, 1024:1152]
    aoT = PB[:, 1152:2176].rearrange("p (c q) -> p c q", c=8)
    kr = sb["y0"][:, 0:256]
    cosT = PF[:, 0:32]
    sinT = PF[:, 32:64]
    bout = ROW[:, 176:1200]
    eT0, eT1 = sb["eT0"], sb["eT1"]
    C.ybuf = [sb["y0"], BIG2[:, 16512:17536]]

    S.op("pool", lambda g: g.dma_start(Wqav, dr["a1_Wqa"].rearrange("(c p) j -> p c j", p=128)),
         writes=[("a1wqa",)], kind="d")
    S.op("pool", lambda g: g.dma_start(Wkvav, dr["a1_Wkva"].rearrange("(c p) j -> p c j", p=128)),
         writes=[("a1wkva",)], kind="d")
    S.op("pool", lambda g: g.dma_start(Wqb, dr["a1_Wqb"]), writes=[("a1wqb",)], kind="d")
    S.op("pool", lambda g: g.dma_start(Wkvb, dr["a1_Wkvb"]), writes=[("a1wkvb",)], kind="d")
    S.op("pool", lambda g: g.dma_start(tri, dr["tri"]), writes=[("tri",)], kind="d")
    S.op("pool", lambda g: g.dma_start(
        Woutv, dr["a1_Wout"].rearrange("(c p) E -> p c E", p=128)),
        writes=[("a1wout",)], kind="d")
    S.op("pool", lambda g: g.dma_start(bout, dr["a1_bout"]), writes=[("rowbuf",)], kind="d")
    S.op("dve", lambda e: e.memset(Vp[:, :, :, 64:65], 1.0), writes=[("Vv", si)])

    def a1_ln(t, p, ry):
        ytt = sb["ytt0"]
        rytt = ("ytt", si)
        transpose_y(C, ry, p, t, si,
                    lambda e, ytt=ytt: e.copy(ytt[:, :, :],
                                              ps["ptr"].rearrange("p (c q) -> p c q", c=8)),
                    [rytt])
        S.op("sp", lambda e, ytt=ytt, t=t: e.dma_start(ysc[:, :, 128 * t:128 * (t + 1)], ytt[:, :, :]),
             reads=[rytt], writes=[("ysc", si, t // 4)], kind="d",
             wmode="w" if t % 4 == 0 else "acc")

    ln_loop(C, x_src, si, a1_ln)

    for g in range(NG):
        S.op("sp", lambda e, g=g: e.dma_start(ytg, ysc[:, :, 512 * g:512 * (g + 1)]),
             reads=[("ysc", si, g)], writes=[("ytg", 0)], kind="d")
        for c in range(8):
            S.op("pe", lambda e, c=c: e.matmul(
                ps["med0"][0:64, :], Wqav[:, c, :], ytg[:, c, :],
                start=(c == 0), stop=(c == 7)),
                reads=[("a1wqa",), ("ytg", 0)], writes=[("big1", 0)],
                wmode="w" if c == 0 else "acc")
        S.op("dve", lambda e: e.tensor_copy(qa_g, ps["med0"][0:64, :]),
             reads=[("big1", 0)], writes=[("qa_g",)])
        for c in range(8):
            S.op("pe", lambda e, c=c: e.matmul(
                ps["med1"][0:32, :], Wkvav[:, c, :], ytg[:, c, :],
                start=(c == 0), stop=(c == 7)),
                reads=[("a1wkva",), ("ytg", 0)], writes=[("big1", 1)],
                wmode="w" if c == 0 else "acc")
        S.op("dve", lambda e: e.tensor_copy(kva_g, ps["med1"][0:32, :]),
             reads=[("big1", 1)], writes=[("kva_g",)])
        for tt in range(4):
            t = 4 * g + tt
            S.op("pool", lambda e, t=t: e.dma_start(
                cosT, dr["a1_cos"].rearrange("(t p) j -> t p j", p=128)[t]),
                writes=[("cosT",)], kind="d")
            S.op("pool", lambda e, t=t: e.dma_start(
                sinT, dr["a1_sin"].rearrange("(t p) j -> t p j", p=128)[t]),
                writes=[("sinT",)], kind="d")
            for h2 in range(2):
                cl = slice(512 * h2, 512 * (h2 + 1))
                S.op("pe", lambda e, tt=tt, cl=cl: e.matmul(
                    ps["big0"][:, cl], qa_g[:, 128 * tt:128 * (tt + 1)], Wqb[:, cl],
                    start=True, stop=True),
                    reads=[("qa_g",), ("a1wqb",)], writes=[("big0", h2)],
                    wmode="w")
            x1 = ps["big0"].rearrange("p (h d) -> p h d", h=16)[:, :, 0:64:2]
            x2 = ps["big0"].rearrange("p (h d) -> p h d", h=16)[:, :, 1:64:2]
            cB = cosT[:, None, :].broadcast_to([128, 16, 32])
            sB = sinT[:, None, :].broadcast_to([128, 16, 32])
            t1 = eT0[:, 0:512].rearrange("p (h d) -> p h d", h=16)
            t2 = eT0[:, 512:1024].rearrange("p (h d) -> p h d", h=16)
            qre = qr.rearrange("p (h d) -> p h d", h=16)[:, :, 0:64:2]
            qro = qr.rearrange("p (h d) -> p h d", h=16)[:, :, 1:64:2]
            S.op("dve", lambda e, x1=x1, cB=cB, t1=t1: e.tensor_mul(t1, x1, cB),
                 reads=[("big0", 0), ("big0", 1), ("cosT",)], writes=[("eT", 0)])
            S.op("dve", lambda e, x2=x2, sB=sB, t2=t2: e.tensor_mul(t2, x2, sB),
                 reads=[("big0", 0), ("big0", 1), ("sinT",)], writes=[("eT", 0)], wmode="acc")
            S.op("pool", lambda e, t1=t1, t2=t2, qre=qre: e.tensor_sub(qre, t1, t2),
                 reads=[("eT", 0)], writes=[("qr",)])
            S.op("dve", lambda e, x1=x1, sB=sB, t1=t1: e.tensor_mul(t1, x1, sB),
                 reads=[("big0", 0), ("big0", 1), ("sinT",)], writes=[("eT", 0)])
            S.op("dve", lambda e, x2=x2, cB=cB, t2=t2: e.tensor_mul(t2, x2, cB),
                 reads=[("big0", 0), ("big0", 1), ("cosT",)], writes=[("eT", 0)], wmode="acc")
            S.op("pool", lambda e, t1=t1, t2=t2, qro=qro: e.tensor_add(qro, t1, t2),
                 reads=[("eT", 0)], writes=[("qr",)], wmode="acc")
            for c in range(8):
                S.op("pe", lambda e, c=c: e.transpose(
                    ps["ptr"][:, 128 * c:128 * (c + 1)], qr[:, 128 * c:128 * (c + 1)], sb["ident"]),
                    reads=[("qr",), ("ident",)], writes=[("ptr",)],
                    wmode="w" if c == 0 else "acc")
            S.op("act", lambda e, t=t: e.copy(
                qrT[:, :, 128 * t:128 * (t + 1)], ps["ptr"].rearrange("p (c q) -> p c q", c=8)),
                reads=[("ptr",)], writes=[("qrT",)],
                wmode="w" if t == 0 else "acc")
            S.op("pe", lambda e, tt=tt: e.matmul(
                ps["med0"][:], kva_g[:, 128 * tt:128 * (tt + 1)], Wkvb[:],
                start=True, stop=True),
                reads=[("kva_g",), ("a1wkvb",)], writes=[("big1", 0)], wmode="w")
            kx1 = ps["med0"].rearrange("p (h d) -> p h d", h=8)[:, 0:4, 0:64:2]
            kx2 = ps["med0"].rearrange("p (h d) -> p h d", h=8)[:, 0:4, 1:64:2]
            kcB = cosT[:, None, :].broadcast_to([128, 4, 32])
            ksB = sinT[:, None, :].broadcast_to([128, 4, 32])
            kt1 = eT0[:, 0:128].rearrange("p (h d) -> p h d", h=4)
            kt2 = eT0[:, 128:256].rearrange("p (h d) -> p h d", h=4)
            kre = kr.rearrange("p (h d) -> p h d", h=4)[:, :, 0:64:2]
            kro = kr.rearrange("p (h d) -> p h d", h=4)[:, :, 1:64:2]
            S.op("dve", lambda e, kx1=kx1, kcB=kcB, kt1=kt1: e.tensor_mul(kt1, kx1, kcB),
                 reads=[("big1", 0), ("cosT",)], writes=[("eT", 0)])
            S.op("dve", lambda e, kx2=kx2, ksB=ksB, kt2=kt2: e.tensor_mul(kt2, kx2, ksB),
                 reads=[("big1", 0), ("sinT",)], writes=[("eT", 0)], wmode="acc")
            S.op("pool", lambda e, kt1=kt1, kt2=kt2, kre=kre: e.tensor_sub(kre, kt1, kt2),
                 reads=[("eT", 0)], writes=[("y", si)])
            S.op("dve", lambda e, kx1=kx1, ksB=ksB, kt1=kt1: e.tensor_mul(kt1, kx1, ksB),
                 reads=[("big1", 0), ("sinT",)], writes=[("eT", 0)])
            S.op("dve", lambda e, kx2=kx2, kcB=kcB, kt2=kt2: e.tensor_mul(kt2, kx2, kcB),
                 reads=[("big1", 0), ("cosT",)], writes=[("eT", 0)], wmode="acc")
            S.op("pool", lambda e, kt1=kt1, kt2=kt2, kro=kro: e.tensor_add(kro, kt1, kt2),
                 reads=[("eT", 0)], writes=[("y", si)], wmode="acc")
            S.op("act", lambda e, t=t: e.copy(
                Vp[:, t, :, 0:64], ps["med0"][:, 256:512].rearrange("p (h j) -> p h j", h=4)),
                reads=[("big1", 0)], writes=[("Vv", si)], wmode="acc")
            for k4 in range(4):
                S.op("pe", lambda e, k4=k4: e.transpose(
                    ps["ptr"][0:64, 128 * k4:128 * k4 + 128], kr[:, 64 * k4:64 * (k4 + 1)], sb["ident"]),
                    reads=[("y", si), ("ident",)], writes=[("ptr",)],
                    wmode="w" if k4 == 0 else "acc")
                S.op("pe", lambda e, k4=k4: e.transpose(
                    ps["ptr"][64:128, 128 * k4:128 * k4 + 128], kr[:, 64 * k4:64 * (k4 + 1)], sb["ident"]),
                    reads=[("y", si), ("ident",)], writes=[("ptr",)], wmode="acc")
            S.op("act", lambda e, t=t: e.copy(
                krT[:, :, 128 * t:128 * (t + 1)], ps["ptr"][:, 0:512].rearrange("p (k q) -> p k q", k=4)),
                reads=[("ptr",)], writes=[("krT",)],
                wmode="w" if t == 0 else "acc")

    ptrf = ps["ptr"].bitcast(F32)
    eAslot = [eT0[:, 0:512], eT0[:, 512:1024], PB[:, 2176:2688]]
    eBslot = [eT1[:, 0:512], eT1[:, 512:1024], PB[:, 2688:3200]]
    avac = [[ps["pd"][:, 0:65], ps["pd"][:, 1024:1089]],
            [ps["pd"][:, 512:577], ptrf[:, 0:65]]]
    avkey = [[("pd",), ("pd", 2)], [("pd", 1), ("ptr",)]]
    aoT = sb["ytt0"]
    sm = sb["small"]
    y0 = sb["y0"]
    xa = sb["xa0"]
    rxa = ("xa", si, 0)
    for QP in range(16):
        nkt = 2 * QP + 2
        KB2 = (nkt + 1) // 2

        def a1_scores(snum, j, ktB2):
            par = snum % 2
            sl = snum % 3
            kts = [k for k in (2 * ktB2, 2 * ktB2 + 1) if k < nkt]
            sA = ps["big0"][:, 512 * par:512 * (par + 1)]
            sB = ps["big1"][:, 512 * par:512 * (par + 1)]
            rsA = ("big0", par)
            rsB = ("big1", par)
            for ktl, kt in enumerate(kts):
                S.op("pe", lambda e, j=j, kt=kt, ktl=ktl, sA=sA, QP=QP: e.matmul(
                    sA[:, 256 * ktl:256 * (ktl + 1)], krT[0:64, j // 2, 128 * kt:128 * (kt + 1)],
                    qrT[0:64, j, 256 * QP:256 * (QP + 1)], start=True, stop=True,
                    tile_position=(0, 0)),
                    reads=[("krT",), ("qrT",)], writes=[rsA],
                    wmode="w" if ktl == 0 else "acc")
                S.op("pe", lambda e, j=j, kt=kt, ktl=ktl, sB=sB, QP=QP: e.matmul(
                    sB[:, 256 * ktl:256 * (ktl + 1)], krT[64:128, j // 2, 128 * kt:128 * (kt + 1)],
                    qrT[64:128, j, 256 * QP:256 * (QP + 1)], start=True, stop=True,
                    tile_position=(64, 0)),
                    reads=[("krT",), ("qrT",)], writes=[rsB],
                    wmode="w" if ktl == 0 else "acc")
            eA = eAslot[sl]
            eB = eBslot[sl]
            nc4 = 256 * len(kts)
            S.op("act", lambda e, sA=sA, eA=eA, nc4=nc4: e.activation(
                eA[:, 0:nc4], sA[:, 0:nc4], AF.Exp, scale=CON_SCALE_A),
                reads=[rsA, ("con",)], writes=[("eA", sl)])
            S.op("act", lambda e, sB=sB, eB=eB, nc4=nc4: e.activation(
                eB[:, 0:nc4], sB[:, 0:nc4], AF.Exp, scale=CON_SCALE_A),
                reads=[rsB, ("con",)], writes=[("eB", sl)])

        def a1_av(snum, j, ktB2):
            sl = snum % 3
            kts = [k for k in (2 * ktB2, 2 * ktB2 + 1) if k < nkt]
            for hs, ee, ree in ((0, eAslot[sl], ("eA", sl)), (1, eBslot[sl], ("eB", sl))):
                hh = 2 * j + hs
                kvh = hh // 4
                for qsub in range(2):
                    qglob = 2 * QP + qsub
                    for ktl, kt in enumerate(kts):
                        if kt > qglob:
                            continue
                        first = (ktB2 == 0 and ktl == 0)
                        stop = (kt == qglob)
                        diag = (kt == qglob)
                        src_ap = ee[:, 256 * ktl + 128 * qsub:256 * ktl + 128 * qsub + 128]
                        if diag:
                            S.op("dve", lambda e, src_ap=src_ap: e.tensor_mul(
                                dgb[:], src_ap, tri[:]),
                                reads=[ree, ("tri",)], writes=[("dgb",)])
                            rl = ("dgb",)
                        else:
                            rl = ree
                        S.op("pe", (lambda e, hs=hs, hh=hh, kt=kt, ktl=ktl, qsub=qsub,
                                    ee=ee, kvh=kvh, diag=diag, src_ap=src_ap:
                                    e.matmul(avac[hs][qsub],
                                             dgb[:] if diag else src_ap,
                                             Vp[:, kt, kvh, :],
                                             start=(kt == 0), stop=diag)),
                            reads=[rl, ("Vv", si)],
                            writes=[avkey[hs][qsub]] if not first else
                            [avkey[hs][qsub], ("pd",), ("pd", 1), ("pd", 2), ("ptr",)],
                            wmode="w" if first else "acc")

        def a1_norm(j):
            for qsub in range(2):
                S.op("dve", lambda e, qsub=qsub: e.tensor_copy(
                    sm[:, 2 * qsub:2 * qsub + 1], avac[0][qsub][:, 64:65]),
                    reads=[avkey[0][qsub]], writes=[("small", 0)],
                    wmode="w" if qsub == 0 else "acc")
                S.op("dve", lambda e, qsub=qsub: e.tensor_copy(
                    sm[:, 2 * qsub + 1:2 * qsub + 2], avac[1][qsub][:, 64:65]),
                    reads=[avkey[1][qsub]], writes=[("small", 0)], wmode="acc")
            S.op("dve", lambda e: e.reciprocal(sm[:, 4:8], sm[:, 0:4]),
                 reads=[("small", 0)], writes=[("small", 0)], wmode="acc")
            for qsub in range(2):
                yb = y0 if qsub == 0 else C.ybuf[1]
                ryb = ("yb", si, qsub)
                for hs in range(2):
                    h = 2 * j + hs
                    S.op("dve", lambda e, h=h, hs=hs, qsub=qsub, yb=yb: e.tensor_scalar_mul(
                        yb[:, 64 * h:64 * (h + 1)], avac[hs][qsub][:, 0:64],
                        sm[:, 4 + 2 * qsub + hs:5 + 2 * qsub + hs]),
                        reads=[avkey[hs][qsub], ("small", 0)], writes=[ryb],
                        wmode="w" if (j == 0 and hs == 0) else "acc")

        steps = [(j, ktB2) for j in range(8) for ktB2 in range(KB2)]
        LOOKA = 2 if len(steps) >= 2 else 1
        for snum, (j, ktB2) in enumerate(steps):
            a1_scores(snum, j, ktB2)
            if snum >= LOOKA:
                pj, pktB2 = steps[snum - LOOKA]
                a1_av(snum - LOOKA, pj, pktB2)
                if pktB2 == KB2 - 1:
                    a1_norm(pj)
        for snum in range(len(steps) - LOOKA, len(steps)):
            pj, pktB2 = steps[snum]
            a1_av(snum, pj, pktB2)
            if pktB2 == KB2 - 1:
                a1_norm(pj)

        for qsub in range(2):
            qt = 2 * QP + qsub
            yb = y0 if qsub == 0 else C.ybuf[1]
            ryb = ("yb", si, qsub)
            S.op("sp", lambda e, qt=qt, xa=xa: e.dma_start(
                xa[:], x_src[128 * qt:128 * (qt + 1), :]),
                reads=[("x", si)], writes=[rxa], kind="d")
            for c in range(8):
                S.op("pe", lambda e, c=c, yb=yb: e.transpose(
                    ps["ptr"][:, 128 * c:128 * (c + 1)], yb[:, 128 * c:128 * (c + 1)], sb["ident"]),
                    reads=[ryb, ("ident",)], writes=[("ptr",)],
                    wmode="w" if c == 0 else "acc")
            S.op("act", lambda e: e.copy(aoT[:, :, :], ps["ptr"].rearrange("p (c q) -> p c q", c=8)),
                 reads=[("ptr",)], writes=[("aoT",)])
            for h2 in range(2):
                cl = slice(512 * h2, 512 * (h2 + 1))
                for c in range(8):
                    S.op("pe", lambda e, c=c, cl=cl: e.matmul(
                        ps["big0"][:, cl], aoT[:, c, :], Woutv[:, c, cl],
                        start=(c == 0), stop=False),
                        reads=[("aoT",), ("a1wout",)], writes=[("big0", 0), ("big0", 1)],
                        wmode="w" if (c == 0 and h2 == 0) else "acc")
                S.op("pe", lambda e, cl=cl: e.matmul(
                    ps["big0"][:, cl], sb["ones_col"], bout[0:1, cl],
                    start=False, stop=True),
                    reads=[("ones_col",), ("rowbuf",)], writes=[("big0", 0), ("big0", 1)], wmode="acc")
            xm = sb["xmix0"]
            S.op("dve", lambda e, xa=xa, xm=xm: e.tensor_add(xm[:], ps["big0"][:], xa[:]),
                 reads=[("big0", 0), ("big0", 1), rxa], writes=[("xmix",)])
            S.op("sp", lambda e, qt=qt, xm=xm: e.dma_start(
                x_dst[128 * qt:128 * (qt + 1), :], xm[:]),
                reads=[("xmix",)], writes=[("x", si + 1)],
                wmode="w" if qt == 0 else "acc", kind="d")


def emit_moe(C, x_src, x_dst, si):
    S, sb, ps, dr, nc = C.S, C.sb, C.ps, C.dr, C.nc
    ysc = dr["ysc"]
    BIG1, BIG2, MEDA, MEDB = sb["BIG1"], sb["BIG2"], sb["MEDA"], sb["MEDB"]
    PB, PF, ROW = sb["POOLB"], sb["POOLF"], sb["ROW"]
    w1sb = BIG1.rearrange("p (c f q) -> p c f q", c=8, f=32)
    w2sb = BIG2[:, 0:32768].rearrange("p (f h) -> p f h", f=32)
    gT = MEDA[:, 0:16384].rearrange("p (f q) -> p f q", f=32)
    ytg = MEDB.rearrange("p (b c q) -> p b c q", b=2, c=8)
    routerT = PB[:, 0:64].rearrange("p (c e) -> p c e", c=8)
    c05b = PB[:, 64:320]
    equ = PB[:, 320:328].bitcast(U8)
    eq1 = equ[:, 0:8]; eq2 = equ[:, 8:16]
    b1sb = PF[:, 0:256].rearrange("p (e f) -> p e f", e=8)
    logitsAll = PF[:, 256:512]
    dAll = PF[:, 512:544]
    w1All = PF[:, 544:576]
    w2All = PF[:, 576:608]
    m1c = PF[:, 608:609]; m2c = PF[:, 609:610]
    l2buf = PF[:, 616:624]
    neg30 = PF[:, 624:632]
    b2row = ROW[:, 176:1200]
    C.ybuf = [sb["y0"], PB[:, 1024:2048]]

    S.op("pool", lambda g: g.dma_start(
        routerT, dr["m_routerT"].rearrange("(c p) e -> p c e", p=128)),
        writes=[("mrt",)], kind="d")
    S.op("sp", lambda e: e.dma_start(
        b1sb, dr["m_b1"].rearrange("e p f -> p e f")),
        writes=[("mb1",)], kind="d")
    S.op("dve", lambda e: e.memset(neg30, -1e30), writes=[("neg30",)])
    S.op("pool", lambda e: e.dma_start(x_dst, x_src),
         reads=[("x", si)], writes=[("x", si + 1), ("xinit", si)], kind="d")

    def moe_ln(t, p, ry):
        ytt = sb["ytt0"]
        rytt = ("ytt", si)
        transpose_y(C, ry, p, t, si,
                    lambda e, ytt=ytt: e.copy(ytt[:, :, :],
                                              ps["ptr"].rearrange("p (c q) -> p c q", c=8)),
                    [rytt])
        S.op("sp", lambda e, ytt=ytt, t=t: e.dma_start(ysc[:, :, 128 * t:128 * (t + 1)], ytt[:, :, :]),
             reads=[rytt], writes=[("ysc", si, t // 4)], kind="d",
             wmode="w" if t % 4 == 0 else "acc")
        mp = ps["med%d" % p]
        rmp = ("big1", p)
        for c in range(8):
            S.op("pe", lambda e, c=c, ytt=ytt, mp=mp: e.matmul(
                mp[:, 0:8], ytt[:, c, :], routerT[:, c, :],
                start=(c == 0), stop=(c == 7)),
                reads=[rytt, ("mrt",)], writes=[rmp],
                wmode="w" if c == 0 else "acc")
        S.op("dve", lambda e, mp=mp, t=t: e.tensor_copy(
            logitsAll[:, 8 * t:8 * (t + 1)], mp[:, 0:8]),
            reads=[rmp], writes=[("logits",)],
            wmode="w" if t < 2 else "acc")

    ln_loop(C, x_src, si, moe_ln)

    for t in range(NT):
        lt = logitsAll[:, 8 * t:8 * (t + 1)]
        S.op("dve", lambda e, lt=lt: e.reduce_max(m1c, lt, axis=AX.X),
             reads=[("logits",)], writes=[("rt",)])
        S.op("dve", lambda e, lt=lt: e.tensor_single_scalar(eq1, lt, m1c, ALU.is_equal),
             reads=[("logits",), ("rt",)], writes=[("equ", 1)])
        S.op("dve", lambda e, lt=lt: e.select(l2buf, eq1, neg30, lt, add_drain=True),
             reads=[("logits",), ("equ", 1), ("neg30",)], writes=[("l2",)])
        S.op("dve", lambda e: e.reduce_max(m2c, l2buf, axis=AX.X),
             reads=[("l2",)], writes=[("rt",)], wmode="acc")
        S.op("dve", lambda e: e.tensor_single_scalar(eq2, l2buf, m2c, ALU.is_equal),
             reads=[("l2",), ("rt",)], writes=[("equ", 2)])
        S.op("dve", lambda e, t=t: e.scalar_tensor_tensor(
            dAll[:, t:t + 1], m1c, 1.0, m2c, ALU.mult, ALU.subtract),
            reads=[("rt",)], writes=[("dAll",)],
            wmode="w" if t == 0 else "acc")
        S.op("dve", lambda e, t=t: e.tensor_copy(c05b[:, 8 * t:8 * (t + 1)], eq2),
             reads=[("equ", 2)], writes=[("c05",)],
             wmode="w" if t == 0 else "acc")
        S.op("dve", lambda e, t=t: e.tensor_copy(logitsAll[:, 8 * t:8 * (t + 1)], eq1),
             reads=[("equ", 1)], writes=[("logits",)], wmode="acc")
    S.op("act", lambda e: e.activation(w1All, dAll, AF.Sigmoid),
         reads=[("dAll",)], writes=[("w1All",)])
    S.op("dve", lambda e: e.tensor_scalar(w2All, w1All, -0.5, 0.5, ALU.mult, ALU.add),
         reads=[("w1All",)], writes=[("w2All",)])
    for t in range(NT):
        S.op("dve", lambda e, t=t: e.tensor_scalar(
            l2buf, logitsAll[:, 8 * t:8 * (t + 1)], w1All[:, t:t + 1], 0.5, ALU.mult, ALU.mult),
            reads=[("logits",), ("w1All",)], writes=[("l2",)])
        S.op("dve", lambda e, t=t: e.scalar_tensor_tensor(
            logitsAll[:, 8 * t:8 * (t + 1)], c05b[:, 8 * t:8 * (t + 1)],
            w2All[:, t:t + 1], l2buf, ALU.mult, ALU.add),
            reads=[("c05",), ("w2All",), ("l2",)], writes=[("logits",)], wmode="acc")

    for e_ in range(8):
        S.op("pool", lambda g, e_=e_: g.dma_start(
            w1sb, dr["m_w1"][e_].rearrange("(c p) (f q) -> p c f q", p=128, q=128)),
            writes=[("w1m", 0), ("w1m", 1), ("w1m", 2), ("w1m", 3)], kind="d")
        S.op("pool", lambda g, e_=e_: g.dma_start(
            w2sb, dr["m_w2"][e_].rearrange("(f p) h -> p f h", p=128)),
            writes=[("w2m",)], kind="d")
        S.op("pool", lambda g, e_=e_: g.dma_start(
            b2row, dr["m_b2"][0:1, D * e_:D * (e_ + 1)]),
            writes=[("rowbuf",)], kind="d")
        for g in range(NG):
            gp = g % 2
            rytg = ("ytg", gp)
            S.op("sp", lambda e, g=g, gp=gp: e.dma_start(
                ytg[:, gp], ysc[:, :, 512 * g:512 * (g + 1)]),
                reads=[("ysc", si, g)], writes=[rytg], kind="d")
            for F in range(32):
                par = F % 2
                mp = ps["med%d" % par]
                rmp = ("big1", par)
                for c in range(8):
                    S.op("pe", lambda e, c=c, F=F, gp=gp, mp=mp: e.matmul(
                        mp[:], w1sb[:, c, F, :], ytg[:, gp, c, :],
                        start=(c == 0), stop=(c == 7)),
                        reads=[rytg, ("w1m", F // 8)], writes=[rmp],
                        wmode="w" if c == 0 else "acc")
                S.op("act", lambda e, F=F, e_=e_, mp=mp: e.activation(
                    gT[:, F, :], mp[:], AF.Gelu, bias=b1sb[:, e_, F:F + 1]),
                    reads=[rmp, ("mb1",)], writes=[("gT", F)])
            for tt in range(4):
                t = 4 * g + tt
                bp = tt % 2
                bigp = ps["big0"] if bp == 0 else ps["pd"][:, 0:1024]
                rbig = ("big0",) if bp == 0 else ("pd",)
                for h2 in range(2):
                    cl = slice(512 * h2, 512 * (h2 + 1))
                    for fi, F in enumerate(range(31, -1, -1)):
                        S.op("pe", lambda e, F=F, tt=tt, cl=cl, bigp=bigp, fi=fi: e.matmul(
                            bigp[:, cl], gT[:, F, 128 * tt:128 * (tt + 1)], w2sb[:, F, cl],
                            start=(fi == 0), stop=False),
                            reads=[("gT", F), ("w2m",)], writes=[rbig],
                            wmode="w" if (fi == 0 and h2 == 0) else "acc")
                    S.op("pe", lambda e, cl=cl, bigp=bigp: e.matmul(
                        bigp[:, cl], sb["ones_col"], b2row[0:1, cl],
                        start=False, stop=True),
                        reads=[("ones_col",), ("rowbuf",)], writes=[rbig], wmode="acc")
                xm = sb["xmix0"]
                S.op("act", lambda e, t=t, e_=e_, xm=xm, bigp=bigp: e.mul(
                    xm[:], bigp[:], logitsAll[:, 8 * t + e_:8 * t + e_ + 1]),
                    reads=[rbig, ("logits",)], writes=[("xmix",)])
                S.op("pool", lambda e, t=t, xm=xm: e.dma_start(
                    x_dst[128 * t:128 * (t + 1), :], xm[:], accum_op=ALU.add),
                    reads=[("xmix",), ("xinit", si)], writes=[("x", si + 1)],
                    kind="d", wmode="acc")


def kernel(**inputs):
    w = prep_weights(inputs)
    x = np.ascontiguousarray(inputs["L0_attn_x"], dtype=np.float32)
    cur = [np.ascontiguousarray(x[b]) for b in range(8)]
    for stages in STAGE_GROUPS:
        nc, innames = build_nc(list(stages))
        in_maps = []
        for b in range(8):
            m = {}
            for n in innames:
                if n == "xin":
                    m[n] = cur[b]
                else:
                    m[n] = w[n]
            in_maps.append(m)
        res = run_bass_kernel_spmd(nc, in_maps, core_ids=list(range(8)))
        cur = [np.asarray(res.results[b]["xout"], dtype=np.float32) for b in range(8)]
    return np.stack(cur, axis=0)
